# version 20
# baseline (speedup 1.0000x reference)
import contextlib
import numpy as np
import ml_dtypes
import concourse.bass as bass
import concourse.mybir as mybir
from concourse.bass_utils import run_bass_kernel_spmd
from concourse.bass_core_rust import IndirectOffsetOnAxis

F32 = mybir.dt.float32
BF16 = mybir.dt.bfloat16
ALU = mybir.AluOpType
AF = mybir.ActivationFunctionType
AX = mybir.AxisListType

D = 1024
L = 4096
NH = 16
DH = 64
NE = 8
FF = 4096
NCORE = 8
EPS_LN = 1e-5
EPS_B = 1e-5
EPS_ATT = 1e-6
CAP = 4096
LC = 512
NLC = L // LC
NFC = D // 128
HL = L // 2
SLOT = 768

ENGINES = ("sync", "vector", "scalar", "tensor", "gpsimd")


class Sem:
    __slots__ = ("h", "n")

    def __init__(self, h):
        self.h = h
        self.n = 0


class Prog:
    def __init__(self, nc):
        self.nc = nc
        self.cmds = {e: [] for e in ENGINES}

    def do(self, engine, fn, sem=None, k=1):
        if sem is None:
            self.cmds[engine].append(fn)
            return None
        sem.n += k
        v = sem.n
        self.cmds[engine].append(lambda eng: fn(eng).then_inc(sem.h, k))
        return v

    def wait(self, engine, sem, val=None):
        v = sem.n if val is None else val
        self.cmds[engine].append(lambda eng: eng.wait_ge(sem.h, v))
        return v

    def barrier(self, bar):
        for e in ENGINES:
            bar.n += 1
            self.cmds[e].append(lambda eng, s=bar.h: eng.drain().then_inc(s, 1))
        tgt = bar.n
        for e in ENGINES:
            self.cmds[e].append(lambda eng, s=bar.h, v=tgt: eng.wait_ge(s, v))

    def replay(self, block):
        def run(eng, cmds):
            for c in cmds:
                c(eng)

        @block.sync
        def _(eng):
            run(eng, self.cmds["sync"])

        @block.vector
        def _(eng):
            run(eng, self.cmds["vector"])

        @block.scalar
        def _(eng):
            run(eng, self.cmds["scalar"])

        @block.tensor
        def _(eng):
            run(eng, self.cmds["tensor"])

        @block.gpsimd
        def _(eng):
            run(eng, self.cmds["gpsimd"])


class Pool:

    def __init__(self, r, tag, bufs):
        self.bufs = bufs
        self.sems = [r.sem(f"{tag}_s{i}") for i in range(len(bufs))]
        self.free_at = [[] for _ in bufs]
        self.k = 0

    def acquire(self, p, writer_engine):
        s = self.k % len(self.bufs)
        self.k += 1
        if self.free_at[s]:
            for sem, v in self.free_at[s]:
                p.wait(writer_engine, sem, v)
        self.free_at[s] = []
        return self.bufs[s], s

    def dma(self, p, queue, slot, fn):
        v = p.do(queue, fn, self.sems[slot], 16)
        return self.sems[slot], v

    def release(self, slot, sem, val):
        if self.free_at[slot] is None:
            self.free_at[slot] = []
        self.free_at[slot].append((sem, val))


class R:
    def __init__(self, nc, prog, stack):
        self.nc = nc
        self.p = prog
        self.stack = stack
        self.sems = {}
        self.uid = 0

    def sem(self, name):
        if name not in self.sems:
            h = self.stack.enter_context(self.nc.semaphore(name))
            self.sems[name] = Sem(h)
        return self.sems[name]

    def name(self, base):
        self.uid += 1
        return f"{base}{self.uid}"

    def sb(self, scope, shape, dt, n=1, base="t"):
        ts = [scope.enter_context(self.nc.sbuf_tensor(self.name(base), shape, dt))
              for _ in range(n)]
        return ts[0] if n == 1 else ts

    def ps(self, scope, shape, base="ps"):
        return scope.enter_context(self.nc.psum_tensor(self.name(base), shape, F32))



def emit_cast(r, scope, pairs):
    p = r.p
    cv = r.sem("c_cv")
    CC = 2048
    fpool = Pool(r, "p0", r.sb(scope, [128, CC], F32, 3, "castf"))
    bpool = Pool(r, "p1", r.sb(scope, [128, CC], BF16, 3, "castb"))
    for (src, dst, rows, cols) in pairs:
        cc = min(cols, CC)
        assert cols % cc == 0, (rows, cols)
        for i in range((rows + 127) // 128):
            pr = min(128, rows - 128 * i)
            for j in range(max(cols // CC, 1)):
                fb, fs = fpool.acquire(p, "sync")
                lsem, lv = fpool.dma(p, "sync", fs, lambda eng, fb=fb, src=src, i=i, j=j,
                                     cc=cc, pr=pr: eng.dma_start(
                    out=fb[0:pr, 0:cc], in_=src[128 * i:128 * i + pr, cc * j:cc * (j + 1)]))
                bb, bs = bpool.acquire(p, "vector")
                p.wait("vector", lsem, lv)
                v = p.do("vector", lambda eng, fb=fb, bb=bb, cc=cc, pr=pr: eng.tensor_copy(
                    bb[0:pr, 0:cc], fb[0:pr, 0:cc]), cv, 1)
                fpool.release(fs, cv, v)
                p.wait("gpsimd", cv)
                ssem, sv = bpool.dma(p, "gpsimd", bs, lambda eng, bb=bb, dst=dst, i=i, j=j,
                                     cc=cc, pr=pr: eng.dma_start(
                    out=dst[128 * i:128 * i + pr, cc * j:cc * (j + 1)], in_=bb[0:pr, 0:cc]))
                bpool.release(bs, ssem, sv)
    for s in bpool.sems:
        p.wait("gpsimd", s)



def emit_ln(r, scope, src, xn_bf, xn_f32=None):
    p = r.p
    sq = r.sem("ln_sq")
    mm = r.sem("ln_mm")
    ev = r.sem("ln_ev")
    st = r.sem("ln_st")

    xpool = Pool(r, "p0", r.sb(scope, [128, LC], F32, 3, "lnx"))
    sqpool = Pool(r, "p1", r.sb(scope, [128, LC], F32, 2, "lnsq"))
    mu_row = r.sb(scope, [1, L], F32, base="lnmu")
    rs_row = r.sb(scope, [1, L], F32, base="lnrs")
    srow = r.sb(scope, [1, LC], F32, base="lns")
    qrow = r.sb(scope, [1, LC], F32, base="lnq")
    trow = r.sb(scope, [1, LC], F32, base="lnt")
    ps_s = r.ps(scope, [1, LC], "lnps")
    ps_q = r.ps(scope, [1, LC], "lnpq")

    for lc in range(NLC):
        if lc > 0:
            p.wait("tensor", ev)
        for fc in range(NFC):
            xb, xs = xpool.acquire(p, "sync")
            lsem, lv = xpool.dma(p, "sync", xs, lambda eng, xb=xb, fc=fc, lc=lc: eng.dma_start(
                out=xb[:], in_=src[128 * fc:128 * (fc + 1), LC * lc:LC * (lc + 1)]))
            sb_, ss_ = sqpool.acquire(p, "scalar")
            p.wait("scalar", lsem, lv)
            vq = p.do("scalar", lambda eng, xb=xb, sb_=sb_: eng.activation(
                sb_[:], xb[:], AF.Square), sq, 1)
            p.wait("tensor", lsem, lv)
            v1 = p.do("tensor", lambda eng, xb=xb, f0=(fc == 0), fl=(fc == NFC - 1): eng.matmul(
                ps_s[:], r.ones_col[:], xb[:], start=f0, stop=fl), mm, 1)
            xpool.release(xs, mm, v1)
            xpool.release(xs, sq, vq)
            p.wait("tensor", sq, vq)
            v2 = p.do("tensor", lambda eng, sb_=sb_, f0=(fc == 0), fl=(fc == NFC - 1): eng.matmul(
                ps_q[:], r.ones_col[:], sb_[:], start=f0, stop=fl), mm, 1)
            sqpool.release(ss_, mm, v2)
        sl = slice(LC * lc, LC * (lc + 1))
        p.wait("vector", mm)
        p.do("vector", lambda eng, sl=sl: eng.tensor_single_scalar(
            mu_row[:, sl], ps_s[:], 1.0 / D, ALU.mult), ev, 1)
        p.do("vector", lambda eng: eng.tensor_single_scalar(
            qrow[:], ps_q[:], 1.0 / D, ALU.mult), ev, 1)
        p.wait("vector", ev)
        p.do("vector", lambda eng, sl=sl: eng.tensor_mul(
            trow[:], mu_row[:, sl], mu_row[:, sl]), ev, 1)
        p.wait("vector", ev)
        p.do("vector", lambda eng: eng.tensor_sub(srow[:], qrow[:], trow[:]), ev, 1)
        p.wait("scalar", ev)
        p.do("scalar", lambda eng: eng.activation(trow[:], srow[:], AF.Ln, bias=EPS_LN), ev, 1)
        p.wait("scalar", ev)
        p.do("scalar", lambda eng, sl=sl: eng.activation(
            rs_row[:, sl], trow[:], AF.Exp, scale=-0.5), ev, 1)
    p.wait("gpsimd", ev)
    p.do("gpsimd", lambda eng: eng.dma_start(out=r.d_murow[:], in_=mu_row[:]), st, 16)
    p.do("gpsimd", lambda eng: eng.dma_start(out=r.d_rsrow[:], in_=rs_row[:]), st, 16)
    p.wait("sync", st)

    mupool = Pool(r, "p2", r.sb(scope, [128, LC], F32, 2, "lnmub"))
    rspool = Pool(r, "p3", r.sb(scope, [128, LC], F32, 2, "lnrsb"))
    tpool = Pool(r, "p4", r.sb(scope, [128, LC], F32, 2, "lntb"))
    opool = Pool(r, "p5", r.sb(scope, [128, LC], BF16, 2, "lnob"))
    fpool = Pool(r, "p6", r.sb(scope, [128, LC], F32, 2, "lnfb")) if xn_f32 is not None else None
    for lc in range(NLC):
        sl = slice(LC * lc, LC * (lc + 1))
        mu_bc, mus = mupool.acquire(p, "sync")
        msem, mv = mupool.dma(p, "sync", mus, lambda eng, mu_bc=mu_bc, sl=sl: eng.dma_start(
            out=mu_bc[:], in_=r.d_murow[:, sl].broadcast_to((128, LC))))
        rs_bc, rss = rspool.acquire(p, "sync")
        rsem, rv = rspool.dma(p, "sync", rss, lambda eng, rs_bc=rs_bc, sl=sl: eng.dma_start(
            out=rs_bc[:], in_=r.d_rsrow[:, sl].broadcast_to((128, LC))))
        p.wait("vector", msem, mv)
        p.wait("vector", rsem, rv)
        last = None
        for fc in range(NFC):
            xb, xs = xpool.acquire(p, "sync")
            lsem, lv = xpool.dma(p, "sync", xs, lambda eng, xb=xb, fc=fc, sl=sl: eng.dma_start(
                out=xb[:], in_=src[128 * fc:128 * (fc + 1), sl]))
            tb, ts = tpool.acquire(p, "vector")
            p.wait("vector", lsem, lv)
            v = p.do("vector", lambda eng, xb=xb, tb=tb, mu_bc=mu_bc: eng.tensor_sub(
                tb[:], xb[:], mu_bc[:]), ev, 1)
            xpool.release(xs, ev, v)
            if xn_f32 is not None:
                fb, fs = fpool.acquire(p, "vector")
                p.wait("vector", ev, v)
                v2 = p.do("vector", lambda eng, tb=tb, fb=fb, rs_bc=rs_bc: eng.tensor_mul(
                    fb[:], tb[:], rs_bc[:]), ev, 1)
                tpool.release(ts, ev, v2)
                ob, os_ = opool.acquire(p, "vector")
                p.wait("vector", ev, v2)
                v3 = p.do("vector", lambda eng, fb=fb, ob=ob: eng.tensor_copy(
                    ob[:], fb[:]), ev, 1)
                last = v3
                p.wait("gpsimd", ev, v3)
                s1, w1 = opool.dma(p, "gpsimd", os_, lambda eng, ob=ob, fc=fc, sl=sl: eng.dma_start(
                    out=xn_bf[128 * fc:128 * (fc + 1), sl], in_=ob[:]))
                opool.release(os_, s1, w1)
                s2, w2 = fpool.dma(p, "gpsimd", fs, lambda eng, fb=fb, fc=fc, sl=sl: eng.dma_start(
                    out=xn_f32[128 * fc:128 * (fc + 1), sl], in_=fb[:]))
                fpool.release(fs, s2, w2)
            else:
                ob, os_ = opool.acquire(p, "vector")
                p.wait("vector", ev, v)
                v2 = p.do("vector", lambda eng, tb=tb, ob=ob, rs_bc=rs_bc: eng.tensor_mul(
                    ob[:], tb[:], rs_bc[:]), ev, 1)
                tpool.release(ts, ev, v2)
                last = v2
                p.wait("gpsimd", ev, v2)
                s1, w1 = opool.dma(p, "gpsimd", os_, lambda eng, ob=ob, fc=fc, sl=sl: eng.dma_start(
                    out=xn_bf[128 * fc:128 * (fc + 1), sl], in_=ob[:]))
                opool.release(os_, s1, w1)
        mupool.release(mus, ev, last)
        rspool.release(rss, ev, last)
    for pool in (opool, fpool) if fpool is not None else (opool,):
        for s in pool.sems:
            p.wait("gpsimd", s)



def emit_attn_a(r, scope, x_src, xn_bf, x_dst, stop_stage=None):
    p = r.p
    nc = r.nc
    ld = r.sem("aa_ld")
    mm = r.sem("aa_mm")
    ev = r.sem("aa_ev")
    st = r.sem("aa_st")
    sp = r.sem("aa_sp")

    s1 = scope.enter_context(contextlib.ExitStack())
    WT = r.sb(s1, [128, NFC * 3 * D], BF16, base="aaWT")
    ldq = {q: r.sem("aa_ld_" + q) for q in ("sync", "scalar", "gpsimd")}
    ldq["sync"] = ld
    for m, mat in enumerate((r.d_wqT_bf, r.d_wkT_bf, r.d_wvT_bf)):
        for dc in range(NFC):
            q = ("sync", "scalar", "gpsimd")[(m * NFC + dc) % 3]
            p.do(q, lambda eng, mat=mat, dc=dc, m=m: eng.dma_start(
                out=WT[:, 3 * D * dc + D * m: 3 * D * dc + D * (m + 1)],
                in_=mat[128 * dc:128 * (dc + 1), :]), ldq[q], 16)
    xnpool = Pool(r, "p0", r.sb(s1, [128, D], BF16, 2, "aaxn"))
    NEN = 6
    pss = [r.ps(s1, [128, 512], "aapq") for _ in range(NEN)]
    m_sb = r.sb(s1, [128, 512], F32, base="aam")
    a_sb = r.sb(s1, [128, 512], F32, base="aaa")
    e_sb = r.sb(s1, [128, 512], F32, base="aae")
    l_sb = r.sb(s1, [128, 512], F32, base="aal")
    opool = Pool(r, "p1", r.sb(s1, [128, 512], BF16, 3, "aao"))
    wld = p.wait("tensor", ld)
    for q in ("scalar", "gpsimd"):
        p.wait("tensor", ldq[q])
    for lt in range(L // 128):
        xb, xs = xnpool.acquire(p, "sync")
        xsem, xv = xnpool.dma(p, "sync", xs, lambda eng, xb=xb, lt=lt: eng.dma_start(
            out=xb[:].rearrange("p (c l) -> p c l", c=NFC),
            in_=xn_bf[:, 128 * lt:128 * (lt + 1)].rearrange(
                "(c p) l -> p c l", p=128)))
        p.wait("tensor", xsem, xv)
        if lt > 0:
            p.wait("tensor", ev)
        vmm = None
        for dc in range(NFC):
            for en in range(NEN):
                vmm = p.do("tensor", lambda eng, xb=xb, dc=dc, en=en,
                           f0=(dc == 0), fl=(dc == NFC - 1): eng.matmul(
                    pss[en][:], xb[:, 128 * dc:128 * (dc + 1)],
                    WT[:, 3 * D * dc + 512 * en: 3 * D * dc + 512 * (en + 1)],
                    start=f0, stop=fl), mm, 1)
        xnpool.release(xs, mm, vmm)
        for en in range(NEN):
            dst = (r.d_qtok, r.d_ktok, r.d_vtok)[en // 2]
            csl = slice(512 * (en % 2), 512 * (en % 2 + 1))
            ob, os_ = opool.acquire(p, "vector")
            if en < 4:
                p.wait("scalar", ev)
                p.wait("scalar", mm)
                vab = p.do("scalar", lambda eng, en=en: eng.activation(
                    a_sb[:], pss[en][:], AF.Abs), sp, 1)
                p.wait("vector", ev)
                p.wait("vector", sp, vab)
                p.do("vector", lambda eng, en=en: eng.tensor_scalar_max(
                    m_sb[:], pss[en][:], 0.0), ev, 1)
                p.wait("scalar", sp)
                p.do("scalar", lambda eng: eng.activation(
                    e_sb[:], a_sb[:], AF.Exp, scale=-1.0), sp, 1)
                p.wait("scalar", sp)
                p.do("scalar", lambda eng: eng.activation(
                    l_sb[:], e_sb[:], AF.Ln, bias=1.0), sp, 1)
                p.wait("vector", sp)
                p.wait("vector", ev)
                v = p.do("vector", lambda eng, ob=ob: eng.scalar_tensor_tensor(
                    ob[:], m_sb[:], EPS_ATT, l_sb[:], ALU.add, ALU.add), ev, 1)
            else:
                p.wait("vector", mm)
                v = p.do("vector", lambda eng, ob=ob, en=en: eng.tensor_copy(
                    ob[:], pss[en][:]), ev, 1)
            p.wait("gpsimd", ev, v)
            wsem, wv = opool.dma(p, "gpsimd", os_, lambda eng, ob=ob, dst=dst, lt=lt, csl=csl: eng.dma_start(
                out=dst[128 * lt:128 * (lt + 1), csl], in_=ob[:]))
            opool.release(os_, wsem, wv)
    for s in opool.sems:
        p.wait("gpsimd", s)

    for s in opool.sems:
        p.wait("sync", s)
    if stop_stage == 1:
        s1.close()
        return
    s1.close()
    p.wait("tensor", ev)

    ao_full = r.sb(scope, [128, 32 * D], BF16, base="aaao")
    s2 = scope.enter_context(contextlib.ExitStack())
    qpool = Pool(r, "p2", r.sb(s2, [128, D], BF16, 2, "aaq"))
    kpool = Pool(r, "p3", r.sb(s2, [128, D], BF16, 2, "aak"))
    vpool = Pool(r, "p4", r.sb(s2, [128, D], BF16, 2, "aav"))
    kfT = r.sb(s2, [64, 128], BF16, base="aakfT")
    qfT = r.sb(s2, [64, 128], BF16, base="aaqfT")
    Sb = r.sb(s2, [128, 128], BF16, base="aaSb")
    zi_bf = r.sb(s2, [128, 64], BF16, base="aazi")
    den_sc = r.sb(s2, [128, 64], BF16, base="aadsc")
    den = r.sb(s2, [128, 1], F32, base="aaden")
    den2 = r.sb(s2, [128, 1], F32, base="aaden2")
    rcol = r.sb(s2, [128, 1], F32, base="aar")
    KVbf = r.sb(s2, [64, D], BF16, base="aaKV")
    zrow = r.sb(s2, [1, D], BF16, base="aaz")
    KV32 = r.sb(s2, [64, D], F32, 2, base="aaKV32")
    z32 = r.sb(s2, [1, D], F32, 2, base="aaz32")
    ps_kv = r.ps(s2, [64, D], "aapkv")
    ps_z = [r.ps(s2, [1, 512], "aapz") for _ in range(2)]
    ps_tr = s2.enter_context(nc.psum_tensor(r.name("aaptr"), [64, 256], BF16))
    ps_S = r.ps(s2, [128, 128], "aapS")
    ps_zi = r.ps(s2, [128, 64], "aapzi")
    ps_o = r.ps(s2, [128, 64], "aapo")

    p.do("vector", lambda eng: eng.memset(KVbf[:], 0.0), ev, 1)
    p.do("vector", lambda eng: eng.memset(zrow[:], 0.0), ev, 1)
    p.do("vector", lambda eng: eng.memset(KV32[0][:], 0.0), ev, 1)
    p.do("vector", lambda eng: eng.memset(z32[0][:], 0.0), ev, 1)
    for tc in range(L // 128):
        qb, qs = qpool.acquire(p, "sync")
        qsem, qv = qpool.dma(p, "sync", qs, lambda eng, qb=qb, tc=tc: eng.dma_start(
            out=qb[:], in_=r.d_qtok[128 * tc:128 * (tc + 1), :]))
        kb, ks = kpool.acquire(p, "sync")
        ksem, kv_ = kpool.dma(p, "sync", ks, lambda eng, kb=kb, tc=tc: eng.dma_start(
            out=kb[:], in_=r.d_ktok[128 * tc:128 * (tc + 1), :]))
        vb, vs = vpool.acquire(p, "sync")
        vsem, vv_ = vpool.dma(p, "sync", vs, lambda eng, vb=vb, tc=tc: eng.dma_start(
            out=vb[:], in_=r.d_vtok[128 * tc:128 * (tc + 1), :]))
        p.wait("tensor", qsem, qv)
        p.wait("tensor", ksem, kv_)
        p.wait("tensor", vsem, vv_)
        lastv = None
        for h in range(NH):
            hs = slice(64 * h, 64 * (h + 1))
            p.wait("tensor", ev)
            p.do("tensor", lambda eng, kb=kb, hs=hs: eng.transpose(
                ps_tr[:, 0:128], kb[:, hs], r.ident_bf[:]), mm, 1)
            p.do("tensor", lambda eng, qb=qb, hs=hs: eng.transpose(
                ps_tr[:, 128:256], qb[:, hs], r.ident_bf[:]), mm, 1)
            p.wait("vector", mm)
            p.do("vector", lambda eng: eng.tensor_copy(kfT[:], ps_tr[:, 0:128]), ev, 1)
            vtr = p.do("vector", lambda eng: eng.tensor_copy(qfT[:], ps_tr[:, 128:256]), ev, 1)
            p.wait("tensor", ev, vtr)
            vS = p.do("tensor", lambda eng: eng.matmul(
                ps_S[:], kfT[:], qfT[:], start=True, stop=True), mm, 1)
            p.wait("vector", mm, vS)
            vSb = p.do("vector", lambda eng: eng.tensor_mul(
                Sb[:], ps_S[:], r.mask_f32[:]), ev, 1)
            vzi = p.do("tensor", lambda eng, kb=kb, hs=hs: eng.matmul(
                ps_zi[:], r.mask_bf[:], kb[:, hs], start=True, stop=False), mm, 1)
            vzi = p.do("tensor", lambda eng, hs=hs: eng.matmul(
                ps_zi[:], r.ones_row_bf[:, 0:128], zrow[:, hs], start=False, stop=True), mm, 1)
            p.wait("vector", mm, vzi)
            p.wait("vector", ev)
            vzb = p.do("vector", lambda eng: eng.tensor_copy(zi_bf[:], ps_zi[:]), ev, 1)
            p.wait("vector", ev, vzb)
            p.do("vector", lambda eng, qb=qb, hs=hs: eng.scalar_tensor_tensor(
                den_sc[:], qb[:, hs], 1.0, zi_bf[:], ALU.mult, ALU.mult,
                accum_out=den[:]), ev, 1)
            p.wait("vector", ev)
            p.do("vector", lambda eng: eng.tensor_scalar_add(den2[:], den[:], EPS_ATT), ev, 1)
            p.wait("vector", ev)
            vr = p.do("vector", lambda eng: eng.reciprocal(rcol[:], den2[:]), ev, 1)
            p.wait("tensor", ev, vSb)
            p.do("tensor", lambda eng, vb=vb, hs=hs: eng.matmul(
                ps_o[:], Sb[:], vb[:, hs], start=True, stop=False), mm, 1)
            vo = p.do("tensor", lambda eng, hs=hs: eng.matmul(
                ps_o[:], qfT[:], KVbf[:, hs], start=False, stop=True), mm, 1)
            vkv = p.do("tensor", lambda eng, kb=kb, vb=vb, hs=hs: eng.matmul(
                ps_kv[:, hs], kb[:, hs], vb[:, hs], start=True, stop=True), mm, 1)
            vzu = p.do("tensor", lambda eng, kb=kb, hs=hs, h=h: eng.matmul(
                ps_z[h // 8][:, 64 * (h % 8):64 * (h % 8 + 1)], r.ones_col_bf[:],
                kb[:, hs], start=True, stop=True), mm, 1)
            p.wait("vector", mm, vo)
            p.wait("vector", ev, vr)
            lastv = p.do("vector", lambda eng, tc=tc, h=h: eng.tensor_scalar(
                ao_full[:, 1024 * tc + 64 * h: 1024 * tc + 64 * (h + 1)], ps_o[:],
                rcol[:], None, ALU.mult), ev, 1)
            cur, prv = (tc + 1) % 2, tc % 2
            p.wait("vector", mm, vkv)
            va = p.do("vector", lambda eng, hs=hs, cur=cur, prv=prv: eng.tensor_add(
                KV32[cur][:, hs], KV32[prv][:, hs], ps_kv[:, hs]), ev, 1)
            p.wait("vector", ev, va)
            p.do("vector", lambda eng, hs=hs, cur=cur: eng.tensor_copy(
                KVbf[:, hs], KV32[cur][:, hs]), ev, 1)
            p.wait("vector", mm, vzu)
            vz2 = p.do("vector", lambda eng, hs=hs, h=h, cur=cur, prv=prv: eng.tensor_add(
                z32[cur][:, hs], z32[prv][:, hs],
                ps_z[h // 8][:, 64 * (h % 8):64 * (h % 8 + 1)]), ev, 1)
            p.wait("vector", ev, vz2)
            lastz = p.do("vector", lambda eng, hs=hs, cur=cur: eng.tensor_copy(
                zrow[:, hs], z32[cur][:, hs]), ev, 1)
        qpool.release(qs, ev, lastv)
        kpool.release(ks, ev, ev.n)
        vpool.release(vs, ev, lastv)

    if stop_stage == 2:
        dmp = r.nc.dram_tensor("aodump", [128, 32 * D], BF16, kind="ExternalOutput").ap()
        dsem = r.sem("aadmp")
        p.wait("gpsimd", ev)
        for tcd in range(32):
            p.do("gpsimd", lambda eng, tcd=tcd: eng.dma_start(
                out=dmp[:, 1024 * tcd:1024 * (tcd + 1)],
                in_=ao_full[:, 1024 * tcd:1024 * (tcd + 1)]), dsem, 16)
        p.wait("gpsimd", dsem)
        s2.close()
        return
    s2.close()
    p.wait("tensor", ev)

    aoT = r.sb(scope, [128, NFC * L], BF16, base="aaaoT")
    s3 = scope.enter_context(contextlib.ExitStack())
    ps_t2 = [s3.enter_context(nc.psum_tensor(r.name("aapt2"), [128, 128], BF16))
             for _ in range(2)]
    v_t2ev = [None, None]
    for tc in range(L // 128):
        for fc in range(NFC):
            half = (tc * NFC + fc) % 2
            if v_t2ev[half] is not None:
                p.wait("tensor", ev, v_t2ev[half])
            vt = p.do("tensor", lambda eng, tc=tc, fc=fc, half=half: eng.transpose(
                ps_t2[half][:],
                ao_full[:, 1024 * tc + 128 * fc: 1024 * tc + 128 * (fc + 1)],
                r.ident_bf[:]), mm, 1)
            p.wait("vector", mm, vt)
            v_t2ev[half] = p.do("vector", lambda eng, tc=tc, fc=fc, half=half: eng.tensor_copy(
                aoT[:, L * fc + 128 * tc: L * fc + 128 * (tc + 1)],
                ps_t2[half][:]), ev, 1)
    s3.close()
    p.wait("tensor", ev)

    s4 = scope.enter_context(contextlib.ExitStack())
    WoT = r.sb(s4, [128, NFC * D], BF16, base="aaWo")
    ld5 = r.sem("aa_ld5")
    for dc in range(NFC):
        q, s_ = (("sync", ld), ("gpsimd", ld5))[dc % 2]
        p.do(q, lambda eng, dc=dc: eng.dma_start(
            out=WoT[:, D * dc:D * (dc + 1)],
            in_=r.d_woT_bf[128 * dc:128 * (dc + 1), :]), s_, 16)
    xpool2 = Pool(r, "p5", r.sb(s4, [128, LC], F32, 2, "aax2"))
    rpool = Pool(r, "p6", r.sb(s4, [128, LC], F32, 2, "aares"))
    ps_w = [r.ps(s4, [128, 512], "aapw") for _ in range(2)]
    p.wait("tensor", ld)
    p.wait("tensor", ld5)
    p.wait("tensor", ev)
    k = 0
    v_w4ev = [None, None]
    for ec in range(NFC):
        for lc in range(NLC):
            pw = ps_w[k % 2]
            xb, xs = xpool2.acquire(p, "sync")
            x2sem, x2v = xpool2.dma(p, "sync", xs, lambda eng, xb=xb, ec=ec, lc=lc: eng.dma_start(
                out=xb[:], in_=x_src[128 * ec:128 * (ec + 1), LC * lc:LC * (lc + 1)]))
            if v_w4ev[k % 2] is not None:
                p.wait("tensor", ev, v_w4ev[k % 2])
            vw = None
            for dc in range(NFC):
                vw = p.do("tensor", lambda eng, pw=pw, dc=dc, ec=ec, lc=lc,
                          f0=(dc == 0), fl=(dc == NFC - 1): eng.matmul(
                    pw[:], WoT[:, D * dc + 128 * ec: D * dc + 128 * (ec + 1)],
                    aoT[:, L * dc + LC * lc: L * dc + LC * (lc + 1)],
                    start=f0, stop=fl), mm, 1)
            rb, rs = rpool.acquire(p, "vector")
            p.wait("vector", mm, vw)
            p.wait("vector", x2sem, x2v)
            v = p.do("vector", lambda eng, pw=pw, xb=xb, rb=rb: eng.tensor_add(
                rb[:], pw[:], xb[:]), ev, 1)
            v_w4ev[k % 2] = v
            xpool2.release(xs, ev, v)
            p.wait("gpsimd", ev, v)
            wsem, wv = rpool.dma(p, "gpsimd", rs, lambda eng, rb=rb, ec=ec, lc=lc: eng.dma_start(
                out=x_dst[128 * ec:128 * (ec + 1), LC * lc:LC * (lc + 1)], in_=rb[:]))
            rpool.release(rs, wsem, wv)
            k += 1
    for s in rpool.sems:
        p.wait("gpsimd", s)


def build_nc(stop_after="all", taps=(), skip_cc=0):
    nc = bass.Bass("TRN2", target_bir_lowering=False)
    with contextlib.ExitStack() as stack:
        prog = Prog(nc)
        r = R(nc, prog, stack)
        r.skip_cc = skip_cc
        p = prog

        r.d_xT = nc.dram_tensor("xT", [D, L], F32, kind="ExternalInput").ap()
        r.d_wqT = nc.dram_tensor("wqT", [D, D], F32, kind="ExternalInput").ap()
        r.d_wkT = nc.dram_tensor("wkT", [D, D], F32, kind="ExternalInput").ap()
        r.d_wvT = nc.dram_tensor("wvT", [D, D], F32, kind="ExternalInput").ap()
        r.d_woT = nc.dram_tensor("woT", [D, D], F32, kind="ExternalInput").ap()
        r.d_wg = nc.dram_tensor("wg", [D, NE], F32, kind="ExternalInput").ap()
        r.d_w1 = nc.dram_tensor("w1", [NE * D, FF], F32, kind="ExternalInput").ap()
        r.d_b1 = nc.dram_tensor("b1p", [128, 256], F32, kind="ExternalInput").ap()
        r.d_w2 = nc.dram_tensor("w2", [NE * FF, D], F32, kind="ExternalInput").ap()
        r.d_b2 = nc.dram_tensor("b2m", [NE, D], F32, kind="ExternalInput").ap()
        r.d_memask = nc.dram_tensor("memask", [32, 1], F32, kind="ExternalInput").ap()
        r.d_winT = nc.dram_tensor("winT", [D, D], F32, kind="ExternalInput").ap()
        r.d_binp = nc.dram_tensor("binp", [128, 8], F32, kind="ExternalInput").ap()
        r.d_cvT = nc.dram_tensor("cvT", [3 * D, D], F32, kind="ExternalInput").ap()
        r.d_cbp = nc.dram_tensor("cbp", [128, 8], F32, kind="ExternalInput").ap()
        r.d_alog = nc.dram_tensor("alogd", [D, 16], F32, kind="ExternalInput").ap()
        r.d_Dp = nc.dram_tensor("Dp", [128, 8], F32, kind="ExternalInput").ap()
        r.d_wbcdT = nc.dram_tensor("wbcdT", [D, 96], F32, kind="ExternalInput").ap()
        r.d_bbcd = nc.dram_tensor("bbcd", [96, 1], F32, kind="ExternalInput").ap()
        r.d_wd2T = nc.dram_tensor("wd2T", [64, D], F32, kind="ExternalInput").ap()
        r.d_bd2p = nc.dram_tensor("bd2p", [128, 8], F32, kind="ExternalInput").ap()
        r.d_woutT = nc.dram_tensor("woutT", [D, D], F32, kind="ExternalInput").ap()
        r.d_boutp = nc.dram_tensor("boutp", [128, 8], F32, kind="ExternalInput").ap()

        r.d_out = nc.dram_tensor("outT", [D, L], F32, kind="ExternalOutput").ap()
        r.d_x1 = nc.dram_tensor("x1", [D, L], F32).ap()
        r.d_xnbf = nc.dram_tensor("xnbf", [D, L], BF16).ap()
        r.d_murow = nc.dram_tensor("murow", [1, L], F32).ap()
        r.d_rsrow = nc.dram_tensor("rsrow", [1, L], F32).ap()
        r.d_qtok = nc.dram_tensor("qtok", [L, D], BF16).ap()
        r.d_ktok = nc.dram_tensor("ktok", [L, D], BF16).ap()
        r.d_vtok = nc.dram_tensor("vtok", [L, D], BF16).ap()
        r.d_x2 = nc.dram_tensor("x2", [D, L], F32).ap()
        r.d_xn32 = nc.dram_tensor("xn32", [D, L], F32).ap()
        r.d_w1bf = nc.dram_tensor("w1b", [NE * D, FF], BF16).ap()
        r.d_w2bf = nc.dram_tensor("w2b", [NE * FF, D], BF16).ap()
        r.d_sall = nc.dram_tensor("sall", [NE, L], F32).ap()
        r.d_facc = nc.dram_tensor("facc", [D, L], F32).ap()
        r.d_cnt256 = nc.dram_tensor("cnt256", [1, 256], F32).ap()
        r.d_goffd = nc.dram_tensor("goffd", [1, 8], F32).ap()
        r.d_slots = nc.dram_tensor("slots", [128, 32], mybir.dt.int32).ap()
        r.d_sslot = nc.dram_tensor("sslot", [NE * SLOT, 1], F32).ap()
        r.d_bufx = nc.dram_tensor("bufx", [NE * SLOT, D], BF16).ap()
        r.d_ybuf = nc.dram_tensor("ybuf", [NE * SLOT, D], F32).ap()
        if stop_after == "ffn1_dbg":
            r.dbg_ffn = True
            r.d_h1dump = nc.dram_tensor("h1dump", [128, 32 * LGT], BF16,
                                        kind="ExternalOutput").ap()
        r.d_run256 = nc.dram_tensor("run256", [32, 8], F32).ap()
        r.d_x3 = nc.dram_tensor("x3", [D, L], F32).ap()
        r.d_winT_bf = nc.dram_tensor("winTb", [D, D], BF16).ap()
        r.d_cvT_bf = nc.dram_tensor("cvTb", [3 * D, D], BF16).ap()
        r.d_wbcdT_bf = nc.dram_tensor("wbcdTb", [D, 96], BF16).ap()
        r.d_wd2T_bf = nc.dram_tensor("wd2Tb", [64, D], BF16).ap()
        r.d_woutT_bf = nc.dram_tensor("woutTb", [D, D], BF16).ap()
        r.d_h0f = nc.dram_tensor("h0f", [D, L], F32).ap()
        r.d_h0pb = nc.dram_tensor("h0pb", [D, L + 2], BF16).ap()
        r.d_hcf = nc.dram_tensor("hcf", [D, L], F32).ap()
        r.d_h2f = nc.dram_tensor("h2f", [D, L], F32).ap()
        r.d_h2b = nc.dram_tensor("h2b", [D, L], BF16).ap()
        r.d_gf = nc.dram_tensor("gf", [D, L], F32).ap()
        r.d_gnb = nc.dram_tensor("gnb", [D, L], BF16).ap()
        r.d_bcrows = nc.dram_tensor("bcrows", [32, L], F32).ap()
        r.d_gcells = nc.dram_tensor("gcells", [1, 8], F32).ap()
        r.d_cin1 = nc.dram_tensor("cin1", [4, 8], F32).ap()
        r.d_cout1 = nc.dram_tensor("cout1", [32, 8], F32).ap()
        r.d_cin2 = nc.dram_tensor("cin2", [4, 8], F32).ap()
        r.d_cout2 = nc.dram_tensor("cout2", [32, 8], F32).ap()
        r.d_wqT_bf = nc.dram_tensor("wqTb", [D, D], BF16).ap()
        r.d_wkT_bf = nc.dram_tensor("wkTb", [D, D], BF16).ap()
        r.d_wvT_bf = nc.dram_tensor("wvTb", [D, D], BF16).ap()
        r.d_woT_bf = nc.dram_tensor("woTb", [D, D], BF16).ap()

        tri = np.triu(np.ones((128, 128), np.float32))
        sut = np.triu(np.ones((128, 128), np.float32), 1)
        su32 = np.triu(np.ones((32, 32), np.float32), 1)
        r.c_ident = nc.inline_tensor(np.eye(128, dtype=ml_dtypes.bfloat16), name="idbf").ap()
        r.c_mask_bf = nc.inline_tensor(tri.astype(ml_dtypes.bfloat16), name="maskbf").ap()
        r.c_mask_f32 = nc.inline_tensor(tri, name="maskf").ap()
        r.c_ones_col = nc.inline_tensor(np.ones((128, 1), np.float32), name="ones_c").ap()
        r.c_ones_col_bf = nc.inline_tensor(np.ones((128, 1), ml_dtypes.bfloat16), name="ones_cb").ap()
        r.c_ones_row_bf = nc.inline_tensor(np.ones((1, 512), ml_dtypes.bfloat16), name="ones_rb").ap()
        r.c_ident_f32 = nc.inline_tensor(np.eye(128, dtype=np.float32), name="idf").ap()
        r.c_su_f32 = nc.inline_tensor(sut, name="suf").ap()
        r.c_su32_f32 = nc.inline_tensor(su32, name="su32f").ap()
        r.c_ones_row_f32 = nc.inline_tensor(np.ones((1, 512), np.float32), name="ones_rf").ap()
        r.c_iotaE = nc.inline_tensor(
            np.tile(np.arange(8, dtype=np.float32), (128, 1)), name="iotaE").ap()

        gscope = stack
        r.ident_bf = r.sb(gscope, [128, 128], BF16, base="identbf")
        r.mask_bf = r.sb(gscope, [128, 128], BF16, base="maskbf")
        r.mask_f32 = r.sb(gscope, [128, 128], F32, base="maskf")
        r.ones_col = r.sb(gscope, [128, 1], F32, base="onesc")
        r.ones_col_bf = r.sb(gscope, [128, 1], BF16, base="onescb")
        r.ones_row_bf = r.sb(gscope, [1, 512], BF16, base="onesrb")
        r.ident_f32 = r.sb(gscope, [128, 128], F32, base="identf")
        r.su_f32 = r.sb(gscope, [128, 128], F32, base="suf")
        r.su32_f32 = r.sb(gscope, [32, 32], F32, base="su32f")
        r.ones_row_f32 = r.sb(gscope, [1, 512], F32, base="onesrf")
        r.iotaE = r.sb(gscope, [128, 8], F32, base="iotaE")
        cl = r.sem("const_ld")
        for dst, srcc in ((r.ident_bf, r.c_ident), (r.mask_bf, r.c_mask_bf),
                          (r.mask_f32, r.c_mask_f32), (r.ones_col, r.c_ones_col),
                          (r.ones_col_bf, r.c_ones_col_bf), (r.ones_row_bf, r.c_ones_row_bf),
                          (r.ident_f32, r.c_ident_f32), (r.su_f32, r.c_su_f32),
                          (r.su32_f32, r.c_su32_f32), (r.ones_row_f32, r.c_ones_row_f32),
                          (r.iotaE, r.c_iotaE)):
            p.do("sync", lambda eng, d=dst, s=srcc: eng.dma_start(out=d[:], in_=s[:]), cl, 16)
        p.wait("sync", cl)

        cvals = [EPS_LN, 1.0]
        carr = np.array(cvals, np.float32)[None, :].repeat(128, 0)
        c_biases = nc.inline_tensor(np.ascontiguousarray(carr), name="biasconsts").ap()
        bias_sb = r.sb(gscope, [128, len(cvals)], F32, base="biasc")
        p.do("sync", lambda eng: eng.dma_start(out=bias_sb[:], in_=c_biases[:]), cl, 16)
        p.wait("sync", cl)
        for i, v in enumerate(cvals):
            nc.const_aps.register(F32, v, bias_sb[:, i:i + 1])

        bar = r.sem("bar")

        with contextlib.ExitStack() as sc:
            pairs = [
                (r.d_wqT, r.d_wqT_bf, D, D), (r.d_wkT, r.d_wkT_bf, D, D),
                (r.d_wvT, r.d_wvT_bf, D, D), (r.d_woT, r.d_woT_bf, D, D),
            ]
            if stop_after in ("attn_b", "all"):
                pairs += [(r.d_winT, r.d_winT_bf, D, D), (r.d_cvT, r.d_cvT_bf, 3 * D, D),
                          (r.d_wbcdT, r.d_wbcdT_bf, D, 96), (r.d_wd2T, r.d_wd2T_bf, 64, D),
                          (r.d_woutT, r.d_woutT_bf, D, D)]

            emit_cast(r, sc, pairs)
            if stop_after in ("ffn1", "ffn1_dbg", "attn_b", "all"):
                zi = r.sem("zinit")
                zt = r.sb(sc, [128, D], BF16, base="zinit")
                ztf = r.sb(sc, [128, NE * SLOT // 128], F32, base="zinitf")
                p.do("vector", lambda eng: eng.memset(zt[:], 0.0), zi, 1)
                p.do("vector", lambda eng: eng.memset(ztf[:], 0.0), zi, 1)
                p.wait("gpsimd", zi)
                for i_ in range(NE * SLOT // 128):
                    p.do("gpsimd", lambda eng, i_=i_: eng.dma_start(
                        out=r.d_bufx[128 * i_:128 * (i_ + 1), :], in_=zt[:]), zi, 16)
                p.do("gpsimd", lambda eng: eng.dma_start(
                    out=r.d_sslot.rearrange("(o s) x -> o (s x)", o=128), in_=ztf[:]), zi, 16)
                p.wait("gpsimd", zi)
            p.barrier(bar)
        if stop_after != "cast":
            with contextlib.ExitStack() as sc:
                emit_ln(r, sc, r.d_xT, r.d_xnbf)
                p.barrier(bar)
        if stop_after not in ("cast", "ln"):
            dst_a = r.d_out if stop_after in ("attn_a", "attn_a_s2", "attn_a_s1") else r.d_x1
            with contextlib.ExitStack() as sc:
                emit_attn_a(r, sc, r.d_xT, r.d_xnbf, dst_a,
                            stop_stage={"attn_a_s2": 2, "attn_a_s1": 1}.get(stop_after))
                p.barrier(bar)
        if stop_after not in ("cast", "ln", "attn_a", "attn_a_s1", "attn_a_s2"):
            dst_f = r.d_out if stop_after in ("ffn1", "ffn1_dbg") else r.d_x2
            with contextlib.ExitStack() as sc:
                emit_ffn_gating(r, sc, r.d_x1, 1)
            with contextlib.ExitStack() as sc:
                emit_ffn_experts_compact(r, sc, r.d_x1, dst_f)
                p.barrier(bar)
        if stop_after in ("attn_b", "all"):
            dst_b = r.d_out if stop_after == "attn_b" else r.d_x3
            with contextlib.ExitStack() as sc:
                emit_attn_b(r, sc, r.d_x2, dst_b)
        if stop_after == "all":
            with contextlib.ExitStack() as sc:
                emit_ffn_gating(r, sc, r.d_x3, 2)
            with contextlib.ExitStack() as sc:
                emit_ffn_experts_compact(r, sc, r.d_x3, r.d_out)
                p.barrier(bar)

        if taps:
            tsem = r.sem("tap")
            for i, name in enumerate(taps):
                srcd = getattr(r, "d_" + name)
                shp = [srcd.shape[0], srcd.shape[1]]
                dt = srcd.dtype
                dout = nc.dram_tensor(f"tap_{name}", shp, dt, kind="ExternalOutput").ap()
                p.do("sync", lambda eng, dout=dout, srcd=srcd: eng.dma_start(
                    out=dout[:], in_=srcd[:]), tsem, 16)
            p.wait("sync", tsem)

        with nc.Block() as block:
            prog.replay(block)
    return nc



LGT = 1024


def emit_ffn_gating(r, scope, x_src, call_idx):
    p = r.p
    nc = r.nc
    mm = r.sem("ff_mm")
    ev = r.sem("ff_ev")
    sq = r.sem("ff_sq")
    cc = r.sem("ff_cc")
    st = r.sem("ff_st")
    gld = r.sem("ff_gld")

    cin = r.d_cin1
    cout = r.d_cout1
    d_sall = r.d_sall

    with contextlib.ExitStack() as sln:
        emit_ln(r, sln, x_src, r.d_xnbf, r.d_xn32)
    p.barrier(r.sems["bar"])

    sg = scope.enter_context(contextlib.ExitStack())
    wg_sb = r.sb(sg, [128, NFC * NE], F32, base="fgwg")
    xnpool = Pool(r, "p0", r.sb(sg, [128, D], F32, 2, "fgxn"))
    lg_sb = r.sb(sg, [8, 128], F32, base="fglg")
    lgtok = r.sb(sg, [128, 8], F32, base="fglt")
    negm = r.sb(sg, [128, 1], F32, base="fgnm")
    ex = r.sb(sg, [128, 8], F32, base="fgex")
    sumex = r.sb(sg, [128, 1], F32, base="fgse")
    rsum = r.sb(sg, [128, 1], F32, base="fgrs")
    ismax = r.sb(sg, [128, 8], F32, base="fgim")
    c1 = r.sb(sg, [128, 8], F32, base="fgc1")
    c2 = r.sb(sg, [128, 8], F32, base="fgc2")
    onehot = r.sb(sg, [128, 8], F32, base="fgoh")
    oh_bf = r.sb(sg, [128, 8], BF16, base="fgohb")
    exo = r.sb(sg, [128, 1], F32, base="fgexo")
    gown = r.sb(sg, [128, 1], F32, base="fggo")
    scr8 = r.sb(sg, [128, 8], F32, base="fgscr")
    rank_own = r.sb(sg, [128, 1], F32, base="fgro")
    kept = r.sb(sg, [128, 1], F32, base="fgkp")
    scolp = Pool(r, "p8", r.sb(sg, [128, 1], F32, 3, "fgsc"))
    smat = r.sb(sg, [128, 8], F32, base="fgsm")
    oh_all = r.sb(sg, [128, 32 * 8], BF16, base="fgoha")
    ohf_all = r.sb(sg, [128, 32 * 8], F32, base="fgohf")
    gown_all = r.sb(sg, [128, 32], F32, base="fggoa")
    s_all_sb = r.sb(sg, [8, L], F32, base="fgsa")
    cnt_sb = r.sb(sg, [1, 256], F32, base="fgcnt")
    cnt_mat = r.sb(sg, [32, 8], F32, base="fgcm")
    tot_sb = r.sb(sg, [4, 8], F32, base="fgtot")
    cnt_all = r.sb(sg, [32, 8], F32, base="fgca")
    mask_me = r.sb(sg, [32, 1], F32, base="fgmm")
    masked = r.sb(sg, [32, 8], F32, base="fgmk")
    goff_sb = r.sb(sg, [1, 8], F32, base="fggf")
    run_sb = r.sb(sg, [32, 8], F32, base="fgrn")
    runrow = r.sb(sg, [1, 256], F32, base="fgrr")
    ps_lg = r.ps(sg, [8, 128], "fgplg")
    ps_tr = r.ps(sg, [128, 8], "fgptr")
    ps_cnt = r.ps(sg, [1, 256], "fgpc")
    ps_sm8 = r.ps(sg, [8, 128], "fgpsm")
    ps_tot = r.ps(sg, [1, 8], "fgpt")
    ps_run = r.ps(sg, [32, 8], "fgprn")
    ps_r = [r.ps(sg, [128, 8], "fgpr") for _ in range(2)]
    goff_bc = r.sb(sg, [128, 8], F32, base="fggb")
    own_goff = r.sb(sg, [128, 1], F32, base="fgog")
    lrank = r.sb(sg, [128, 1], F32, base="fglr")
    eidc = r.sb(sg, [128, 1], F32, base="fgei")
    slotf = r.sb(sg, [128, 1], F32, base="fgsf")
    slots_sb = r.sb(sg, [128, 32], mybir.dt.int32, base="fgsl")

    p.do("sync", lambda eng: eng.dma_start(
        out=wg_sb[:].rearrange("p (c e) -> p c e", c=NFC),
        in_=r.d_wg.rearrange("(c p) e -> p c e", p=128)), gld, 16)
    p.do("sync", lambda eng: eng.dma_start(out=mask_me[:], in_=r.d_memask[:]), gld, 16)
    wgv = p.wait("tensor", gld)

    for lt in range(32):
        xb, xs = xnpool.acquire(p, "sync")
        xsem, xv = xnpool.dma(p, "sync", xs, lambda eng, xb=xb, lt=lt: eng.dma_start(
            out=xb[:].rearrange("p (c l) -> p c l", c=NFC),
            in_=r.d_xn32[:, 128 * lt:128 * (lt + 1)].rearrange("(c p) l -> p c l", p=128)))
        p.wait("tensor", xsem, xv)
        if lt > 0:
            p.wait("tensor", ev)
        for fc in range(NFC):
            p.do("tensor", lambda eng, xb=xb, fc=fc, f0=(fc == 0), fl=(fc == NFC - 1): eng.matmul(
                ps_lg[:], wg_sb[:, 8 * fc:8 * (fc + 1)], xb[:, 128 * fc:128 * (fc + 1)],
                start=f0, stop=fl), mm, 1)
        p.wait("vector", mm)
        vlg = p.do("vector", lambda eng: eng.tensor_copy(lg_sb[:], ps_lg[:]), ev, 1)
        xnpool.release(xs, ev, vlg)
        p.wait("tensor", ev, vlg)
        vtr = p.do("tensor", lambda eng: eng.transpose(
            ps_tr[:], lg_sb[:], r.ident_f32[0:8, 0:8]), mm, 1)
        p.wait("vector", mm, vtr)
        p.do("vector", lambda eng: eng.tensor_copy(lgtok[:], ps_tr[:]), ev, 1)
        p.wait("vector", ev)
        p.do("vector", lambda eng: eng.reduce_max(
            negm[:], lgtok[:], AX.X, negate=True), ev, 1)
        p.wait("scalar", ev)
        p.do("scalar", lambda eng: eng.activation(
            ex[:], lgtok[:], AF.Exp, bias=negm[:], accum_out=sumex[:]), sq, 1)
        p.wait("vector", ev)
        p.do("vector", lambda eng: eng.tensor_scalar(
            ismax[:], lgtok[:], negm[:], 0.0, ALU.add, ALU.is_ge), ev, 1)
        p.wait("vector", ev)
        p.do("vector", lambda eng: eng.tensor_copy(c1[:, 0:1], ismax[:, 0:1]), ev, 1)
        p.do("vector", lambda eng: eng.tensor_add(
            c1[:, 1:8], ismax[:, 1:8], ismax[:, 0:7]), ev, 1)
        p.wait("vector", ev)
        p.do("vector", lambda eng: eng.tensor_copy(c2[:, 0:2], c1[:, 0:2]), ev, 1)
        p.do("vector", lambda eng: eng.tensor_add(c2[:, 2:8], c1[:, 2:8], c1[:, 0:6]), ev, 1)
        p.wait("vector", ev)
        p.do("vector", lambda eng: eng.tensor_copy(c1[:, 0:4], c2[:, 0:4]), ev, 1)
        p.do("vector", lambda eng: eng.tensor_add(c1[:, 4:8], c2[:, 4:8], c2[:, 0:4]), ev, 1)
        p.wait("vector", ev)
        p.do("vector", lambda eng: eng.tensor_single_scalar(
            c2[:], c1[:], 1.0, ALU.is_equal), ev, 1)
        p.wait("vector", ev)
        p.do("vector", lambda eng, lt=lt: eng.tensor_mul(onehot[:], ismax[:], c2[:]), ev, 1)
        p.wait("vector", ev)
        p.do("vector", lambda eng, lt=lt: eng.tensor_copy(
            oh_all[:, 8 * lt:8 * (lt + 1)], onehot[:]), ev, 1)
        p.do("vector", lambda eng, lt=lt: eng.tensor_copy(
            ohf_all[:, 8 * lt:8 * (lt + 1)], onehot[:]), ev, 1)
        p.do("vector", lambda eng: eng.tensor_copy(oh_bf[:], onehot[:]), ev, 1)
        p.wait("vector", sq)
        p.do("vector", lambda eng: eng.scalar_tensor_tensor(
            scr8[:], ex[:], 1.0, onehot[:], ALU.mult, ALU.mult, accum_out=exo[:]), ev, 1)
        p.do("vector", lambda eng: eng.reciprocal(rsum[:], sumex[:]), ev, 1)
        p.wait("vector", ev)
        vgo = p.do("vector", lambda eng, lt=lt: eng.tensor_mul(
            gown_all[:, lt:lt + 1], exo[:], rsum[:]), ev, 1)
        p.wait("tensor", ev, vgo)
        p.do("tensor", lambda eng, lt=lt: eng.matmul(
            ps_cnt[:, 8 * lt:8 * (lt + 1)], r.ones_col_bf[:], oh_bf[:],
            start=True, stop=True), mm, 1)
    p.wait("vector", mm)
    vc = p.do("vector", lambda eng: eng.tensor_copy(cnt_sb[:], ps_cnt[:]), ev, 1)
    p.wait("gpsimd", ev, vc)
    p.do("gpsimd", lambda eng: eng.dma_start(out=r.d_cnt256[:], in_=cnt_sb[:]), st, 16)
    p.wait("sync", st)
    p.do("sync", lambda eng: eng.dma_start(
        out=cnt_mat[:], in_=r.d_cnt256.rearrange("o (t e) -> (o t) e", e=8)), gld, 16)
    p.wait("tensor", gld)
    p.do("tensor", lambda eng: eng.matmul(
        ps_tot[:], r.ones_col[0:32, :], cnt_mat[:], start=True, stop=True), mm, 1)
    p.wait("vector", mm)
    p.do("vector", lambda eng: eng.memset(tot_sb[:], 0.0), ev, 1)
    p.wait("vector", ev)
    vt = p.do("vector", lambda eng: eng.tensor_copy(tot_sb[0:1, :], ps_tot[:]), ev, 1)
    p.wait("gpsimd", ev)
    p.do("gpsimd", lambda eng: eng.dma_start(out=cin[:], in_=tot_sb[:]), st, 16)
    p.wait("gpsimd", st)
    if getattr(r, "skip_cc", 0) == call_idx:
        p.do("gpsimd", lambda eng: eng.dma_start(out=cout[0:4, :], in_=cin[:]), st, 16)
        p.wait("gpsimd", st)
    else:
        p.do("gpsimd", lambda eng: eng.collective_compute(
            "AllGather", ALU.bypass, replica_groups=[list(range(NCORE))],
            ins=[cin.opt()], outs=[cout.opt()]), cc, 1)
        p.wait("gpsimd", cc)
    p.do("gpsimd", lambda eng: eng.dma_start(out=cnt_all[:], in_=cout[:]), st, 16)
    p.wait("vector", st)
    p.do("vector", lambda eng: eng.tensor_scalar(
        masked[:], cnt_all[:], mask_me[:], None, ALU.mult), ev, 1)
    p.wait("tensor", ev)
    p.do("tensor", lambda eng: eng.matmul(
        ps_tot[:], r.ones_col[0:32, :], masked[:], start=True, stop=True), mm, 1)
    p.wait("vector", mm)
    p.do("vector", lambda eng: eng.tensor_copy(goff_sb[:], ps_tot[:]), ev, 1)
    p.wait("tensor", ev)
    p.do("tensor", lambda eng: eng.matmul(
        ps_run[:], r.su32_f32[:], cnt_mat[:], start=True, stop=False), mm, 1)
    p.do("tensor", lambda eng: eng.matmul(
        ps_run[:], r.ones_row_f32[:, 0:32], goff_sb[:], start=False, stop=True), mm, 1)
    p.wait("vector", mm)
    vr = p.do("vector", lambda eng: eng.tensor_copy(run_sb[:], ps_run[:]), ev, 1)
    vgf = p.do("vector", lambda eng: eng.tensor_copy(tot_sb[0:1, :], goff_sb[:]), ev, 1)
    p.wait("gpsimd", ev, vr)
    p.do("gpsimd", lambda eng: eng.dma_start(out=r.d_run256[:], in_=run_sb[:]), st, 16)
    p.do("gpsimd", lambda eng: eng.dma_start(out=r.d_goffd[:], in_=tot_sb[0:1, :]), st, 16)
    p.wait("sync", st)
    p.do("sync", lambda eng: eng.dma_start(
        out=goff_bc[:], in_=r.d_goffd.broadcast_to((128, 8))), gld, 16)
    p.do("sync", lambda eng: eng.dma_start(
        out=runrow[:], in_=r.d_run256.rearrange("(o t) e -> o (t e)", o=1)), gld, 16)
    rrv = p.wait("tensor", gld)

    for lt in range(32):
        pr = ps_r[lt % 2]
        if lt >= 2:
            p.wait("tensor", ev)
        p.do("tensor", lambda eng, pr=pr, lt=lt: eng.matmul(
            pr[:], r.su_f32[:], ohf_all[:, 8 * lt:8 * (lt + 1)], start=True, stop=False), mm, 1)
        vrk = p.do("tensor", lambda eng, pr=pr, lt=lt: eng.matmul(
            pr[:], r.ones_row_f32[:, 0:128], runrow[:, 8 * lt:8 * (lt + 1)],
            start=False, stop=True), mm, 1)
        p.wait("vector", mm, vrk)
        p.do("vector", lambda eng, pr=pr, lt=lt: eng.scalar_tensor_tensor(
            scr8[:], ohf_all[:, 8 * lt:8 * (lt + 1)], 1.0, pr[:], ALU.mult, ALU.mult,
            accum_out=rank_own[:]), ev, 1)
        p.wait("vector", ev)
        p.do("vector", lambda eng: eng.tensor_single_scalar(
            kept[:], rank_own[:], float(CAP), ALU.is_lt), ev, 1)
        p.wait("vector", ev)
        s_col, scs = scolp.acquire(p, "vector")
        p.do("vector", lambda eng, lt=lt, s_col=s_col: eng.tensor_mul(
            s_col[:], gown_all[:, lt:lt + 1], kept[:]), ev, 1)
        p.wait("vector", ev)
        vsm = p.do("vector", lambda eng, lt=lt, s_col=s_col: eng.tensor_scalar(
            smat[:], ohf_all[:, 8 * lt:8 * (lt + 1)], s_col[:], None, ALU.mult), ev, 1)
        scolp.release(scs, ev, vsm)
        p.wait("vector", gld)
        p.do("vector", lambda eng, lt=lt: eng.scalar_tensor_tensor(
            scr8[:], ohf_all[:, 8 * lt:8 * (lt + 1)], 1.0, goff_bc[:], ALU.mult,
            ALU.mult, accum_out=own_goff[:]), ev, 1)
        p.do("vector", lambda eng, lt=lt: eng.scalar_tensor_tensor(
            scr8[:], ohf_all[:, 8 * lt:8 * (lt + 1)], 1.0, r.iotaE[:], ALU.mult,
            ALU.mult, accum_out=eidc[:]), ev, 1)
        p.wait("vector", ev)
        p.do("vector", lambda eng: eng.tensor_sub(lrank[:], rank_own[:], own_goff[:]), ev, 1)
        p.wait("vector", ev)
        p.do("vector", lambda eng: eng.tensor_scalar(
            slotf[:], eidc[:], float(SLOT), None, ALU.mult), ev, 1)
        p.wait("vector", ev)
        p.do("vector", lambda eng: eng.tensor_add(slotf[:], slotf[:], lrank[:]), ev, 1)
        p.wait("vector", ev)
        vsl = p.do("vector", lambda eng, lt=lt: eng.tensor_copy(
            slots_sb[:, lt:lt + 1], slotf[:]), ev, 1)
        p.wait("gpsimd", ev, vsl)
        scsem, scv = scolp.dma(p, "gpsimd", scs, lambda eng, lt=lt, s_col=s_col: eng.indirect_dma_start(
            r.d_sslot[:], IndirectOffsetOnAxis(slots_sb[:, lt:lt + 1], 0),
            s_col[:], None, oob_is_err=False))
        scolp.release(scs, scsem, scv)
        p.wait("tensor", ev, vsm)
        vtr2 = p.do("tensor", lambda eng, lt=lt: eng.transpose(
            ps_sm8[:], smat[:], r.ident_f32[:]), mm, 1)
        p.wait("vector", mm, vtr2)
        p.do("vector", lambda eng, lt=lt: eng.tensor_copy(
            s_all_sb[:, 128 * lt:128 * (lt + 1)], ps_sm8[:]), ev, 1)
    p.wait("gpsimd", ev)
    p.do("gpsimd", lambda eng: eng.dma_start(out=d_sall[:], in_=s_all_sb[:]), st, 16)
    p.do("gpsimd", lambda eng: eng.dma_start(out=r.d_slots[:], in_=slots_sb[:]), st, 16)
    p.wait("sync", st)
    p.wait("gpsimd", st)
    sg.close()
    p.barrier(r.sems["bar"])

    ssc = scope.enter_context(contextlib.ExitStack())
    xnp2 = Pool(r, "p1", r.sb(ssc, [128, NFC * 128], BF16, 2, "fsxn"))
    tokp = Pool(r, "p2", r.sb(ssc, [128, D], BF16, 2, "fstok"))
    slots2 = r.sb(ssc, [128, 32], mybir.dt.int32, base="fssl")
    ps_tb = [ssc.enter_context(nc.psum_tensor(r.name("fsptb"), [128, 128], BF16))
             for _ in range(2)]
    p.do("sync", lambda eng: eng.dma_start(out=slots2[:], in_=r.d_slots[:]), gld, 16)
    p.wait("gpsimd", gld)
    for lt in range(32):
        xb, xs = xnp2.acquire(p, "sync")
        xsem, xv = xnp2.dma(p, "sync", xs, lambda eng, xb=xb, lt=lt: eng.dma_start(
            out=xb[:].rearrange("p (c l) -> p c l", c=NFC),
            in_=r.d_xnbf[:, 128 * lt:128 * (lt + 1)].rearrange("(c p) l -> p c l", p=128)))
        tb, ts_ = tokp.acquire(p, "tensor")
        p.wait("tensor", xsem, xv)
        lastv = None
        for fc in range(NFC):
            pt = ps_tb[fc % 2]
            if fc >= 2 or lt > 0:
                p.wait("tensor", ev)
            vt = p.do("tensor", lambda eng, xb=xb, fc=fc, pt=pt: eng.transpose(
                pt[:], xb[:, 128 * fc:128 * (fc + 1)], r.ident_bf[:]), mm, 1)
            p.wait("vector", mm, vt)
            lastv = p.do("vector", lambda eng, tb=tb, fc=fc, pt=pt: eng.tensor_copy(
                tb[:, 128 * fc:128 * (fc + 1)], pt[:]), ev, 1)
        xnp2.release(xs, mm, mm.n)
        p.wait("gpsimd", ev, lastv)
        ssem, sv = tokp.dma(p, "gpsimd", ts_, lambda eng, tb=tb, lt=lt: eng.indirect_dma_start(
            r.d_bufx[:], IndirectOffsetOnAxis(slots2[:, lt:lt + 1], 0),
            tb[:], None, oob_is_err=False))
        tokp.release(ts_, ssem, sv)
    for s in tokp.sems:
        p.wait("sync", s)
        p.wait("gpsimd", s)
    ssc.close()
    p.barrier(r.sems["bar"])

def emit_ffn_experts(r, scope, x_src, x_dst):
    p = r.p
    nc = r.nc
    mm = r.sem("fe_mm")
    ev = r.sem("fe_ev")
    st = r.sem("fe_st")
    gld = r.sem("fe_gld")

    se = scope.enter_context(contextlib.ExitStack())
    xnpool = Pool(r, "p3", r.sb(se, [128, NFC * LGT], BF16, 2, "fexn"))
    h1_big = r.sb(se, [128, 32 * LGT], BF16, base="feh1")
    w2e = r.sb(se, [128, 32 * D], BF16, base="few2")
    w1pool = Pool(r, "p0", r.sb(se, [128, NFC * 128], BF16, 3, "few1"))
    sepool = Pool(r, "p1", r.sb(se, [128, LGT], F32, 2, "fese"))
    b1_sb = r.sb(se, [128, 256], F32, base="feb1")
    relu_t = r.sb(se, [128, LGT], F32, base="fert")
    b2_sb = r.sb(se, [8, D], F32, base="feb2")
    s8 = r.sb(se, [8, 512], F32, base="fes8")
    opool = Pool(r, "p2", r.sb(se, [128, LGT], F32, 3, "feo"))
    ps_h = [r.ps(se, [128, LGT], "feph") for _ in range(2)]
    ps_o = [r.ps(se, [128, LGT], "fepo") for _ in range(2)]

    ld0 = r.sem("fe_ld0")
    p.do("sync", lambda eng: eng.dma_start(out=b1_sb[:], in_=r.d_b1[:]), ld0, 16)
    p.do("sync", lambda eng: eng.dma_start(out=b2_sb[:], in_=r.d_b2[:]), ld0, 16)
    p.wait("tensor", ld0)
    p.wait("vector", ld0)

    w2sem = r.sem("fe_w2s")
    w2free = None
    accchain = {}

    for lc in range(NLC):
        sl = slice(LC * lc, LC * (lc + 1))
        p.do("sync", lambda eng, sl=sl: eng.dma_start(
            out=s8[:], in_=r.d_sall[:, sl]), ld0, 16)
        sv8 = p.wait("tensor", ld0)
        for oc in range(NFC):
            po = ps_o[oc % 2]
            if lc > 0 or oc >= 2:
                p.wait("tensor", ev)
            vb = p.do("tensor", lambda eng, po=po, oc=oc: eng.matmul(
                po[:, 0:LC], b2_sb[:, 128 * oc:128 * (oc + 1)], s8[:],
                start=True, stop=True), mm, 1)
            ob, os_ = opool.acquire(p, "vector")
            p.wait("vector", mm, vb)
            vcp = p.do("vector", lambda eng, po=po, ob=ob: eng.tensor_copy(
                ob[:, 0:LC], po[:, 0:LC]), ev, 1)
            p.wait("gpsimd", ev, vcp)
            ssem, sv = opool.dma(p, "gpsimd", os_, lambda eng, ob=ob, oc=oc, sl=sl: eng.dma_start(
                out=r.d_facc[128 * oc:128 * (oc + 1), sl], in_=ob[:, 0:LC]))
            opool.release(os_, ssem, sv)
            accchain[(lc, oc)] = (ssem, sv)

    NLG = L // LGT
    for lg in range(NLG):
        lsl = slice(LGT * lg, LGT * (lg + 1))
        xn_lg, xns = xnpool.acquire(p, "sync")
        xnsem, xnv = xnpool.dma(p, "sync", xns, lambda eng, lsl=lsl, xn_lg=xn_lg: eng.dma_start(
            out=xn_lg[:].rearrange("p (c l) -> p c l", c=NFC),
            in_=r.d_xnbf[:, lsl].rearrange("(c p) l -> p c l", p=128)))
        p.wait("tensor", xnsem, xnv)
        last_hidden_mm = None
        for e in range(NE):
            if w2free is not None:
                p.wait("sync", w2free[0], w2free[1])
            p.do("sync", lambda eng, e=e: eng.dma_start(
                out=w2e[:].rearrange("p (c o) -> p c o", c=32),
                in_=r.d_w2bf[FF * e:FF * (e + 1), :].rearrange(
                    "(c p) o -> p c o", p=128)), w2sem, 16)
            w2v = w2sem.n
            seb, ses = sepool.acquire(p, "sync")
            sesem, sev = sepool.dma(p, "sync", ses, lambda eng, seb=seb, e=e, lsl=lsl: eng.dma_start(
                out=seb[:], in_=r.d_sall[e:e + 1, lsl].broadcast_to((128, LGT))))
            for fo in range(32):
                w1b, w1s = w1pool.acquire(p, "sync")
                w1sem, w1v = w1pool.dma(p, "sync", w1s, lambda eng, w1b=w1b, e=e, fo=fo: eng.dma_start(
                    out=w1b[:].rearrange("p (c f) -> p c f", c=NFC),
                    in_=r.d_w1bf[D * e:D * (e + 1), 128 * fo:128 * (fo + 1)].rearrange(
                        "(c p) f -> p c f", p=128)))
                ph = ps_h[fo % 2]
                p.wait("tensor", w1sem, w1v)
                if fo >= 2 or e > 0 or lg > 0:
                    p.wait("tensor", ev)
                vmm = None
                for half in range(2):
                    for dk in range(NFC):
                        vmm = p.do("tensor", lambda eng, ph=ph, w1b=w1b, dk=dk, half=half,
                                   xn_lg=xn_lg,
                                   f0=(dk == 0), fl=(dk == NFC - 1): eng.matmul(
                            ph[:, 512 * half:512 * (half + 1)],
                            w1b[:, 128 * dk:128 * (dk + 1)],
                            xn_lg[:, LGT * dk + 512 * half:LGT * dk + 512 * half + 512],
                            start=f0, stop=fl), mm, 1)
                last_hidden_mm = vmm
                w1pool.release(w1s, mm, vmm)
                p.wait("vector", mm, vmm)
                p.wait("vector", sesem, sev)
                p.do("vector", lambda eng, ph=ph, e=e, fo=fo: eng.tensor_scalar(
                    relu_t[:], ph[:], b1_sb[:, 32 * e + fo:32 * e + fo + 1], 0.0,
                    ALU.add, ALU.max), ev, 1)
                p.wait("vector", ev)
                p.do("vector", lambda eng, fo=fo, seb=seb: eng.tensor_mul(
                    h1_big[:, LGT * fo:LGT * (fo + 1)], relu_t[:], seb[:]), ev, 1)
            sepool.release(ses, ev, ev.n)
            if getattr(r, "dbg_ffn", False) and lg == 0 and e == 0:
                dsem = r.sem("fedbg")
                p.wait("gpsimd", ev)
                p.do("gpsimd", lambda eng: eng.dma_start(
                    out=r.d_h1dump[:], in_=h1_big[:]), dsem, 16)
                p.wait("gpsimd", dsem)
            p.wait("tensor", w2sem, w2v)
            p.wait("tensor", ev)
            for oc in range(NE):
                po = ps_o[oc % 2]
                if oc >= 2:
                    p.wait("tensor", ev)
                vmm = None
                for half in range(2):
                    for fk in range(32):
                        vmm = p.do("tensor", lambda eng, po=po, fk=fk, oc=oc, half=half,
                                   f0=(fk == 0), fl=(fk == 31): eng.matmul(
                            po[:, 512 * half:512 * (half + 1)],
                            w2e[:, D * fk + 128 * oc: D * fk + 128 * (oc + 1)],
                            h1_big[:, LGT * fk + 512 * half:LGT * fk + 512 * half + 512],
                            start=f0, stop=fl), mm, 1)
                if oc == NE - 1:
                    w2free = (mm, vmm)
                ob, os_ = opool.acquire(p, "vector")
                p.wait("vector", mm, vmm)
                vcp = p.do("vector", lambda eng, po=po, ob=ob: eng.tensor_copy(
                    ob[:], po[:]), ev, 1)
                p.wait("gpsimd", ev, vcp)
                for key in ((2 * lg, oc), (2 * lg + 1, oc)):
                    if key in accchain:
                        asem, av = accchain[key]
                        p.wait("gpsimd", asem, av)
                ssem, sv = opool.dma(p, "gpsimd", os_, lambda eng, ob=ob, oc=oc,
                                     lsl=lsl: eng.dma_start(
                    out=r.d_facc[128 * oc:128 * (oc + 1), lsl], in_=ob[:],
                    accum_op=ALU.add))
                opool.release(os_, ssem, sv)
                accchain[(2 * lg, oc)] = (ssem, sv)
                accchain[(2 * lg + 1, oc)] = (ssem, sv)
        xnpool.release(xns, mm, last_hidden_mm)
    for s in opool.sems:
        p.wait("gpsimd", s)
    se.close()

    sr_ = scope.enter_context(contextlib.ExitStack())
    xpool = Pool(r, "p4", r.sb(sr_, [128, LC], F32, 2, "frx"))
    apool = Pool(r, "p5", r.sb(sr_, [128, LC], F32, 2, "fra"))
    rpool = Pool(r, "p6", r.sb(sr_, [128, LC], F32, 2, "frr"))
    for fc in range(NFC):
        for lc in range(NLC):
            sl = slice(LC * lc, LC * (lc + 1))
            xb, xs = xpool.acquire(p, "sync")
            xsem, xv = xpool.dma(p, "sync", xs, lambda eng, xb=xb, fc=fc, sl=sl: eng.dma_start(
                out=xb[:], in_=x_src[128 * fc:128 * (fc + 1), sl]))
            ab, asl = apool.acquire(p, "sync")
            asem, av = apool.dma(p, "sync", asl, lambda eng, ab=ab, fc=fc, sl=sl: eng.dma_start(
                out=ab[:], in_=r.d_facc[128 * fc:128 * (fc + 1), sl]))
            rb, rs = rpool.acquire(p, "vector")
            p.wait("vector", xsem, xv)
            p.wait("vector", asem, av)
            v = p.do("vector", lambda eng, xb=xb, ab=ab, rb=rb: eng.tensor_add(
                rb[:], xb[:], ab[:]), ev, 1)
            xpool.release(xs, ev, v)
            apool.release(asl, ev, v)
            p.wait("gpsimd", ev, v)
            ssem, sv = rpool.dma(p, "gpsimd", rs, lambda eng, rb=rb, fc=fc, sl=sl: eng.dma_start(
                out=x_dst[128 * fc:128 * (fc + 1), sl], in_=rb[:]))
            rpool.release(rs, ssem, sv)
    for s in rpool.sems:
        p.wait("gpsimd", s)
    sr_.close()


def emit_attn_b(r, scope, x_src, x_dst):
    p = r.p
    nc = r.nc
    mm = r.sem("ab_mm")
    ev = r.sem("ab_ev")
    sq = r.sem("ab_sq")
    st = r.sem("ab_st")
    gld = r.sem("ab_gld")
    bar = r.sems["bar"]

    with contextlib.ExitStack() as sln:
        emit_ln(r, sln, x_src, r.d_xnbf, r.d_xn32)
    p.barrier(bar)

    s1 = scope.enter_context(contextlib.ExitStack())
    xpool = Pool(r, "p0", r.sb(s1, [128, LC], F32, 3, "abx"))
    sqpool = Pool(r, "p1", r.sb(s1, [128, LC], F32, 2, "absq"))
    cells = r.sb(s1, [1, 8], F32, base="abcell")
    rowsc = r.sb(s1, [1, LC], F32, base="abrsc")
    ps_s = r.ps(s1, [1, LC], "abps")
    ps_q = r.ps(s1, [1, LC], "abpq")
    p.do("vector", lambda eng: eng.memset(cells[:], 0.0), ev, 1)
    for lc in range(NLC):
        if lc > 0:
            p.wait("tensor", ev)
        for fc in range(NFC):
            xb, xs = xpool.acquire(p, "sync")
            lsem, lv = xpool.dma(p, "sync", xs, lambda eng, xb=xb, fc=fc, lc=lc: eng.dma_start(
                out=xb[:], in_=r.d_xn32[128 * fc:128 * (fc + 1), LC * lc:LC * (lc + 1)]))
            sb_, ss_ = sqpool.acquire(p, "scalar")
            p.wait("scalar", lsem, lv)
            vq = p.do("scalar", lambda eng, xb=xb, sb_=sb_: eng.activation(
                sb_[:], xb[:], AF.Square), sq, 1)
            p.wait("tensor", lsem, lv)
            v1 = p.do("tensor", lambda eng, xb=xb, f0=(fc == 0), fl=(fc == NFC - 1): eng.matmul(
                ps_s[:], r.ones_col[:], xb[:], start=f0, stop=fl), mm, 1)
            xpool.release(xs, mm, v1)
            xpool.release(xs, sq, vq)
            p.wait("tensor", sq, vq)
            v2 = p.do("tensor", lambda eng, sb_=sb_, f0=(fc == 0), fl=(fc == NFC - 1): eng.matmul(
                ps_q[:], r.ones_col[:], sb_[:], start=f0, stop=fl), mm, 1)
            sqpool.release(ss_, mm, v2)
        p.wait("vector", mm)
        p.do("vector", lambda eng: eng.tensor_reduce(
            rowsc[:, 0:1], ps_s[:], AX.X, ALU.add), ev, 1)
        p.do("vector", lambda eng: eng.tensor_reduce(
            rowsc[:, 1:2], ps_q[:], AX.X, ALU.add), ev, 1)
        p.wait("vector", ev)
        p.do("vector", lambda eng: eng.tensor_add(
            cells[:, 0:1], cells[:, 0:1], rowsc[:, 0:1]), ev, 1)
        p.wait("vector", ev)
        vS = p.do("vector", lambda eng: eng.tensor_add(
            cells[:, 1:2], cells[:, 1:2], rowsc[:, 1:2]), ev, 1)
    NT = float(L * D)
    p.wait("vector", ev)
    p.do("vector", lambda eng: eng.tensor_single_scalar(
        cells[:, 2:3], cells[:, 0:1], 1.0 / NT, ALU.mult), ev, 1)
    p.wait("vector", ev)
    p.do("vector", lambda eng: eng.tensor_single_scalar(
        cells[:, 6:7], cells[:, 1:2], 1.0 / NT, ALU.mult), ev, 1)
    p.wait("vector", ev)
    p.do("vector", lambda eng: eng.tensor_mul(cells[:, 7:8], cells[:, 2:3], cells[:, 2:3]), ev, 1)
    p.wait("vector", ev)
    p.do("vector", lambda eng: eng.tensor_sub(cells[:, 3:4], cells[:, 6:7], cells[:, 7:8]), ev, 1)
    p.wait("scalar", ev)
    p.do("scalar", lambda eng: eng.activation(cells[:, 6:7], cells[:, 3:4], AF.Ln, bias=EPS_B), ev, 1)
    p.wait("scalar", ev)
    p.do("scalar", lambda eng: eng.activation(cells[:, 4:5], cells[:, 6:7], AF.Exp, scale=-0.5), ev, 1)
    p.wait("vector", ev)
    p.do("vector", lambda eng: eng.tensor_mul(cells[:, 5:6], cells[:, 4:5], cells[:, 2:3]), ev, 1)
    p.wait("vector", ev)
    p.do("vector", lambda eng: eng.tensor_single_scalar(
        cells[:, 5:6], cells[:, 5:6], -1.0, ALU.mult), ev, 1)
    p.wait("gpsimd", ev)
    p.do("gpsimd", lambda eng: eng.dma_start(out=r.d_gcells[:], in_=cells[:]), st, 16)
    p.wait("sync", st)
    rg_col = r.sb(s1, [128, 1], F32, base="abrg")
    nmr_col = r.sb(s1, [128, 1], F32, base="abnmr")
    p.do("sync", lambda eng: eng.dma_start(
        out=rg_col[:], in_=r.d_gcells[:, 4:5].broadcast_to((128, 1))), gld, 16)
    p.do("sync", lambda eng: eng.dma_start(
        out=nmr_col[:], in_=r.d_gcells[:, 5:6].broadcast_to((128, 1))), gld, 16)
    p.wait("vector", gld)
    p.wait("tensor", gld)

    winT = r.sb(s1, [128, NFC * D], BF16, base="abwin")
    bin_sb = r.sb(s1, [128, 8], F32, base="abbin")
    colsum = r.sb(s1, [128, 8], F32, base="abcs")
    bias2 = r.sb(s1, [128, 8], F32, base="abb2")
    zrow2 = r.sb(s1, [128, 2], BF16, base="abz2")
    ldw_q = {q: r.sem("ab_ldw_" + q) for q in ("scalar", "gpsimd")}
    ldw_q["sync"] = gld
    for dc in range(NFC):
        q = ("sync", "scalar", "gpsimd")[dc % 3]
        p.do(q, lambda eng, dc=dc: eng.dma_start(
            out=winT[:, D * dc:D * (dc + 1)],
            in_=r.d_winT_bf[128 * dc:128 * (dc + 1), :]), ldw_q[q], 16)
    p.do("sync", lambda eng: eng.dma_start(out=bin_sb[:], in_=r.d_binp[:]), gld, 16)
    p.wait("tensor", gld)
    for q in ("scalar", "gpsimd"):
        p.wait("tensor", ldw_q[q])
    ps_c = r.ps(s1, [128, 8], "abpc")
    for oc in range(NFC):
        for ic in range(NFC):
            p.do("tensor", lambda eng, oc=oc, ic=ic, f0=(ic == 0), fl=(ic == NFC - 1): eng.matmul(
                ps_c[:, oc:oc + 1], winT[:, D * ic + 128 * oc:D * ic + 128 * (oc + 1)],
                r.ones_col_bf[:], start=f0, stop=fl), mm, 1)
    p.wait("vector", mm)
    p.do("vector", lambda eng: eng.tensor_copy(colsum[:], ps_c[:]), ev, 1)
    p.wait("vector", ev)
    p.do("vector", lambda eng: eng.tensor_scalar(
        bias2[:], colsum[:], nmr_col[:], None, ALU.mult), ev, 1)
    p.wait("vector", ev)
    p.do("vector", lambda eng: eng.tensor_add(bias2[:], bias2[:], bin_sb[:]), ev, 1)
    p.wait("vector", ev)
    p.do("vector", lambda eng: eng.memset(zrow2[:], 0.0), ev, 1)
    zg = ev.n
    for fc in range(NFC):
        p.wait("gpsimd", ev, zg)
        p.do("gpsimd", lambda eng, fc=fc: eng.dma_start(
            out=r.d_h0pb[128 * fc:128 * (fc + 1), 0:1], in_=zrow2[:, 0:1],
            allow_slow_non_contiguous=True), st, 16)
        p.do("gpsimd", lambda eng, fc=fc: eng.dma_start(
            out=r.d_h0pb[128 * fc:128 * (fc + 1), L + 1:L + 2], in_=zrow2[:, 1:2],
            allow_slow_non_contiguous=True), st, 16)
    xnpool = Pool(r, "p2", r.sb(s1, [128, NFC * LC], BF16, 2, "abxn"))
    fpool = Pool(r, "p3", r.sb(s1, [128, LC], F32, 2, "abf"))
    bpool = Pool(r, "p4", r.sb(s1, [128, LC], BF16, 2, "abb"))
    ps_w = [r.ps(s1, [128, LC], "abpw") for _ in range(2)]
    v_wev = [None, None]
    k = 0
    for lc in range(NLC):
        sl = slice(LC * lc, LC * (lc + 1))
        xb, xs = xnpool.acquire(p, "sync")
        xsem, xv = xnpool.dma(p, "sync", xs, lambda eng, xb=xb, sl=sl: eng.dma_start(
            out=xb[:].rearrange("p (c l) -> p c l", c=NFC),
            in_=r.d_xnbf[:, sl].rearrange("(c p) l -> p c l", p=128)))
        p.wait("tensor", xsem, xv)
        vwl = None
        for oc in range(NFC):
            pw = ps_w[k % 2]
            if v_wev[k % 2] is not None:
                p.wait("tensor", ev, v_wev[k % 2])
            vw = None
            for ic in range(NFC):
                vw = p.do("tensor", lambda eng, pw=pw, ic=ic, oc=oc, xb=xb,
                          f0=(ic == 0), fl=(ic == NFC - 1): eng.matmul(
                    pw[:], winT[:, D * ic + 128 * oc:D * ic + 128 * (oc + 1)],
                    xb[:, LC * ic:LC * ic + LC], start=f0, stop=fl), mm, 1)
            vwl = vw
            fb, fs = fpool.acquire(p, "vector")
            p.wait("vector", mm, vw)
            v1 = p.do("vector", lambda eng, pw=pw, fb=fb, oc=oc: eng.tensor_scalar(
                fb[:], pw[:], rg_col[:], bias2[:, oc:oc + 1], ALU.mult, ALU.add), ev, 1)
            v_wev[k % 2] = v1
            bb, bs = bpool.acquire(p, "vector")
            p.wait("vector", ev, v1)
            v2 = p.do("vector", lambda eng, fb=fb, bb=bb: eng.tensor_copy(bb[:], fb[:]), ev, 1)
            p.wait("gpsimd", ev, v2)
            s1m, w1_ = fpool.dma(p, "gpsimd", fs, lambda eng, fb=fb, oc=oc, sl=sl: eng.dma_start(
                out=r.d_h0f[128 * oc:128 * (oc + 1), sl], in_=fb[:]))
            fpool.release(fs, s1m, w1_)
            s2m, w2_ = bpool.dma(p, "gpsimd", bs, lambda eng, bb=bb, oc=oc, lc=lc: eng.dma_start(
                out=r.d_h0pb[128 * oc:128 * (oc + 1), 1 + LC * lc:1 + LC * (lc + 1)], in_=bb[:]))
            bpool.release(bs, s2m, w2_)
            k += 1
        xnpool.release(xs, mm, vwl)
    for s in bpool.sems + fpool.sems:
        p.wait("sync", s)
        p.wait("gpsimd", s)
    s1.close()
    p.barrier(bar)

    s3 = scope.enter_context(contextlib.ExitStack())
    cvT = r.sb(s3, [128, 3 * NFC * D], BF16, base="abcv")
    cb_sb = r.sb(s3, [128, 8], F32, base="abcb")
    hpool = Pool(r, "p0", r.sb(s3, [128, NFC * (LC + 2)], BF16, 2, "abhp"))
    fpool3 = Pool(r, "p1", r.sb(s3, [128, LC], F32, 2, "abf3"))
    sum_c = r.sb(s3, [128, 8], F32, base="absc")
    sq_c = r.sb(s3, [128, 8], F32, base="absq2")
    col1 = r.sb(s3, [128, 1], F32, base="abc1")
    col2 = r.sb(s3, [128, 1], F32, base="abc2")
    sqt = r.sb(s3, [128, LC], F32, base="absqt")
    ps_w3 = [r.ps(s3, [128, LC], "abpw3") for _ in range(2)]
    v_w3ev = [None, None]
    ldc_q = {q: r.sem("ab_ldc_" + q) for q in ("scalar", "gpsimd")}
    ldc_q["sync"] = gld
    for kk in range(3):
        for ic in range(NFC):
            q = ("sync", "scalar", "gpsimd")[(kk * NFC + ic) % 3]
            p.do(q, lambda eng, kk=kk, ic=ic: eng.dma_start(
                out=cvT[:, D * (NFC * kk + ic):D * (NFC * kk + ic + 1)],
                in_=r.d_cvT_bf[D * kk + 128 * ic:D * kk + 128 * (ic + 1), :]), ldc_q[q], 16)
    p.do("sync", lambda eng: eng.dma_start(out=cb_sb[:], in_=r.d_cbp[:]), gld, 16)
    p.wait("tensor", gld)
    for q in ("scalar", "gpsimd"):
        p.wait("tensor", ldc_q[q])
    p.do("vector", lambda eng: eng.memset(sum_c[:], 0.0), ev, 1)
    p.do("vector", lambda eng: eng.memset(sq_c[:], 0.0), ev, 1)
    k = 0
    for lc in range(NLC):
        hb, hs_ = hpool.acquire(p, "sync")
        hsem, hv = hpool.dma(p, "sync", hs_, lambda eng, hb=hb, lc=lc: eng.dma_start(
            out=hb[:].rearrange("p (c l) -> p c l", c=NFC),
            in_=r.d_h0pb[:, LC * lc:LC * lc + LC + 2].rearrange(
                "(c p) l -> p c l", p=128)))
        p.wait("tensor", hsem, hv)
        vwl = None
        for oc in range(NFC):
            pw = ps_w3[k % 2]
            if v_w3ev[k % 2] is not None:
                p.wait("tensor", ev, v_w3ev[k % 2])
            vw = None
            for kk in range(3):
                for ic in range(NFC):
                    vw = p.do("tensor", lambda eng, pw=pw, kk=kk, ic=ic, oc=oc, hb=hb,
                              f0=(kk == 0 and ic == 0),
                              fl=(kk == 2 and ic == NFC - 1): eng.matmul(
                        pw[:], cvT[:, D * (NFC * kk + ic) + 128 * oc:
                                   D * (NFC * kk + ic) + 128 * (oc + 1)],
                        hb[:, (LC + 2) * ic + kk:(LC + 2) * ic + kk + LC],
                        start=f0, stop=fl), mm, 1)
            vwl = vw
            fb, fs = fpool3.acquire(p, "vector")
            p.wait("vector", mm, vw)
            v1 = p.do("vector", lambda eng, pw=pw, fb=fb, oc=oc: eng.tensor_scalar(
                fb[:], pw[:], cb_sb[:, oc:oc + 1], None, ALU.add), ev, 1)
            v_w3ev[k % 2] = v1
            p.wait("vector", ev, v1)
            p.do("vector", lambda eng, fb=fb: eng.tensor_reduce(
                col1[:], fb[:], AX.X, ALU.add), ev, 1)
            p.wait("scalar", ev)
            vsq = p.do("scalar", lambda eng, fb=fb: eng.activation(
                sqt[:], fb[:], AF.Square, accum_out=col2[:]), sq, 1)
            p.wait("vector", ev)
            p.do("vector", lambda eng, oc=oc: eng.tensor_add(
                sum_c[:, oc:oc + 1], sum_c[:, oc:oc + 1], col1[:]), ev, 1)
            p.wait("vector", sq, vsq)
            p.wait("vector", ev)
            v3 = p.do("vector", lambda eng, oc=oc: eng.tensor_add(
                sq_c[:, oc:oc + 1], sq_c[:, oc:oc + 1], col2[:]), ev, 1)
            p.wait("gpsimd", ev, v3)
            sm, wv_ = fpool3.dma(p, "gpsimd", fs, lambda eng, fb=fb, oc=oc, lc=lc: eng.dma_start(
                out=r.d_hcf[128 * oc:128 * (oc + 1), LC * lc:LC * (lc + 1)], in_=fb[:]))
            fpool3.release(fs, sm, wv_)
            k += 1
        hpool.release(hs_, mm, vwl)
    for s in fpool3.sems:
        p.wait("sync", s)
    mu_c = r.sb(s3, [128, 8], F32, base="abmc")
    rs_c = r.sb(s3, [128, 8], F32, base="abrc")
    t_c = r.sb(s3, [128, 8], F32, base="abtc")
    p.wait("vector", ev)
    p.do("vector", lambda eng: eng.tensor_single_scalar(
        mu_c[:], sum_c[:], 1.0 / L, ALU.mult), ev, 1)
    p.wait("vector", ev)
    p.do("vector", lambda eng: eng.tensor_single_scalar(
        t_c[:], sq_c[:], 1.0 / L, ALU.mult), ev, 1)
    p.wait("vector", ev)
    p.do("vector", lambda eng: eng.tensor_mul(sq_c[:], mu_c[:], mu_c[:]), ev, 1)
    p.wait("vector", ev)
    p.do("vector", lambda eng: eng.tensor_sub(t_c[:], t_c[:], sq_c[:]), ev, 1)
    p.wait("scalar", ev)
    p.do("scalar", lambda eng: eng.activation(t_c[:], t_c[:], AF.Ln, bias=EPS_B), ev, 1)
    p.wait("scalar", ev)
    p.do("scalar", lambda eng: eng.activation(rs_c[:], t_c[:], AF.Exp, scale=-0.5), ev, 1)
    bpool3 = Pool(r, "p2", r.sb(s3, [128, LC], BF16, 2, "abb3"))
    spool3 = Pool(r, "p3", r.sb(s3, [128, LC], F32, 2, "abs3"))
    zpool3 = Pool(r, "p4", r.sb(s3, [128, LC], F32, 2, "abz3"))
    p.wait("vector", ev)
    for oc in range(NFC):
        for lc in range(NLC):
            fb, fs = fpool3.acquire(p, "sync")
            fsem, fv = fpool3.dma(p, "sync", fs, lambda eng, fb=fb, oc=oc, lc=lc: eng.dma_start(
                out=fb[:], in_=r.d_hcf[128 * oc:128 * (oc + 1), LC * lc:LC * (lc + 1)]))
            sb3, ss3 = spool3.acquire(p, "vector")
            p.wait("vector", fsem, fv)
            v1 = p.do("vector", lambda eng, fb=fb, sb3=sb3, oc=oc: eng.tensor_scalar(
                sb3[:], fb[:], mu_c[:, oc:oc + 1], rs_c[:, oc:oc + 1],
                ALU.subtract, ALU.mult), ev, 1)
            fpool3.release(fs, ev, v1)
            zb, zs = zpool3.acquire(p, "scalar")
            p.wait("scalar", ev, v1)
            v2 = p.do("scalar", lambda eng, sb3=sb3, zb=zb: eng.activation(
                zb[:], sb3[:], AF.Silu), sq, 1)
            spool3.release(ss3, sq, v2)
            bb, bs = bpool3.acquire(p, "vector")
            p.wait("vector", sq, v2)
            v3 = p.do("vector", lambda eng, zb=zb, bb=bb: eng.tensor_copy(bb[:], zb[:]), ev, 1)
            p.wait("gpsimd", sq, v2)
            p.wait("gpsimd", ev, v3)
            sm1, wv1 = zpool3.dma(p, "gpsimd", zs, lambda eng, zb=zb, oc=oc, lc=lc: eng.dma_start(
                out=r.d_h2f[128 * oc:128 * (oc + 1), LC * lc:LC * (lc + 1)], in_=zb[:]))
            zpool3.release(zs, sm1, wv1)
            sm2, wv2 = bpool3.dma(p, "gpsimd", bs, lambda eng, bb=bb, oc=oc, lc=lc: eng.dma_start(
                out=r.d_h2b[128 * oc:128 * (oc + 1), LC * lc:LC * (lc + 1)], in_=bb[:]))
            bpool3.release(bs, sm2, wv2)
    for s in bpool3.sems + zpool3.sems:
        p.wait("sync", s)
        p.wait("gpsimd", s)
    s3.close()
    p.barrier(bar)

    s5 = scope.enter_context(contextlib.ExitStack())
    d1h_bf = r.sb(s5, [64, L], BF16, base="abd1")
    gs_row = r.sb(s5, [1, L], F32, base="abgs")
    gq_row = r.sb(s5, [1, L], F32, base="abgq")
    s5b = contextlib.ExitStack()
    wbcd = r.sb(s5b, [128, NFC * 96], BF16, base="abwb")
    bbcd = r.sb(s5b, [96, 1], F32, base="abbb")
    BCd = r.sb(s5b, [96, L], F32, base="abBC")
    h2pool = Pool(r, "p0", r.sb(s5b, [128, LC], BF16, 3, "abh2"))
    ps_bc = [r.ps(s5b, [96, LC], "abpbc") for _ in range(2)]
    for ic in range(NFC):
        p.do("sync", lambda eng, ic=ic: eng.dma_start(
            out=wbcd[:, 96 * ic:96 * (ic + 1)],
            in_=r.d_wbcdT_bf[128 * ic:128 * (ic + 1), :]), gld, 16)
    p.do("sync", lambda eng: eng.dma_start(out=bbcd[:], in_=r.d_bbcd[:]), gld, 16)
    p.wait("tensor", gld)
    v_bcev = [None, None]
    for lc in range(NLC):
        pb = ps_bc[lc % 2]
        if v_bcev[lc % 2] is not None:
            p.wait("tensor", ev, v_bcev[lc % 2])
        vw = None
        for ic in range(NFC):
            hb, hs_ = h2pool.acquire(p, "sync")
            hsem, hv = h2pool.dma(p, "sync", hs_, lambda eng, hb=hb, ic=ic, lc=lc: eng.dma_start(
                out=hb[:], in_=r.d_h2b[128 * ic:128 * (ic + 1), LC * lc:LC * (lc + 1)]))
            p.wait("tensor", hsem, hv)
            vw = p.do("tensor", lambda eng, pb=pb, ic=ic, hb=hb,
                      f0=(ic == 0), fl=(ic == NFC - 1): eng.matmul(
                pb[:], wbcd[:, 96 * ic:96 * (ic + 1)], hb[:], start=f0, stop=fl), mm, 1)
            h2pool.release(hs_, mm, vw)
        p.wait("vector", mm, vw)
        v1 = p.do("vector", lambda eng, pb=pb, lc=lc: eng.tensor_scalar(
            BCd[:, LC * lc:LC * (lc + 1)], pb[:], bbcd[:], None, ALU.add), ev, 1)
        v_bcev[lc % 2] = v1
        p.wait("vector", ev, v1)
        p.do("vector", lambda eng, lc=lc: eng.tensor_copy(
            d1h_bf[:, LC * lc:LC * (lc + 1)], BCd[0:64, LC * lc:LC * (lc + 1)]), ev, 1)
    p.wait("gpsimd", ev)
    p.do("gpsimd", lambda eng: eng.dma_start(out=r.d_bcrows[:], in_=BCd[64:96, :]), st, 16)
    p.wait("sync", st)
    p.wait("gpsimd", st)
    p.wait("tensor", ev)
    s5b.close()

    s6 = contextlib.ExitStack()
    wd2 = r.sb(s6, [64, D], BF16, base="abwd2")
    bd2_sb = r.sb(s6, [128, 8], F32, base="abbd2")
    D_sb = r.sb(s6, [128, 8], F32, base="abD")
    alog = r.sb(s6, [128, 16], F32, base="abal")
    negA = r.sb(s6, [128, 16], F32, base="abnA")
    delta = r.sb(s6, [128, L], F32, base="abdel")
    dh = r.sb(s6, [128, L], F32, base="abdh")
    abar = r.sb(s6, [128, L], F32, base="abab")
    bx = r.sb(s6, [128, L], F32, base="abbx")
    S_sb = r.sb(s6, [128, L], F32, base="abS")
    pr = r.sb(s6, [128, L], F32, base="abpr")
    bbc = [r.sb(s6, [128, L], F32, base=f"abbbc{i}") for i in range(2)]
    cshared = r.sb(s6, [128, L], F32, base="abcsh")
    cded = r.sb(s6, [128, L], F32, base="abcde")
    mcol = bx[:, 0:LC]
    mcol2 = bx[:, LC:2 * LC]
    ps_y = r.ps(s6, [128, L], "abpy")
    al_s = r.sem("ab_al")
    h0_s = r.sem("ab_h0s")
    bbs = [r.sem("ab_bb0"), r.sem("ab_bb1")]
    cs_s = r.sem("ab_cs")
    cd_s = r.sem("ab_cd")
    cs2_s = r.sem("ab_cs2")
    h02_s = r.sem("ab_h0s2")
    gst_s = r.sem("ab_gst")
    p.do("sync", lambda eng: eng.dma_start(out=wd2[:], in_=r.d_wd2T_bf[:]), gld, 16)
    p.do("sync", lambda eng: eng.dma_start(out=bd2_sb[:], in_=r.d_bd2p[:]), gld, 16)
    p.do("sync", lambda eng: eng.dma_start(out=D_sb[:], in_=r.d_Dp[:]), gld, 16)
    p.wait("tensor", gld)
    p.do("vector", lambda eng: eng.memset(gs_row[:], 0.0), ev, 1)
    p.do("vector", lambda eng: eng.memset(gq_row[:], 0.0), ev, 1)
    QRR = ("sync", "scalar", "gpsimd")
    qk = 0
    vA = vA2 = vg = None
    v_stt_last = None
    v_prod15_prev = None
    v_bx = [None, None]
    for fg in range(NFC):
        if vA is not None:
            p.wait("sync", sq, vA)
        p.do("sync", lambda eng, fg=fg: eng.dma_start(
            out=alog[:], in_=r.d_alog[128 * fg:128 * (fg + 1), :]), al_s, 16)
        p.wait("scalar", al_s)
        if v_stt_last is not None:
            p.wait("sync", ev, v_stt_last)
            p.wait("gpsimd", ev, v_stt_last)
        h2a_v = p.do("sync", lambda eng, fg=fg: eng.dma_start(
            out=cshared[:, 0:HL], in_=r.d_h2f[128 * fg:128 * (fg + 1), 0:HL]), cs_s, 16)
        h2a2_v = p.do("gpsimd", lambda eng, fg=fg: eng.dma_start(
            out=cshared[:, HL:L], in_=r.d_h2f[128 * fg:128 * (fg + 1), HL:L]), cs2_s, 16)
        if fg > 0:
            p.wait("vector", gst_s, 16 * fg)
            p.wait("tensor", ev)
        for lc in range(NLC):
            vw = p.do("tensor", lambda eng, fg=fg, lc=lc: eng.matmul(
                ps_y[:, LC * lc:LC * (lc + 1)], wd2[:, 128 * fg:128 * (fg + 1)],
                d1h_bf[:, LC * lc:LC * (lc + 1)], start=True, stop=True), mm, 1)
            p.wait("scalar", mm, vw)
            p.wait("scalar", ev)
            va = p.do("scalar", lambda eng, fg=fg, lc=lc: eng.activation(
                mcol[:], ps_y[:, LC * lc:LC * (lc + 1)], AF.Abs,
                bias=bd2_sb[:, fg:fg + 1]), sq, 1)
            p.wait("scalar", sq, va)
            ve = p.do("scalar", lambda eng: eng.activation(
                mcol2[:], mcol[:], AF.Exp, scale=-1.0), sq, 1)
            p.wait("scalar", sq, ve)
            vl = p.do("scalar", lambda eng: eng.activation(
                mcol[:], mcol2[:], AF.Ln, bias=1.0), sq, 1)
            p.wait("vector", sq, va)
            vm = p.do("vector", lambda eng, fg=fg, lc=lc: eng.tensor_scalar(
                delta[:, LC * lc:LC * (lc + 1)], ps_y[:, LC * lc:LC * (lc + 1)],
                bd2_sb[:, fg:fg + 1], 0.0, ALU.add, ALU.max), ev, 1)
            p.wait("vector", sq, vl)
            p.wait("vector", ev, vm)
            p.do("vector", lambda eng, lc=lc: eng.tensor_add(
                delta[:, LC * lc:LC * (lc + 1)], delta[:, LC * lc:LC * (lc + 1)],
                mcol[:]), ev, 1)
        p.wait("scalar", sq)
        if vA2 is not None:
            p.wait("scalar", ev, vA2)
        vA = p.do("scalar", lambda eng: eng.activation(
            negA[:], alog[:], AF.Exp), sq, 1)
        p.wait("vector", sq, vA)
        vA2 = p.do("vector", lambda eng: eng.tensor_single_scalar(
            negA[:], negA[:], -1.0, ALU.mult), ev, 1)
        p.wait("vector", cs_s, h2a_v)
        p.wait("vector", cs2_s, h2a2_v)
        vdh = p.do("vector", lambda eng: eng.tensor_mul(dh[:], delta[:], cshared[:]), ev, 1)
        v_scan = [None] * 16
        v_prod = [None] * 16
        v_y = [None] * 16
        v_ab = [None] * 16
        for n in range(16):
            cb = cshared if n % 2 == 0 else cded
            cb_sem = cs_s if n % 2 == 0 else cd_s
            qb = QRR[qk % 3]; qk += 1
            p.wait(qb, ev, v_bx[n % 2] if n >= 2 else vdh)
            bb_v = p.do(qb, lambda eng, n=n: eng.dma_start(
                out=bbc[n % 2][:], in_=r.d_bcrows[n:n + 1, :].broadcast_to((128, L))),
                bbs[n % 2], 16)
            qc = QRR[qk % 3]; qk += 1
            if n >= 2:
                p.wait(qc, ev, v_prod[n - 2])
            elif n == 0:
                p.wait(qc, ev, vdh)
            elif v_prod15_prev is not None:
                p.wait(qc, ev, v_prod15_prev)
            cb_v = p.do(qc, lambda eng, n=n, cb=cb: eng.dma_start(
                out=cb[:], in_=r.d_bcrows[16 + n:16 + n + 1, :].broadcast_to((128, L))),
                cb_sem, 16)
            p.wait("scalar", ev, v_scan[n - 1] if n > 0 else vdh)
            v_ab[n] = p.do("scalar", lambda eng, n=n: eng.activation(
                abar[:], delta[:], AF.Exp, scale=negA[:, n:n + 1]), sq, 1)
            p.wait("vector", bbs[n % 2], bb_v)
            v_bx[n % 2] = p.do("vector", lambda eng, n=n: eng.tensor_mul(
                bx[:], dh[:], bbc[n % 2][:]), ev, 1)
            p.wait("vector", sq, v_ab[n])
            v_scan[n] = p.do("vector", lambda eng: eng.tensor_tensor_scan(
                S_sb[:], abar[:], bx[:], 0.0, ALU.mult, ALU.add), ev, 1)
            p.wait("vector", cb_sem, cb_v)
            if n >= 1:
                p.wait("vector", mm, v_y[n - 1])
            v_prod[n] = p.do("vector", lambda eng, cb=cb: eng.tensor_mul(
                pr[:], S_sb[:], cb[:]), ev, 1)
            p.wait("tensor", ev, v_prod[n])
            for c in range(8):
                v_y[n] = p.do("tensor", lambda eng, n=n, c=c: eng.matmul(
                    ps_y[:, 512 * c:512 * (c + 1)], r.ident_f32[:],
                    pr[:, 512 * c:512 * (c + 1)],
                    start=(n == 0), stop=(n == 15)), mm, 1)
        v_prod15_prev = v_prod[15]
        p.wait("sync", ev, v_prod[14])
        p.wait("gpsimd", ev, v_prod[14])
        h2b_v = p.do("sync", lambda eng, fg=fg: eng.dma_start(
            out=cshared[:, 0:HL], in_=r.d_h2f[128 * fg:128 * (fg + 1), 0:HL]), cs_s, 16)
        h2b2_v = p.do("gpsimd", lambda eng, fg=fg: eng.dma_start(
            out=cshared[:, HL:L], in_=r.d_h2f[128 * fg:128 * (fg + 1), HL:L]), cs2_s, 16)
        p.wait("sync", ev, v_scan[15])
        p.wait("gpsimd", ev, v_scan[15])
        p.do("sync", lambda eng, fg=fg: eng.dma_start(
            out=bx[:, 0:HL], in_=r.d_h0f[128 * fg:128 * (fg + 1), 0:HL]), h0_s, 16)
        p.do("gpsimd", lambda eng, fg=fg: eng.dma_start(
            out=bx[:, HL:L], in_=r.d_h0f[128 * fg:128 * (fg + 1), HL:L]), h02_s, 16)
        p.wait("vector", mm, v_y[15])
        p.wait("vector", cs_s, h2b_v)
        p.wait("vector", cs2_s, h2b2_v)
        for c in range(8):
            v_stt_last = p.do("vector", lambda eng, fg=fg, c=c: eng.scalar_tensor_tensor(
                dh[:, 512 * c:512 * (c + 1)], cshared[:, 512 * c:512 * (c + 1)],
                D_sb[:, fg:fg + 1], ps_y[:, 512 * c:512 * (c + 1)],
                ALU.mult, ALU.add), ev, 1)
        p.wait("scalar", h0_s)
        p.wait("scalar", h02_s)
        p.wait("scalar", mm, v_y[15])
        vsil = p.do("scalar", lambda eng: eng.activation(pr[:], bx[:], AF.Silu), sq, 1)
        p.wait("vector", sq, vsil)
        vg = p.do("vector", lambda eng: eng.tensor_mul(delta[:], dh[:], pr[:]), ev, 1)
        p.wait("gpsimd", ev, vg)
        p.do("gpsimd", lambda eng, fg=fg: eng.dma_start(
            out=r.d_gf[128 * fg:128 * (fg + 1), :], in_=delta[:]), gst_s, 16)
        p.wait("tensor", ev, vg)
        v_gs = [None] * 8
        for c in range(8):
            v_gs[c] = p.do("tensor", lambda eng, c=c: eng.matmul(
                ps_y[0:1, 512 * c:512 * (c + 1)], r.ones_col[:],
                delta[:, 512 * c:512 * (c + 1)], start=True, stop=True), mm, 1)
        vadd_gs = [None] * 8
        for c in range(8):
            p.wait("vector", mm, v_gs[c])
            vadd_gs[c] = p.do("vector", lambda eng, c=c: eng.tensor_add(
                gs_row[:, 512 * c:512 * (c + 1)], gs_row[:, 512 * c:512 * (c + 1)],
                ps_y[0:1, 512 * c:512 * (c + 1)]), ev, 1)
        v_gq_prev = None
        for c in range(8):
            p.wait("scalar", ev, vg)
            if v_gq_prev is not None:
                p.wait("scalar", mm, v_gq_prev)
            vq2 = p.do("scalar", lambda eng, c=c: eng.activation(
                mcol[:], delta[:, 512 * c:512 * (c + 1)], AF.Square), sq, 1)
            p.wait("tensor", sq, vq2)
            p.wait("tensor", ev, vadd_gs[c])
            v_gq_prev = p.do("tensor", lambda eng, c=c: eng.matmul(
                ps_y[0:1, 512 * c:512 * (c + 1)], r.ones_col[:], mcol[:],
                start=True, stop=True), mm, 1)
            p.wait("vector", mm, v_gq_prev)
            p.do("vector", lambda eng, c=c: eng.tensor_add(
                gq_row[:, 512 * c:512 * (c + 1)], gq_row[:, 512 * c:512 * (c + 1)],
                ps_y[0:1, 512 * c:512 * (c + 1)]), ev, 1)
    p.wait("sync", gst_s)
    p.wait("sync", ev)
    p.wait("sync", mm)
    p.wait("tensor", ev)
    p.wait("gpsimd", ev)
    p.wait("gpsimd", cs_s)
    p.wait("gpsimd", cs2_s)
    p.wait("gpsimd", h02_s)
    p.wait("gpsimd", cd_s)
    p.wait("scalar", cs_s)
    p.wait("scalar", cs2_s)
    p.wait("scalar", h02_s)
    p.wait("scalar", cd_s)
    p.wait("scalar", bbs[0])
    p.wait("scalar", bbs[1])
    s6.close()

    mu_row = r.sb(s5, [1, L], F32, base="abmu")
    rs_row = r.sb(s5, [1, L], F32, base="abrs")
    nmu_row = r.sb(s5, [1, L], F32, base="abnm")
    p.wait("vector", ev)
    p.do("vector", lambda eng: eng.tensor_single_scalar(
        mu_row[:], gs_row[:], 1.0 / D, ALU.mult), ev, 1)
    p.wait("vector", ev)
    p.do("vector", lambda eng: eng.tensor_single_scalar(
        gq_row[:], gq_row[:], 1.0 / D, ALU.mult), ev, 1)
    p.wait("vector", ev)
    p.do("vector", lambda eng: eng.tensor_mul(gs_row[:], mu_row[:], mu_row[:]), ev, 1)
    p.wait("vector", ev)
    p.do("vector", lambda eng: eng.tensor_sub(gq_row[:], gq_row[:], gs_row[:]), ev, 1)
    p.wait("scalar", ev)
    p.do("scalar", lambda eng: eng.activation(gq_row[:], gq_row[:], AF.Ln, bias=EPS_B), ev, 1)
    p.wait("scalar", ev)
    p.do("scalar", lambda eng: eng.activation(rs_row[:], gq_row[:], AF.Exp, scale=-0.5), ev, 1)
    p.wait("vector", ev)
    p.do("vector", lambda eng: eng.tensor_single_scalar(
        nmu_row[:], mu_row[:], -1.0, ALU.mult), ev, 1)
    p.wait("gpsimd", ev)
    p.do("gpsimd", lambda eng: eng.dma_start(out=r.d_murow[:], in_=rs_row[:]), st, 16)
    p.do("gpsimd", lambda eng: eng.dma_start(out=r.d_rsrow[:], in_=nmu_row[:]), st, 16)
    p.wait("sync", st)
    gpool = Pool(r, "p1", r.sb(s5, [128, LC], F32, 3, "abgp"))
    rsbcp = Pool(r, "p2", r.sb(s5, [128, LC], F32, 2, "abrsb"))
    nmbcp = Pool(r, "p3", r.sb(s5, [128, LC], F32, 2, "abnmb"))
    t1pool = Pool(r, "p4", r.sb(s5, [128, LC], F32, 2, "abt1"))
    gnpool = Pool(r, "p5", r.sb(s5, [128, LC], BF16, 2, "abgn"))
    for lc in range(NLC):
        sl = slice(LC * lc, LC * (lc + 1))
        rsb, rss = rsbcp.acquire(p, "sync")
        rssem, rsv = rsbcp.dma(p, "sync", rss, lambda eng, rsb=rsb, sl=sl: eng.dma_start(
            out=rsb[:], in_=r.d_murow[:, sl].broadcast_to((128, LC))))
        nmb, nms = nmbcp.acquire(p, "sync")
        nmsem, nmv = nmbcp.dma(p, "sync", nms, lambda eng, nmb=nmb, sl=sl: eng.dma_start(
            out=nmb[:], in_=r.d_rsrow[:, sl].broadcast_to((128, LC))))
        p.wait("vector", rssem, rsv)
        p.wait("vector", nmsem, nmv)
        lastg = None
        for ic in range(NFC):
            gb, gss = gpool.acquire(p, "sync")
            gsem, gv = gpool.dma(p, "sync", gss, lambda eng, gb=gb, ic=ic, sl=sl: eng.dma_start(
                out=gb[:], in_=r.d_gf[128 * ic:128 * (ic + 1), sl]))
            t1, t1s = t1pool.acquire(p, "vector")
            p.wait("vector", gsem, gv)
            v1 = p.do("vector", lambda eng, gb=gb, t1=t1, nmb=nmb: eng.tensor_add(
                t1[:], gb[:], nmb[:]), ev, 1)
            gpool.release(gss, ev, v1)
            gnb, gns = gnpool.acquire(p, "vector")
            p.wait("vector", ev, v1)
            v2 = p.do("vector", lambda eng, t1=t1, gnb=gnb, rsb=rsb: eng.tensor_mul(
                gnb[:], t1[:], rsb[:]), ev, 1)
            t1pool.release(t1s, ev, v2)
            lastg = v2
            p.wait("gpsimd", ev, v2)
            gsm, gwv = gnpool.dma(p, "gpsimd", gns, lambda eng, gnb=gnb, ic=ic, sl=sl: eng.dma_start(
                out=r.d_gnb[128 * ic:128 * (ic + 1), sl], in_=gnb[:]))
            gnpool.release(gns, gsm, gwv)
        rsbcp.release(rss, ev, lastg)
        nmbcp.release(nms, ev, lastg)
    for s in gnpool.sems:
        p.wait("sync", s)
    woT = r.sb(s5, [128, NFC * D], BF16, base="abwo")
    bout_sb = r.sb(s5, [128, 8], F32, base="abbo")
    gbpool = Pool(r, "p6", r.sb(s5, [128, NFC * LC], BF16, 2, "abgb"))
    x0pool = Pool(r, "p7", r.sb(s5, [128, LC], F32, 2, "abx0"))
    x2pool = Pool(r, "p8", r.sb(s5, [128, LC], F32, 2, "abx2"))
    rpool = Pool(r, "p9", r.sb(s5, [128, LC], F32, 2, "abrp"))
    ps_w7 = [r.ps(s5, [128, LC], "abpw7") for _ in range(2)]
    ldo5 = r.sem("ab_ldo5")
    for dc in range(NFC):
        q, s_ = (("sync", gld), ("gpsimd", ldo5))[dc % 2]
        p.do(q, lambda eng, dc=dc: eng.dma_start(
            out=woT[:, D * dc:D * (dc + 1)],
            in_=r.d_woutT_bf[128 * dc:128 * (dc + 1), :]), s_, 16)
    p.do("sync", lambda eng: eng.dma_start(out=bout_sb[:], in_=r.d_boutp[:]), gld, 16)
    p.wait("tensor", gld)
    p.wait("tensor", ldo5)
    k = 0
    for lc in range(NLC):
        sl = slice(LC * lc, LC * (lc + 1))
        gb, gss = gbpool.acquire(p, "sync")
        gsem, gv = gbpool.dma(p, "sync", gss, lambda eng, gb=gb, sl=sl: eng.dma_start(
            out=gb[:].rearrange("p (c l) -> p c l", c=NFC),
            in_=r.d_gnb[:, sl].rearrange("(c p) l -> p c l", p=128)))
        p.wait("tensor", gsem, gv)
        vwl = None
        for oc in range(NFC):
            pw = ps_w7[k % 2]
            x0b, x0s = x0pool.acquire(p, "sync")
            x0sem, x0v = x0pool.dma(p, "sync", x0s, lambda eng, x0b=x0b, oc=oc, sl=sl: eng.dma_start(
                out=x0b[:], in_=r.d_xn32[128 * oc:128 * (oc + 1), sl]))
            x2b, x2s = x2pool.acquire(p, "sync")
            x2sem, x2v = x2pool.dma(p, "sync", x2s, lambda eng, x2b=x2b, oc=oc, sl=sl: eng.dma_start(
                out=x2b[:], in_=x_src[128 * oc:128 * (oc + 1), sl]))
            if k >= 2:
                p.wait("tensor", ev)
            vw = None
            for ic in range(NFC):
                vw = p.do("tensor", lambda eng, pw=pw, ic=ic, oc=oc, gb=gb,
                          f0=(ic == 0), fl=(ic == NFC - 1): eng.matmul(
                    pw[:], woT[:, D * ic + 128 * oc:D * ic + 128 * (oc + 1)],
                    gb[:, LC * ic:LC * ic + LC], start=f0, stop=fl), mm, 1)
            vwl = vw
            rb, rs = rpool.acquire(p, "vector")
            p.wait("vector", mm, vw)
            p.wait("vector", x0sem, x0v)
            p.wait("vector", x2sem, x2v)
            v1 = p.do("vector", lambda eng, pw=pw, rb=rb, x0b=x0b, oc=oc: eng.scalar_tensor_tensor(
                rb[:], pw[:], bout_sb[:, oc:oc + 1], x0b[:], ALU.add, ALU.add), ev, 1)
            x0pool.release(x0s, ev, v1)
            p.wait("vector", ev, v1)
            v2 = p.do("vector", lambda eng, rb=rb, x2b=x2b: eng.tensor_add(
                rb[:], rb[:], x2b[:]), ev, 1)
            x2pool.release(x2s, ev, v2)
            p.wait("gpsimd", ev, v2)
            rsm, rwv = rpool.dma(p, "gpsimd", rs, lambda eng, rb=rb, oc=oc, sl=sl: eng.dma_start(
                out=x_dst[128 * oc:128 * (oc + 1), sl], in_=rb[:]))
            rpool.release(rs, rsm, rwv)
            k += 1
        gbpool.release(gss, mm, vwl)
    for s in rpool.sems:
        p.wait("gpsimd", s)
        p.wait("sync", s)
    s5.close()
    p.barrier(bar)

def emit_ffn_experts_compact(r, scope, x_src, x_dst):
    p = r.p
    nc = r.nc
    mm = r.sem("fe_mm")
    ev = r.sem("fe_ev")
    st = r.sem("fe_st")
    gld = r.sem("fe_gld")
    sq = r.sem("fe_sq")
    SL = SLOT
    NCH = [(0, 512), (512, SL - 512)]

    se = scope.enter_context(contextlib.ExitStack())
    h1_big = r.sb(se, [128, 32 * SL], BF16, base="fch1")
    w2e = r.sb(se, [128, 32 * D], BF16, base="fcw2")
    b1_sb = r.sb(se, [128, 256], F32, base="fcb1")
    relu_t = r.sb(se, [128, SL], F32, base="fcrt")
    xep = Pool(r, "p3", [r.sb(se, [128, NFC * SL], BF16, 1, "fcxe")])
    w1pool = Pool(r, "p0", r.sb(se, [128, NFC * 128], BF16, 3, "fcw1"))
    w1fpool = Pool(r, "p4", r.sb(se, [128, NFC * 128], F32, 3, "fcw1f"))
    w2fpool = Pool(r, "p5", r.sb(se, [128, 2 * D], F32, 2, "fcw2f"))
    sepool = Pool(r, "p1", r.sb(se, [128, SL], F32, 2, "fcse"))
    srowp = Pool(r, "p6", r.sb(se, [1, SL], F32, 2, "fcsr"))
    b2rp = Pool(r, "p7", r.sb(se, [1, D], F32, 2, "fcb2"))
    opool = Pool(r, "p2", r.sb(se, [128, D], F32, 2, "fco"))
    ps_h = [r.ps(se, [128, SL], "fcph") for _ in range(2)]
    ps_o = [r.ps(se, [128, D], "fcpo") for _ in range(2)]

    ld0 = r.sem("fe_ld0")
    p.do("sync", lambda eng: eng.dma_start(out=b1_sb[:], in_=r.d_b1[:]), ld0, 16)
    p.wait("vector", ld0)

    sslot_row = r.d_sslot.rearrange("(o s) x -> (o x) s", o=1)
    prev_lastw2 = None
    v_hev = [None, None]
    v_oev = [None, None]
    for e in range(NE):
        xe, xes = xep.acquire(p, "sync")
        xsem = None
        for dk in range(NFC):
            xsem, xv = xep.dma(p, "sync", xes, lambda eng, xe=xe, dk=dk, e=e: eng.dma_start(
                out=xe[:, SL * dk:SL * (dk + 1)],
                in_=r.d_bufx[SL * e:SL * (e + 1), 128 * dk:128 * (dk + 1)],
                transpose=True))
        seb, ses = sepool.acquire(p, "sync")
        sesem, sev = sepool.dma(p, "sync", ses, lambda eng, seb=seb, e=e: eng.dma_start(
            out=seb[:], in_=sslot_row[:, SL * e:SL * (e + 1)].broadcast_to((128, SL))))
        sro, srs = srowp.acquire(p, "sync")
        srsem, srv = srowp.dma(p, "sync", srs, lambda eng, sro=sro, e=e: eng.dma_start(
            out=sro[:], in_=sslot_row[:, SL * e:SL * (e + 1)]))
        b2r, b2s = b2rp.acquire(p, "sync")
        b2sem, b2v = b2rp.dma(p, "sync", b2s, lambda eng, b2r=b2r, e=e: eng.dma_start(
            out=b2r[:], in_=r.d_b2[e:e + 1, :]))
        p.wait("tensor", xsem, xv)
        last_h = None
        for fo in range(32):
            w1f, w1fs = w1fpool.acquire(p, "sync")
            w1fsem, w1fv = w1fpool.dma(p, "sync", w1fs, lambda eng, w1f=w1f, e=e, fo=fo: eng.dma_start(
                out=w1f[:].rearrange("p (c f) -> p c f", c=NFC),
                in_=r.d_w1[D * e:D * (e + 1), 128 * fo:128 * (fo + 1)].rearrange(
                    "(c p) f -> p c f", p=128)))
            w1b, w1s = w1pool.acquire(p, "scalar")
            p.wait("scalar", w1fsem, w1fv)
            vw1c = p.do("scalar", lambda eng, w1b=w1b, w1f=w1f: eng.copy(
                w1b[:], w1f[:]), sq, 1)
            w1fpool.release(w1fs, sq, vw1c)
            ph = ps_h[fo % 2]
            p.wait("tensor", sq, vw1c)
            if v_hev[fo % 2] is not None:
                p.wait("tensor", ev, v_hev[fo % 2])
            vmm = None
            for off, wdt in NCH:
                for dk in range(NFC):
                    vmm = p.do("tensor", lambda eng, ph=ph, w1b=w1b, dk=dk, off=off,
                               wdt=wdt, xe=xe, f0=(dk == 0), fl=(dk == NFC - 1): eng.matmul(
                        ph[:, off:off + wdt],
                        w1b[:, 128 * dk:128 * (dk + 1)],
                        xe[:, SL * dk + off:SL * dk + off + wdt],
                        start=f0, stop=fl), mm, 1)
            last_h = vmm
            w1pool.release(w1s, mm, vmm)
            p.wait("vector", mm, vmm)
            p.wait("vector", ev)
            v1 = p.do("vector", lambda eng, ph=ph, e=e, fo=fo: eng.tensor_scalar(
                relu_t[:], ph[:], b1_sb[:, 32 * e + fo:32 * e + fo + 1], 0.0,
                ALU.add, ALU.max), ev, 1)
            v_hev[fo % 2] = v1
            p.wait("vector", sesem, sev)
            p.wait("vector", ev, v1)
            p.do("vector", lambda eng, fo=fo, seb=seb: eng.tensor_mul(
                h1_big[:, SL * fo:SL * (fo + 1)], relu_t[:], seb[:]), ev, 1)
        xep.release(xes, mm, last_h)
        sepool.release(ses, ev, ev.n)
        if prev_lastw2 is not None:
            p.wait("scalar", mm, prev_lastw2)
        vw2c = None
        for ck in range(16):
            q2 = "gpsimd" if ck % 2 == 0 else "sync"
            w2f, w2fs = w2fpool.acquire(p, q2)
            w2fsem, w2fv = w2fpool.dma(p, q2, w2fs, lambda eng, w2f=w2f, e=e, ck=ck: eng.dma_start(
                out=w2f[:].rearrange("p (c o) -> p c o", c=2),
                in_=r.d_w2[FF * e + 256 * ck:FF * e + 256 * (ck + 1), :].rearrange(
                    "(c p) o -> p c o", p=128)))
            p.wait("scalar", w2fsem, w2fv)
            vw2c = p.do("scalar", lambda eng, w2f=w2f, ck=ck: eng.copy(
                w2e[:, 2 * D * ck:2 * D * (ck + 1)], w2f[:]), sq, 1)
            w2fpool.release(w2fs, sq, vw2c)
        p.wait("tensor", sq, vw2c)
        p.wait("tensor", ev)
        p.wait("tensor", srsem, srv)
        p.wait("tensor", b2sem, b2v)
        lastw2 = None
        for sc in range(SL // 128):
            po = ps_o[sc % 2]
            if v_oev[sc % 2] is not None:
                p.wait("tensor", ev, v_oev[sc % 2])
            vmm = None
            for half in range(2):
                for fk in range(32):
                    vmm = p.do("tensor", lambda eng, po=po, fk=fk, sc=sc, half=half,
                               f0=(fk == 0), fl=(fk == 31): eng.matmul(
                        po[:, 512 * half:512 * (half + 1)],
                        h1_big[:, SL * fk + 128 * sc:SL * fk + 128 * sc + 128],
                        w2e[:, D * fk + 512 * half:D * fk + 512 * half + 512],
                        start=f0, stop=fl), mm, 1)
                vmm = p.do("tensor", lambda eng, po=po, sc=sc, half=half, sro=sro,
                           b2r=b2r: eng.matmul(
                    po[:, 512 * half:512 * (half + 1)],
                    sro[:, 128 * sc:128 * (sc + 1)],
                    b2r[:, 512 * half:512 * (half + 1)], start=False, stop=True), mm, 1)
            lastw2 = vmm
            ob, os_ = opool.acquire(p, "vector")
            p.wait("vector", mm, vmm)
            vcp = p.do("vector", lambda eng, po=po, ob=ob: eng.tensor_copy(ob[:], po[:]), ev, 1)
            v_oev[sc % 2] = vcp
            p.wait("gpsimd", ev, vcp)
            ssem, sv = opool.dma(p, "gpsimd", os_, lambda eng, ob=ob, e=e, sc=sc: eng.dma_start(
                out=r.d_ybuf[SL * e + 128 * sc:SL * e + 128 * (sc + 1), :], in_=ob[:]))
            opool.release(os_, ssem, sv)
        srowp.release(srs, mm, lastw2)
        b2rp.release(b2s, mm, lastw2)
        prev_lastw2 = lastw2
    for s in opool.sems:
        p.wait("sync", s)
        p.wait("gpsimd", s)
    se.close()

    scg = scope.enter_context(contextlib.ExitStack())
    slots3 = r.sb(scg, [128, 32], mybir.dt.int32, base="fcs3")
    gthp = Pool(r, "p4", r.sb(scg, [128, D], F32, 2, "fcgt"))
    x1p = Pool(r, "p5", r.sb(scg, [128, NFC * 128], F32, 2, "fcx1"))
    xoutp = Pool(r, "p1", r.sb(scg, [128, NFC * 128], F32, 2, "fcxo"))
    ps_t = [r.ps(scg, [128, 128], "fcpt") for _ in range(2)]
    v_tev = [None, None]
    p.do("sync", lambda eng: eng.dma_start(out=slots3[:], in_=r.d_slots[:]), gld, 16)
    p.wait("gpsimd", gld)
    for lt in range(32):
        gth, gs_ = gthp.acquire(p, "gpsimd")
        gsem, gv = gthp.dma(p, "gpsimd", gs_, lambda eng, gth=gth, lt=lt: eng.indirect_dma_start(
            gth[:], None, r.d_ybuf[:], IndirectOffsetOnAxis(slots3[:, lt:lt + 1], 0),
            oob_is_err=False))
        x1b, x1s = x1p.acquire(p, "sync")
        x1sem, x1v = x1p.dma(p, "sync", x1s, lambda eng, x1b=x1b, lt=lt: eng.dma_start(
            out=x1b[:].rearrange("p (c l) -> p c l", c=NFC),
            in_=x_src[:, 128 * lt:128 * (lt + 1)].rearrange("(c p) l -> p c l", p=128)))
        xob, xos = xoutp.acquire(p, "vector")
        lastv = None
        for fc in range(NFC):
            pt = ps_t[fc % 2]
            if v_tev[fc % 2] is not None:
                p.wait("tensor", ev, v_tev[fc % 2])
            p.wait("tensor", gsem, gv)
            vt = p.do("tensor", lambda eng, gth=gth, fc=fc, pt=pt: eng.transpose(
                pt[:], gth[:, 128 * fc:128 * (fc + 1)], r.ident_f32[:]), mm, 1)
            p.wait("vector", mm, vt)
            p.wait("vector", x1sem, x1v)
            lastv = p.do("vector", lambda eng, pt=pt, x1b=x1b, xob=xob, fc=fc: eng.tensor_add(
                xob[:, 128 * fc:128 * (fc + 1)], pt[:], x1b[:, 128 * fc:128 * (fc + 1)]),
                ev, 1)
            v_tev[fc % 2] = lastv
        gthp.release(gs_, mm, mm.n)
        x1p.release(x1s, ev, lastv)
        p.wait("gpsimd", ev, lastv)
        ssem, sv = xoutp.dma(p, "gpsimd", xos, lambda eng, xob=xob, lt=lt: eng.dma_start(
            out=x_dst[:, 128 * lt:128 * (lt + 1)].rearrange("(c p) l -> p c l", p=128),
            in_=xob[:].rearrange("p (c l) -> p c l", c=NFC)))
        xoutp.release(xos, ssem, sv)
    for s in xoutp.sems:
        p.wait("gpsimd", s)
        p.wait("sync", s)
    scg.close()



def kernel(**inputs):
    nc = build_nc("all")
    x = np.asarray(inputs["L0_attn_a_x"], np.float32)
    w = {k: np.asarray(v, np.float32) for k, v in inputs.items()}
    col = lambda v: np.ascontiguousarray(v.reshape(8, 128).T)
    shared = {
        "wqT": np.ascontiguousarray(w["L0_attn_a_Wq"].T),
        "wkT": np.ascontiguousarray(w["L0_attn_a_Wk"].T),
        "wvT": np.ascontiguousarray(w["L0_attn_a_Wv"].T),
        "woT": np.ascontiguousarray(w["L0_attn_a_Wo"].T),
        "wg": np.ascontiguousarray(w["L0_ffn_Wg"]),
        "w1": np.ascontiguousarray(w["L0_ffn_W1"].reshape(NE * D, FF)),
        "b1p": np.ascontiguousarray(
            w["L0_ffn_b1"].reshape(NE, 32, 128).transpose(2, 0, 1).reshape(128, 256)),
        "w2": np.ascontiguousarray(w["L0_ffn_W2"].reshape(NE * FF, D)),
        "b2m": np.ascontiguousarray(w["L0_ffn_b2"]),
        "winT": np.ascontiguousarray(w["L0_attn_b_w_in"].T),
        "binp": col(w["L0_attn_b_b_in"]),
        "cvT": np.ascontiguousarray(np.concatenate(
            [w["L0_attn_b_conv_w"][:, :, kk].T for kk in range(3)], axis=0)),
        "cbp": col(w["L0_attn_b_conv_b"]),
        "alogd": np.ascontiguousarray(w["L0_attn_b_A_log"]),
        "Dp": col(w["L0_attn_b_D"]),
        "wbcdT": np.ascontiguousarray(np.concatenate(
            [w["L0_attn_b_w_d1"].T, w["L0_attn_b_w_B"].T, w["L0_attn_b_w_C"].T], axis=1)),
        "bbcd": np.ascontiguousarray(np.concatenate(
            [w["L0_attn_b_b_d1"], w["L0_attn_b_b_B"], w["L0_attn_b_b_C"]])[:, None]),
        "wd2T": np.ascontiguousarray(w["L0_attn_b_w_d2"].T),
        "bd2p": col(w["L0_attn_b_b_d2"]),
        "woutT": np.ascontiguousarray(w["L0_attn_b_w_out"].T),
        "boutp": col(w["L0_attn_b_b_out"]),
    }
    in_maps = []
    for c in range(NCORE):
        mask = np.zeros((32, 1), np.float32)
        for rr_ in range(c):
            mask[4 * rr_, 0] = 1.0
        m = dict(shared)
        m["xT"] = np.ascontiguousarray(x[c].T)
        m["memask"] = mask
        in_maps.append(m)
    res = run_bass_kernel_spmd(nc, in_maps, core_ids=list(range(NCORE)))
    out = np.stack([np.ascontiguousarray(res.results[c]["outT"].T)
                    for c in range(NCORE)], axis=0)
    return out


# version 21
# speedup vs baseline: 1.0040x; 1.0040x over previous
import contextlib
import numpy as np
import ml_dtypes
import concourse.bass as bass
import concourse.mybir as mybir
from concourse.bass_utils import run_bass_kernel_spmd
from concourse.bass_core_rust import IndirectOffsetOnAxis

F32 = mybir.dt.float32
BF16 = mybir.dt.bfloat16
ALU = mybir.AluOpType
AF = mybir.ActivationFunctionType
AX = mybir.AxisListType

D = 1024
L = 4096
NH = 16
DH = 64
NE = 8
FF = 4096
NCORE = 8
EPS_LN = 1e-5
EPS_B = 1e-5
EPS_ATT = 1e-6
CAP = 4096
LC = 512
NLC = L // LC
NFC = D // 128
HL = L // 2
SLOT = 768

ENGINES = ("sync", "vector", "scalar", "tensor", "gpsimd")


class Sem:
    __slots__ = ("h", "n")

    def __init__(self, h):
        self.h = h
        self.n = 0


class Prog:
    def __init__(self, nc):
        self.nc = nc
        self.cmds = {e: [] for e in ENGINES}

    def do(self, engine, fn, sem=None, k=1):
        if sem is None:
            self.cmds[engine].append(fn)
            return None
        sem.n += k
        v = sem.n
        self.cmds[engine].append(lambda eng: fn(eng).then_inc(sem.h, k))
        return v

    def wait(self, engine, sem, val=None):
        v = sem.n if val is None else val
        self.cmds[engine].append(lambda eng: eng.wait_ge(sem.h, v))
        return v

    def barrier(self, bar):
        for e in ENGINES:
            bar.n += 1
            self.cmds[e].append(lambda eng, s=bar.h: eng.drain().then_inc(s, 1))
        tgt = bar.n
        for e in ENGINES:
            self.cmds[e].append(lambda eng, s=bar.h, v=tgt: eng.wait_ge(s, v))

    def replay(self, block):
        def run(eng, cmds):
            for c in cmds:
                c(eng)

        @block.sync
        def _(eng):
            run(eng, self.cmds["sync"])

        @block.vector
        def _(eng):
            run(eng, self.cmds["vector"])

        @block.scalar
        def _(eng):
            run(eng, self.cmds["scalar"])

        @block.tensor
        def _(eng):
            run(eng, self.cmds["tensor"])

        @block.gpsimd
        def _(eng):
            run(eng, self.cmds["gpsimd"])


class Pool:

    def __init__(self, r, tag, bufs):
        self.bufs = bufs
        self.sems = [r.sem(f"{tag}_s{i}") for i in range(len(bufs))]
        self.free_at = [[] for _ in bufs]
        self.k = 0

    def acquire(self, p, writer_engine):
        s = self.k % len(self.bufs)
        self.k += 1
        if self.free_at[s]:
            for sem, v in self.free_at[s]:
                p.wait(writer_engine, sem, v)
        self.free_at[s] = []
        return self.bufs[s], s

    def dma(self, p, queue, slot, fn):
        v = p.do(queue, fn, self.sems[slot], 16)
        return self.sems[slot], v

    def release(self, slot, sem, val):
        if self.free_at[slot] is None:
            self.free_at[slot] = []
        self.free_at[slot].append((sem, val))


class R:
    def __init__(self, nc, prog, stack):
        self.nc = nc
        self.p = prog
        self.stack = stack
        self.sems = {}
        self.uid = 0

    def sem(self, name):
        if name not in self.sems:
            h = self.stack.enter_context(self.nc.semaphore(name))
            self.sems[name] = Sem(h)
        return self.sems[name]

    def name(self, base):
        self.uid += 1
        return f"{base}{self.uid}"

    def sb(self, scope, shape, dt, n=1, base="t"):
        ts = [scope.enter_context(self.nc.sbuf_tensor(self.name(base), shape, dt))
              for _ in range(n)]
        return ts[0] if n == 1 else ts

    def ps(self, scope, shape, base="ps"):
        return scope.enter_context(self.nc.psum_tensor(self.name(base), shape, F32))



def emit_cast(r, scope, pairs):
    p = r.p
    cv = r.sem("c_cv")
    CC = 2048
    fpool = Pool(r, "p0", r.sb(scope, [128, CC], F32, 3, "castf"))
    bpool = Pool(r, "p1", r.sb(scope, [128, CC], BF16, 3, "castb"))
    for (src, dst, rows, cols) in pairs:
        cc = min(cols, CC)
        assert cols % cc == 0, (rows, cols)
        for i in range((rows + 127) // 128):
            pr = min(128, rows - 128 * i)
            for j in range(max(cols // CC, 1)):
                fb, fs = fpool.acquire(p, "sync")
                lsem, lv = fpool.dma(p, "sync", fs, lambda eng, fb=fb, src=src, i=i, j=j,
                                     cc=cc, pr=pr: eng.dma_start(
                    out=fb[0:pr, 0:cc], in_=src[128 * i:128 * i + pr, cc * j:cc * (j + 1)]))
                bb, bs = bpool.acquire(p, "vector")
                p.wait("vector", lsem, lv)
                v = p.do("vector", lambda eng, fb=fb, bb=bb, cc=cc, pr=pr: eng.tensor_copy(
                    bb[0:pr, 0:cc], fb[0:pr, 0:cc]), cv, 1)
                fpool.release(fs, cv, v)
                p.wait("gpsimd", cv)
                ssem, sv = bpool.dma(p, "gpsimd", bs, lambda eng, bb=bb, dst=dst, i=i, j=j,
                                     cc=cc, pr=pr: eng.dma_start(
                    out=dst[128 * i:128 * i + pr, cc * j:cc * (j + 1)], in_=bb[0:pr, 0:cc]))
                bpool.release(bs, ssem, sv)
    for s in bpool.sems:
        p.wait("gpsimd", s)



def emit_ln(r, scope, src, xn_bf, xn_f32=None):
    p = r.p
    sq = r.sem("ln_sq")
    mm = r.sem("ln_mm")
    ev = r.sem("ln_ev")
    st = r.sem("ln_st")

    xpool = Pool(r, "p0", r.sb(scope, [128, LC], F32, 3, "lnx"))
    sqpool = Pool(r, "p1", r.sb(scope, [128, LC], F32, 2, "lnsq"))
    mu_row = r.sb(scope, [1, L], F32, base="lnmu")
    rs_row = r.sb(scope, [1, L], F32, base="lnrs")
    srow = r.sb(scope, [1, LC], F32, base="lns")
    qrow = r.sb(scope, [1, LC], F32, base="lnq")
    trow = r.sb(scope, [1, LC], F32, base="lnt")
    ps_s = r.ps(scope, [1, LC], "lnps")
    ps_q = r.ps(scope, [1, LC], "lnpq")

    for lc in range(NLC):
        if lc > 0:
            p.wait("tensor", ev)
        for fc in range(NFC):
            xb, xs = xpool.acquire(p, "sync")
            lsem, lv = xpool.dma(p, "sync", xs, lambda eng, xb=xb, fc=fc, lc=lc: eng.dma_start(
                out=xb[:], in_=src[128 * fc:128 * (fc + 1), LC * lc:LC * (lc + 1)]))
            sb_, ss_ = sqpool.acquire(p, "scalar")
            p.wait("scalar", lsem, lv)
            vq = p.do("scalar", lambda eng, xb=xb, sb_=sb_: eng.activation(
                sb_[:], xb[:], AF.Square), sq, 1)
            p.wait("tensor", lsem, lv)
            v1 = p.do("tensor", lambda eng, xb=xb, f0=(fc == 0), fl=(fc == NFC - 1): eng.matmul(
                ps_s[:], r.ones_col[:], xb[:], start=f0, stop=fl), mm, 1)
            xpool.release(xs, mm, v1)
            xpool.release(xs, sq, vq)
            p.wait("tensor", sq, vq)
            v2 = p.do("tensor", lambda eng, sb_=sb_, f0=(fc == 0), fl=(fc == NFC - 1): eng.matmul(
                ps_q[:], r.ones_col[:], sb_[:], start=f0, stop=fl), mm, 1)
            sqpool.release(ss_, mm, v2)
        sl = slice(LC * lc, LC * (lc + 1))
        p.wait("vector", mm)
        p.do("vector", lambda eng, sl=sl: eng.tensor_single_scalar(
            mu_row[:, sl], ps_s[:], 1.0 / D, ALU.mult), ev, 1)
        p.do("vector", lambda eng: eng.tensor_single_scalar(
            qrow[:], ps_q[:], 1.0 / D, ALU.mult), ev, 1)
        p.wait("vector", ev)
        p.do("vector", lambda eng, sl=sl: eng.tensor_mul(
            trow[:], mu_row[:, sl], mu_row[:, sl]), ev, 1)
        p.wait("vector", ev)
        p.do("vector", lambda eng: eng.tensor_sub(srow[:], qrow[:], trow[:]), ev, 1)
        p.wait("scalar", ev)
        p.do("scalar", lambda eng: eng.activation(trow[:], srow[:], AF.Ln, bias=EPS_LN), ev, 1)
        p.wait("scalar", ev)
        p.do("scalar", lambda eng, sl=sl: eng.activation(
            rs_row[:, sl], trow[:], AF.Exp, scale=-0.5), ev, 1)
    p.wait("gpsimd", ev)
    p.do("gpsimd", lambda eng: eng.dma_start(out=r.d_murow[:], in_=mu_row[:]), st, 16)
    p.do("gpsimd", lambda eng: eng.dma_start(out=r.d_rsrow[:], in_=rs_row[:]), st, 16)
    p.wait("sync", st)

    mupool = Pool(r, "p2", r.sb(scope, [128, LC], F32, 2, "lnmub"))
    rspool = Pool(r, "p3", r.sb(scope, [128, LC], F32, 2, "lnrsb"))
    tpool = Pool(r, "p4", r.sb(scope, [128, LC], F32, 2, "lntb"))
    opool = Pool(r, "p5", r.sb(scope, [128, LC], BF16, 2, "lnob"))
    fpool = Pool(r, "p6", r.sb(scope, [128, LC], F32, 2, "lnfb")) if xn_f32 is not None else None
    for lc in range(NLC):
        sl = slice(LC * lc, LC * (lc + 1))
        mu_bc, mus = mupool.acquire(p, "sync")
        msem, mv = mupool.dma(p, "sync", mus, lambda eng, mu_bc=mu_bc, sl=sl: eng.dma_start(
            out=mu_bc[:], in_=r.d_murow[:, sl].broadcast_to((128, LC))))
        rs_bc, rss = rspool.acquire(p, "sync")
        rsem, rv = rspool.dma(p, "sync", rss, lambda eng, rs_bc=rs_bc, sl=sl: eng.dma_start(
            out=rs_bc[:], in_=r.d_rsrow[:, sl].broadcast_to((128, LC))))
        p.wait("vector", msem, mv)
        p.wait("vector", rsem, rv)
        last = None
        for fc in range(NFC):
            xb, xs = xpool.acquire(p, "sync")
            lsem, lv = xpool.dma(p, "sync", xs, lambda eng, xb=xb, fc=fc, sl=sl: eng.dma_start(
                out=xb[:], in_=src[128 * fc:128 * (fc + 1), sl]))
            tb, ts = tpool.acquire(p, "vector")
            p.wait("vector", lsem, lv)
            v = p.do("vector", lambda eng, xb=xb, tb=tb, mu_bc=mu_bc: eng.tensor_sub(
                tb[:], xb[:], mu_bc[:]), ev, 1)
            xpool.release(xs, ev, v)
            if xn_f32 is not None:
                fb, fs = fpool.acquire(p, "vector")
                p.wait("vector", ev, v)
                v2 = p.do("vector", lambda eng, tb=tb, fb=fb, rs_bc=rs_bc: eng.tensor_mul(
                    fb[:], tb[:], rs_bc[:]), ev, 1)
                tpool.release(ts, ev, v2)
                ob, os_ = opool.acquire(p, "vector")
                p.wait("vector", ev, v2)
                v3 = p.do("vector", lambda eng, fb=fb, ob=ob: eng.tensor_copy(
                    ob[:], fb[:]), ev, 1)
                last = v3
                p.wait("gpsimd", ev, v3)
                s1, w1 = opool.dma(p, "gpsimd", os_, lambda eng, ob=ob, fc=fc, sl=sl: eng.dma_start(
                    out=xn_bf[128 * fc:128 * (fc + 1), sl], in_=ob[:]))
                opool.release(os_, s1, w1)
                s2, w2 = fpool.dma(p, "gpsimd", fs, lambda eng, fb=fb, fc=fc, sl=sl: eng.dma_start(
                    out=xn_f32[128 * fc:128 * (fc + 1), sl], in_=fb[:]))
                fpool.release(fs, s2, w2)
            else:
                ob, os_ = opool.acquire(p, "vector")
                p.wait("vector", ev, v)
                v2 = p.do("vector", lambda eng, tb=tb, ob=ob, rs_bc=rs_bc: eng.tensor_mul(
                    ob[:], tb[:], rs_bc[:]), ev, 1)
                tpool.release(ts, ev, v2)
                last = v2
                p.wait("gpsimd", ev, v2)
                s1, w1 = opool.dma(p, "gpsimd", os_, lambda eng, ob=ob, fc=fc, sl=sl: eng.dma_start(
                    out=xn_bf[128 * fc:128 * (fc + 1), sl], in_=ob[:]))
                opool.release(os_, s1, w1)
        mupool.release(mus, ev, last)
        rspool.release(rss, ev, last)
    for pool in (opool, fpool) if fpool is not None else (opool,):
        for s in pool.sems:
            p.wait("gpsimd", s)



def emit_attn_a(r, scope, x_src, xn_bf, x_dst, stop_stage=None):
    p = r.p
    nc = r.nc
    ld = r.sem("aa_ld")
    mm = r.sem("aa_mm")
    ev = r.sem("aa_ev")
    st = r.sem("aa_st")
    sp = r.sem("aa_sp")

    s1 = scope.enter_context(contextlib.ExitStack())
    WT = r.sb(s1, [128, NFC * 3 * D], BF16, base="aaWT")
    ldq = {q: r.sem("aa_ld_" + q) for q in ("sync", "scalar", "gpsimd")}
    ldq["sync"] = ld
    for m, mat in enumerate((r.d_wqT_bf, r.d_wkT_bf, r.d_wvT_bf)):
        for dc in range(NFC):
            q = ("sync", "scalar", "gpsimd")[(m * NFC + dc) % 3]
            p.do(q, lambda eng, mat=mat, dc=dc, m=m: eng.dma_start(
                out=WT[:, 3 * D * dc + D * m: 3 * D * dc + D * (m + 1)],
                in_=mat[128 * dc:128 * (dc + 1), :]), ldq[q], 16)
    xnpool = Pool(r, "p0", r.sb(s1, [128, D], BF16, 2, "aaxn"))
    NEN = 6
    pss = [r.ps(s1, [128, 512], "aapq") for _ in range(NEN)]
    m_sb = r.sb(s1, [128, 512], F32, base="aam")
    a_sb = r.sb(s1, [128, 512], F32, base="aaa")
    e_sb = r.sb(s1, [128, 512], F32, base="aae")
    l_sb = r.sb(s1, [128, 512], F32, base="aal")
    opool = Pool(r, "p1", r.sb(s1, [128, 512], BF16, 3, "aao"))
    wld = p.wait("tensor", ld)
    for q in ("scalar", "gpsimd"):
        p.wait("tensor", ldq[q])
    for lt in range(L // 128):
        xb, xs = xnpool.acquire(p, "sync")
        xsem, xv = xnpool.dma(p, "sync", xs, lambda eng, xb=xb, lt=lt: eng.dma_start(
            out=xb[:].rearrange("p (c l) -> p c l", c=NFC),
            in_=xn_bf[:, 128 * lt:128 * (lt + 1)].rearrange(
                "(c p) l -> p c l", p=128)))
        p.wait("tensor", xsem, xv)
        if lt > 0:
            p.wait("tensor", ev)
        vmm = None
        for dc in range(NFC):
            for en in range(NEN):
                vmm = p.do("tensor", lambda eng, xb=xb, dc=dc, en=en,
                           f0=(dc == 0), fl=(dc == NFC - 1): eng.matmul(
                    pss[en][:], xb[:, 128 * dc:128 * (dc + 1)],
                    WT[:, 3 * D * dc + 512 * en: 3 * D * dc + 512 * (en + 1)],
                    start=f0, stop=fl), mm, 1)
        xnpool.release(xs, mm, vmm)
        for en in range(NEN):
            dst = (r.d_qtok, r.d_ktok, r.d_vtok)[en // 2]
            csl = slice(512 * (en % 2), 512 * (en % 2 + 1))
            ob, os_ = opool.acquire(p, "vector")
            if en < 4:
                p.wait("scalar", ev)
                p.wait("scalar", mm)
                vab = p.do("scalar", lambda eng, en=en: eng.activation(
                    a_sb[:], pss[en][:], AF.Abs), sp, 1)
                p.wait("vector", ev)
                p.wait("vector", sp, vab)
                p.do("vector", lambda eng, en=en: eng.tensor_scalar_max(
                    m_sb[:], pss[en][:], 0.0), ev, 1)
                p.wait("scalar", sp)
                p.do("scalar", lambda eng: eng.activation(
                    e_sb[:], a_sb[:], AF.Exp, scale=-1.0), sp, 1)
                p.wait("scalar", sp)
                p.do("scalar", lambda eng: eng.activation(
                    l_sb[:], e_sb[:], AF.Ln, bias=1.0), sp, 1)
                p.wait("vector", sp)
                p.wait("vector", ev)
                v = p.do("vector", lambda eng, ob=ob: eng.scalar_tensor_tensor(
                    ob[:], m_sb[:], EPS_ATT, l_sb[:], ALU.add, ALU.add), ev, 1)
            else:
                p.wait("vector", mm)
                v = p.do("vector", lambda eng, ob=ob, en=en: eng.tensor_copy(
                    ob[:], pss[en][:]), ev, 1)
            p.wait("gpsimd", ev, v)
            wsem, wv = opool.dma(p, "gpsimd", os_, lambda eng, ob=ob, dst=dst, lt=lt, csl=csl: eng.dma_start(
                out=dst[128 * lt:128 * (lt + 1), csl], in_=ob[:]))
            opool.release(os_, wsem, wv)
    for s in opool.sems:
        p.wait("gpsimd", s)

    for s in opool.sems:
        p.wait("sync", s)
    if stop_stage == 1:
        s1.close()
        return
    s1.close()
    p.wait("tensor", ev)

    ao_full = r.sb(scope, [128, 32 * D], BF16, base="aaao")
    s2 = scope.enter_context(contextlib.ExitStack())
    qpool = Pool(r, "p2", r.sb(s2, [128, D], BF16, 2, "aaq"))
    kpool = Pool(r, "p3", r.sb(s2, [128, D], BF16, 2, "aak"))
    vpool = Pool(r, "p4", r.sb(s2, [128, D], BF16, 2, "aav"))
    kfT = r.sb(s2, [64, 128], BF16, base="aakfT")
    qfT = r.sb(s2, [64, 128], BF16, base="aaqfT")
    Sb = r.sb(s2, [128, 128], BF16, base="aaSb")
    zi_bf = r.sb(s2, [128, 64], BF16, base="aazi")
    den_sc = r.sb(s2, [128, 64], BF16, base="aadsc")
    den = r.sb(s2, [128, 1], F32, base="aaden")
    den2 = r.sb(s2, [128, 1], F32, base="aaden2")
    rcol = r.sb(s2, [128, 1], F32, base="aar")
    KVbf = r.sb(s2, [64, D], BF16, base="aaKV")
    zrow = r.sb(s2, [1, D], BF16, base="aaz")
    KV32 = r.sb(s2, [64, D], F32, 2, base="aaKV32")
    z32 = r.sb(s2, [1, D], F32, 2, base="aaz32")
    ps_kv = r.ps(s2, [64, D], "aapkv")
    ps_z = [r.ps(s2, [1, 512], "aapz") for _ in range(2)]
    ps_tr = s2.enter_context(nc.psum_tensor(r.name("aaptr"), [64, 256], BF16))
    ps_S = r.ps(s2, [128, 128], "aapS")
    ps_zi = r.ps(s2, [128, 64], "aapzi")
    ps_o = r.ps(s2, [128, 64], "aapo")

    p.do("vector", lambda eng: eng.memset(KVbf[:], 0.0), ev, 1)
    p.do("vector", lambda eng: eng.memset(zrow[:], 0.0), ev, 1)
    p.do("vector", lambda eng: eng.memset(KV32[0][:], 0.0), ev, 1)
    p.do("vector", lambda eng: eng.memset(z32[0][:], 0.0), ev, 1)
    for tc in range(L // 128):
        qb, qs = qpool.acquire(p, "sync")
        qsem, qv = qpool.dma(p, "sync", qs, lambda eng, qb=qb, tc=tc: eng.dma_start(
            out=qb[:], in_=r.d_qtok[128 * tc:128 * (tc + 1), :]))
        kb, ks = kpool.acquire(p, "sync")
        ksem, kv_ = kpool.dma(p, "sync", ks, lambda eng, kb=kb, tc=tc: eng.dma_start(
            out=kb[:], in_=r.d_ktok[128 * tc:128 * (tc + 1), :]))
        vb, vs = vpool.acquire(p, "sync")
        vsem, vv_ = vpool.dma(p, "sync", vs, lambda eng, vb=vb, tc=tc: eng.dma_start(
            out=vb[:], in_=r.d_vtok[128 * tc:128 * (tc + 1), :]))
        p.wait("tensor", qsem, qv)
        p.wait("tensor", ksem, kv_)
        p.wait("tensor", vsem, vv_)
        lastv = None
        for h in range(NH):
            hs = slice(64 * h, 64 * (h + 1))
            p.wait("tensor", ev)
            p.do("tensor", lambda eng, kb=kb, hs=hs: eng.transpose(
                ps_tr[:, 0:128], kb[:, hs], r.ident_bf[:]), mm, 1)
            p.do("tensor", lambda eng, qb=qb, hs=hs: eng.transpose(
                ps_tr[:, 128:256], qb[:, hs], r.ident_bf[:]), mm, 1)
            p.wait("vector", mm)
            p.do("vector", lambda eng: eng.tensor_copy(kfT[:], ps_tr[:, 0:128]), ev, 1)
            vtr = p.do("vector", lambda eng: eng.tensor_copy(qfT[:], ps_tr[:, 128:256]), ev, 1)
            p.wait("tensor", ev, vtr)
            vS = p.do("tensor", lambda eng: eng.matmul(
                ps_S[:], kfT[:], qfT[:], start=True, stop=True), mm, 1)
            p.wait("vector", mm, vS)
            vSb = p.do("vector", lambda eng: eng.tensor_mul(
                Sb[:], ps_S[:], r.mask_f32[:]), ev, 1)
            vzi = p.do("tensor", lambda eng, kb=kb, hs=hs: eng.matmul(
                ps_zi[:], r.mask_bf[:], kb[:, hs], start=True, stop=False), mm, 1)
            vzi = p.do("tensor", lambda eng, hs=hs: eng.matmul(
                ps_zi[:], r.ones_row_bf[:, 0:128], zrow[:, hs], start=False, stop=True), mm, 1)
            p.wait("vector", mm, vzi)
            p.wait("vector", ev)
            vzb = p.do("vector", lambda eng: eng.tensor_copy(zi_bf[:], ps_zi[:]), ev, 1)
            p.wait("vector", ev, vzb)
            p.do("vector", lambda eng, qb=qb, hs=hs: eng.scalar_tensor_tensor(
                den_sc[:], qb[:, hs], 1.0, zi_bf[:], ALU.mult, ALU.mult,
                accum_out=den[:]), ev, 1)
            p.wait("vector", ev)
            p.do("vector", lambda eng: eng.tensor_scalar_add(den2[:], den[:], EPS_ATT), ev, 1)
            p.wait("vector", ev)
            vr = p.do("vector", lambda eng: eng.reciprocal(rcol[:], den2[:]), ev, 1)
            p.wait("tensor", ev, vSb)
            p.do("tensor", lambda eng, vb=vb, hs=hs: eng.matmul(
                ps_o[:], Sb[:], vb[:, hs], start=True, stop=False), mm, 1)
            vo = p.do("tensor", lambda eng, hs=hs: eng.matmul(
                ps_o[:], qfT[:], KVbf[:, hs], start=False, stop=True), mm, 1)
            vkv = p.do("tensor", lambda eng, kb=kb, vb=vb, hs=hs: eng.matmul(
                ps_kv[:, hs], kb[:, hs], vb[:, hs], start=True, stop=True), mm, 1)
            vzu = p.do("tensor", lambda eng, kb=kb, hs=hs, h=h: eng.matmul(
                ps_z[h // 8][:, 64 * (h % 8):64 * (h % 8 + 1)], r.ones_col_bf[:],
                kb[:, hs], start=True, stop=True), mm, 1)
            p.wait("vector", mm, vo)
            p.wait("vector", ev, vr)
            lastv = p.do("vector", lambda eng, tc=tc, h=h: eng.tensor_scalar(
                ao_full[:, 1024 * tc + 64 * h: 1024 * tc + 64 * (h + 1)], ps_o[:],
                rcol[:], None, ALU.mult), ev, 1)
            cur, prv = (tc + 1) % 2, tc % 2
            p.wait("vector", mm, vkv)
            va = p.do("vector", lambda eng, hs=hs, cur=cur, prv=prv: eng.tensor_add(
                KV32[cur][:, hs], KV32[prv][:, hs], ps_kv[:, hs]), ev, 1)
            p.wait("vector", ev, va)
            p.do("vector", lambda eng, hs=hs, cur=cur: eng.tensor_copy(
                KVbf[:, hs], KV32[cur][:, hs]), ev, 1)
            p.wait("vector", mm, vzu)
            vz2 = p.do("vector", lambda eng, hs=hs, h=h, cur=cur, prv=prv: eng.tensor_add(
                z32[cur][:, hs], z32[prv][:, hs],
                ps_z[h // 8][:, 64 * (h % 8):64 * (h % 8 + 1)]), ev, 1)
            p.wait("vector", ev, vz2)
            lastz = p.do("vector", lambda eng, hs=hs, cur=cur: eng.tensor_copy(
                zrow[:, hs], z32[cur][:, hs]), ev, 1)
        qpool.release(qs, ev, lastv)
        kpool.release(ks, ev, ev.n)
        vpool.release(vs, ev, lastv)

    if stop_stage == 2:
        dmp = r.nc.dram_tensor("aodump", [128, 32 * D], BF16, kind="ExternalOutput").ap()
        dsem = r.sem("aadmp")
        p.wait("gpsimd", ev)
        for tcd in range(32):
            p.do("gpsimd", lambda eng, tcd=tcd: eng.dma_start(
                out=dmp[:, 1024 * tcd:1024 * (tcd + 1)],
                in_=ao_full[:, 1024 * tcd:1024 * (tcd + 1)]), dsem, 16)
        p.wait("gpsimd", dsem)
        s2.close()
        return
    s2.close()
    p.wait("tensor", ev)

    aoT = r.sb(scope, [128, NFC * L], BF16, base="aaaoT")
    s3 = scope.enter_context(contextlib.ExitStack())
    ps_t2 = [s3.enter_context(nc.psum_tensor(r.name("aapt2"), [128, 128], BF16))
             for _ in range(2)]
    v_t2ev = [None, None]
    for tc in range(L // 128):
        for fc in range(NFC):
            half = (tc * NFC + fc) % 2
            if v_t2ev[half] is not None:
                p.wait("tensor", ev, v_t2ev[half])
            vt = p.do("tensor", lambda eng, tc=tc, fc=fc, half=half: eng.transpose(
                ps_t2[half][:],
                ao_full[:, 1024 * tc + 128 * fc: 1024 * tc + 128 * (fc + 1)],
                r.ident_bf[:]), mm, 1)
            p.wait("vector", mm, vt)
            v_t2ev[half] = p.do("vector", lambda eng, tc=tc, fc=fc, half=half: eng.tensor_copy(
                aoT[:, L * fc + 128 * tc: L * fc + 128 * (tc + 1)],
                ps_t2[half][:]), ev, 1)
    s3.close()
    p.wait("tensor", ev)

    s4 = scope.enter_context(contextlib.ExitStack())
    WoT = r.sb(s4, [128, NFC * D], BF16, base="aaWo")
    ld5 = r.sem("aa_ld5")
    for dc in range(NFC):
        q, s_ = (("sync", ld), ("gpsimd", ld5))[dc % 2]
        p.do(q, lambda eng, dc=dc: eng.dma_start(
            out=WoT[:, D * dc:D * (dc + 1)],
            in_=r.d_woT_bf[128 * dc:128 * (dc + 1), :]), s_, 16)
    xpool2 = Pool(r, "p5", r.sb(s4, [128, LC], F32, 2, "aax2"))
    rpool = Pool(r, "p6", r.sb(s4, [128, LC], F32, 2, "aares"))
    ps_w = [r.ps(s4, [128, 512], "aapw") for _ in range(2)]
    p.wait("tensor", ld)
    p.wait("tensor", ld5)
    p.wait("tensor", ev)
    k = 0
    v_w4ev = [None, None]
    for ec in range(NFC):
        for lc in range(NLC):
            pw = ps_w[k % 2]
            xb, xs = xpool2.acquire(p, "sync")
            x2sem, x2v = xpool2.dma(p, "sync", xs, lambda eng, xb=xb, ec=ec, lc=lc: eng.dma_start(
                out=xb[:], in_=x_src[128 * ec:128 * (ec + 1), LC * lc:LC * (lc + 1)]))
            if v_w4ev[k % 2] is not None:
                p.wait("tensor", ev, v_w4ev[k % 2])
            vw = None
            for dc in range(NFC):
                vw = p.do("tensor", lambda eng, pw=pw, dc=dc, ec=ec, lc=lc,
                          f0=(dc == 0), fl=(dc == NFC - 1): eng.matmul(
                    pw[:], WoT[:, D * dc + 128 * ec: D * dc + 128 * (ec + 1)],
                    aoT[:, L * dc + LC * lc: L * dc + LC * (lc + 1)],
                    start=f0, stop=fl), mm, 1)
            rb, rs = rpool.acquire(p, "vector")
            p.wait("vector", mm, vw)
            p.wait("vector", x2sem, x2v)
            v = p.do("vector", lambda eng, pw=pw, xb=xb, rb=rb: eng.tensor_add(
                rb[:], pw[:], xb[:]), ev, 1)
            v_w4ev[k % 2] = v
            xpool2.release(xs, ev, v)
            p.wait("gpsimd", ev, v)
            wsem, wv = rpool.dma(p, "gpsimd", rs, lambda eng, rb=rb, ec=ec, lc=lc: eng.dma_start(
                out=x_dst[128 * ec:128 * (ec + 1), LC * lc:LC * (lc + 1)], in_=rb[:]))
            rpool.release(rs, wsem, wv)
            k += 1
    for s in rpool.sems:
        p.wait("gpsimd", s)


def build_nc(stop_after="all", taps=(), skip_cc=0):
    nc = bass.Bass("TRN2", target_bir_lowering=False)
    with contextlib.ExitStack() as stack:
        prog = Prog(nc)
        r = R(nc, prog, stack)
        r.skip_cc = skip_cc
        p = prog

        r.d_xT = nc.dram_tensor("xT", [D, L], F32, kind="ExternalInput").ap()
        r.d_wqT = nc.dram_tensor("wqT", [D, D], F32, kind="ExternalInput").ap()
        r.d_wkT = nc.dram_tensor("wkT", [D, D], F32, kind="ExternalInput").ap()
        r.d_wvT = nc.dram_tensor("wvT", [D, D], F32, kind="ExternalInput").ap()
        r.d_woT = nc.dram_tensor("woT", [D, D], F32, kind="ExternalInput").ap()
        r.d_wg = nc.dram_tensor("wg", [D, NE], F32, kind="ExternalInput").ap()
        r.d_w1 = nc.dram_tensor("w1", [NE * D, FF], F32, kind="ExternalInput").ap()
        r.d_b1 = nc.dram_tensor("b1p", [128, 256], F32, kind="ExternalInput").ap()
        r.d_w2 = nc.dram_tensor("w2", [NE * FF, D], F32, kind="ExternalInput").ap()
        r.d_b2 = nc.dram_tensor("b2m", [NE, D], F32, kind="ExternalInput").ap()
        r.d_memask = nc.dram_tensor("memask", [32, 1], F32, kind="ExternalInput").ap()
        r.d_winT = nc.dram_tensor("winT", [D, D], F32, kind="ExternalInput").ap()
        r.d_binp = nc.dram_tensor("binp", [128, 8], F32, kind="ExternalInput").ap()
        r.d_cvT = nc.dram_tensor("cvT", [3 * D, D], F32, kind="ExternalInput").ap()
        r.d_cbp = nc.dram_tensor("cbp", [128, 8], F32, kind="ExternalInput").ap()
        r.d_alog = nc.dram_tensor("alogd", [D, 16], F32, kind="ExternalInput").ap()
        r.d_Dp = nc.dram_tensor("Dp", [128, 8], F32, kind="ExternalInput").ap()
        r.d_wbcdT = nc.dram_tensor("wbcdT", [D, 96], F32, kind="ExternalInput").ap()
        r.d_bbcd = nc.dram_tensor("bbcd", [96, 1], F32, kind="ExternalInput").ap()
        r.d_wd2T = nc.dram_tensor("wd2T", [64, D], F32, kind="ExternalInput").ap()
        r.d_bd2p = nc.dram_tensor("bd2p", [128, 8], F32, kind="ExternalInput").ap()
        r.d_woutT = nc.dram_tensor("woutT", [D, D], F32, kind="ExternalInput").ap()
        r.d_boutp = nc.dram_tensor("boutp", [128, 8], F32, kind="ExternalInput").ap()

        r.d_out = nc.dram_tensor("outT", [D, L], F32, kind="ExternalOutput").ap()
        r.d_x1 = nc.dram_tensor("x1", [D, L], F32).ap()
        r.d_xnbf = nc.dram_tensor("xnbf", [D, L], BF16).ap()
        r.d_murow = nc.dram_tensor("murow", [1, L], F32).ap()
        r.d_rsrow = nc.dram_tensor("rsrow", [1, L], F32).ap()
        r.d_qtok = nc.dram_tensor("qtok", [L, D], BF16).ap()
        r.d_ktok = nc.dram_tensor("ktok", [L, D], BF16).ap()
        r.d_vtok = nc.dram_tensor("vtok", [L, D], BF16).ap()
        r.d_x2 = nc.dram_tensor("x2", [D, L], F32).ap()
        r.d_xn32 = nc.dram_tensor("xn32", [D, L], F32).ap()
        r.d_w1bf = nc.dram_tensor("w1b", [NE * D, FF], BF16).ap()
        r.d_w2bf = nc.dram_tensor("w2b", [NE * FF, D], BF16).ap()
        r.d_sall = nc.dram_tensor("sall", [NE, L], F32).ap()
        r.d_facc = nc.dram_tensor("facc", [D, L], F32).ap()
        r.d_cnt256 = nc.dram_tensor("cnt256", [1, 256], F32).ap()
        r.d_goffd = nc.dram_tensor("goffd", [1, 8], F32).ap()
        r.d_slots = nc.dram_tensor("slots", [128, 32], mybir.dt.int32).ap()
        r.d_sslot = nc.dram_tensor("sslot", [NE * SLOT, 1], F32).ap()
        r.d_bufx = nc.dram_tensor("bufx", [NE * SLOT, D], BF16).ap()
        r.d_ybuf = nc.dram_tensor("ybuf", [NE * SLOT, D], F32).ap()
        if stop_after == "ffn1_dbg":
            r.dbg_ffn = True
            r.d_h1dump = nc.dram_tensor("h1dump", [128, 32 * LGT], BF16,
                                        kind="ExternalOutput").ap()
        r.d_run256 = nc.dram_tensor("run256", [32, 8], F32).ap()
        r.d_x3 = nc.dram_tensor("x3", [D, L], F32).ap()
        r.d_winT_bf = nc.dram_tensor("winTb", [D, D], BF16).ap()
        r.d_cvT_bf = nc.dram_tensor("cvTb", [3 * D, D], BF16).ap()
        r.d_wbcdT_bf = nc.dram_tensor("wbcdTb", [D, 96], BF16).ap()
        r.d_wd2T_bf = nc.dram_tensor("wd2Tb", [64, D], BF16).ap()
        r.d_woutT_bf = nc.dram_tensor("woutTb", [D, D], BF16).ap()
        r.d_h0f = nc.dram_tensor("h0f", [D, L], F32).ap()
        r.d_h0pb = nc.dram_tensor("h0pb", [D, L + 2], BF16).ap()
        r.d_hcf = nc.dram_tensor("hcf", [D, L], F32).ap()
        r.d_h2f = nc.dram_tensor("h2f", [D, L], F32).ap()
        r.d_h2b = nc.dram_tensor("h2b", [D, L], BF16).ap()
        r.d_gf = nc.dram_tensor("gf", [D, L], F32).ap()
        r.d_gnb = nc.dram_tensor("gnb", [D, L], BF16).ap()
        r.d_bcrows = nc.dram_tensor("bcrows", [32, L], F32).ap()
        r.d_gcells = nc.dram_tensor("gcells", [1, 8], F32).ap()
        r.d_cin1 = nc.dram_tensor("cin1", [4, 8], F32).ap()
        r.d_cout1 = nc.dram_tensor("cout1", [32, 8], F32).ap()
        r.d_cin2 = nc.dram_tensor("cin2", [4, 8], F32).ap()
        r.d_cout2 = nc.dram_tensor("cout2", [32, 8], F32).ap()
        r.d_wqT_bf = nc.dram_tensor("wqTb", [D, D], BF16).ap()
        r.d_wkT_bf = nc.dram_tensor("wkTb", [D, D], BF16).ap()
        r.d_wvT_bf = nc.dram_tensor("wvTb", [D, D], BF16).ap()
        r.d_woT_bf = nc.dram_tensor("woTb", [D, D], BF16).ap()

        tri = np.triu(np.ones((128, 128), np.float32))
        sut = np.triu(np.ones((128, 128), np.float32), 1)
        su32 = np.triu(np.ones((32, 32), np.float32), 1)
        r.c_ident = nc.inline_tensor(np.eye(128, dtype=ml_dtypes.bfloat16), name="idbf").ap()
        r.c_mask_bf = nc.inline_tensor(tri.astype(ml_dtypes.bfloat16), name="maskbf").ap()
        r.c_mask_f32 = nc.inline_tensor(tri, name="maskf").ap()
        r.c_ones_col = nc.inline_tensor(np.ones((128, 1), np.float32), name="ones_c").ap()
        r.c_ones_col_bf = nc.inline_tensor(np.ones((128, 1), ml_dtypes.bfloat16), name="ones_cb").ap()
        r.c_ones_row_bf = nc.inline_tensor(np.ones((1, 512), ml_dtypes.bfloat16), name="ones_rb").ap()
        r.c_ident_f32 = nc.inline_tensor(np.eye(128, dtype=np.float32), name="idf").ap()
        r.c_su_f32 = nc.inline_tensor(sut, name="suf").ap()
        r.c_su32_f32 = nc.inline_tensor(su32, name="su32f").ap()
        r.c_ones_row_f32 = nc.inline_tensor(np.ones((1, 512), np.float32), name="ones_rf").ap()
        r.c_iotaE = nc.inline_tensor(
            np.tile(np.arange(8, dtype=np.float32), (128, 1)), name="iotaE").ap()

        gscope = stack
        r.ident_bf = r.sb(gscope, [128, 128], BF16, base="identbf")
        r.mask_bf = r.sb(gscope, [128, 128], BF16, base="maskbf")
        r.mask_f32 = r.sb(gscope, [128, 128], F32, base="maskf")
        r.ones_col = r.sb(gscope, [128, 1], F32, base="onesc")
        r.ones_col_bf = r.sb(gscope, [128, 1], BF16, base="onescb")
        r.ones_row_bf = r.sb(gscope, [1, 512], BF16, base="onesrb")
        r.ident_f32 = r.sb(gscope, [128, 128], F32, base="identf")
        r.su_f32 = r.sb(gscope, [128, 128], F32, base="suf")
        r.su32_f32 = r.sb(gscope, [32, 32], F32, base="su32f")
        r.ones_row_f32 = r.sb(gscope, [1, 512], F32, base="onesrf")
        r.iotaE = r.sb(gscope, [128, 8], F32, base="iotaE")
        cl = r.sem("const_ld")
        for dst, srcc in ((r.ident_bf, r.c_ident), (r.mask_bf, r.c_mask_bf),
                          (r.mask_f32, r.c_mask_f32), (r.ones_col, r.c_ones_col),
                          (r.ones_col_bf, r.c_ones_col_bf), (r.ones_row_bf, r.c_ones_row_bf),
                          (r.ident_f32, r.c_ident_f32), (r.su_f32, r.c_su_f32),
                          (r.su32_f32, r.c_su32_f32), (r.ones_row_f32, r.c_ones_row_f32),
                          (r.iotaE, r.c_iotaE)):
            p.do("sync", lambda eng, d=dst, s=srcc: eng.dma_start(out=d[:], in_=s[:]), cl, 16)
        p.wait("sync", cl)

        cvals = [EPS_LN, 1.0]
        carr = np.array(cvals, np.float32)[None, :].repeat(128, 0)
        c_biases = nc.inline_tensor(np.ascontiguousarray(carr), name="biasconsts").ap()
        bias_sb = r.sb(gscope, [128, len(cvals)], F32, base="biasc")
        p.do("sync", lambda eng: eng.dma_start(out=bias_sb[:], in_=c_biases[:]), cl, 16)
        p.wait("sync", cl)
        for i, v in enumerate(cvals):
            nc.const_aps.register(F32, v, bias_sb[:, i:i + 1])

        bar = r.sem("bar")

        with contextlib.ExitStack() as sc:
            pairs = [
                (r.d_wqT, r.d_wqT_bf, D, D), (r.d_wkT, r.d_wkT_bf, D, D),
                (r.d_wvT, r.d_wvT_bf, D, D), (r.d_woT, r.d_woT_bf, D, D),
            ]
            if stop_after in ("attn_b", "all"):
                pairs += [(r.d_winT, r.d_winT_bf, D, D), (r.d_cvT, r.d_cvT_bf, 3 * D, D),
                          (r.d_wbcdT, r.d_wbcdT_bf, D, 96), (r.d_wd2T, r.d_wd2T_bf, 64, D),
                          (r.d_woutT, r.d_woutT_bf, D, D)]

            emit_cast(r, sc, pairs)
            if stop_after in ("ffn1", "ffn1_dbg", "attn_b", "all"):
                zi = r.sem("zinit")
                zt = r.sb(sc, [128, D], BF16, base="zinit")
                ztf = r.sb(sc, [128, NE * SLOT // 128], F32, base="zinitf")
                p.do("vector", lambda eng: eng.memset(zt[:], 0.0), zi, 1)
                p.do("vector", lambda eng: eng.memset(ztf[:], 0.0), zi, 1)
                p.wait("gpsimd", zi)
                for i_ in range(NE * SLOT // 128):
                    p.do("gpsimd", lambda eng, i_=i_: eng.dma_start(
                        out=r.d_bufx[128 * i_:128 * (i_ + 1), :], in_=zt[:]), zi, 16)
                p.do("gpsimd", lambda eng: eng.dma_start(
                    out=r.d_sslot.rearrange("(o s) x -> o (s x)", o=128), in_=ztf[:]), zi, 16)
                p.wait("gpsimd", zi)
            p.barrier(bar)
        if stop_after != "cast":
            with contextlib.ExitStack() as sc:
                emit_ln(r, sc, r.d_xT, r.d_xnbf)
                p.barrier(bar)
        if stop_after not in ("cast", "ln"):
            dst_a = r.d_out if stop_after in ("attn_a", "attn_a_s2", "attn_a_s1") else r.d_x1
            with contextlib.ExitStack() as sc:
                emit_attn_a(r, sc, r.d_xT, r.d_xnbf, dst_a,
                            stop_stage={"attn_a_s2": 2, "attn_a_s1": 1}.get(stop_after))
                p.barrier(bar)
        if stop_after not in ("cast", "ln", "attn_a", "attn_a_s1", "attn_a_s2"):
            dst_f = r.d_out if stop_after in ("ffn1", "ffn1_dbg") else r.d_x2
            with contextlib.ExitStack() as sc:
                emit_ffn_gating(r, sc, r.d_x1, 1)
            with contextlib.ExitStack() as sc:
                emit_ffn_experts_compact(r, sc, r.d_x1, dst_f)
                p.barrier(bar)
        if stop_after in ("attn_b", "all"):
            dst_b = r.d_out if stop_after == "attn_b" else r.d_x3
            with contextlib.ExitStack() as sc:
                emit_attn_b(r, sc, r.d_x2, dst_b)
        if stop_after == "all":
            with contextlib.ExitStack() as sc:
                emit_ffn_gating(r, sc, r.d_x3, 2)
            with contextlib.ExitStack() as sc:
                emit_ffn_experts_compact(r, sc, r.d_x3, r.d_out)
                p.barrier(bar)

        if taps:
            tsem = r.sem("tap")
            for i, name in enumerate(taps):
                srcd = getattr(r, "d_" + name)
                shp = [srcd.shape[0], srcd.shape[1]]
                dt = srcd.dtype
                dout = nc.dram_tensor(f"tap_{name}", shp, dt, kind="ExternalOutput").ap()
                p.do("sync", lambda eng, dout=dout, srcd=srcd: eng.dma_start(
                    out=dout[:], in_=srcd[:]), tsem, 16)
            p.wait("sync", tsem)

        with nc.Block() as block:
            prog.replay(block)
    return nc



LGT = 1024


def emit_ffn_gating(r, scope, x_src, call_idx):
    p = r.p
    nc = r.nc
    mm = r.sem("ff_mm")
    ev = r.sem("ff_ev")
    sq = r.sem("ff_sq")
    cc = r.sem("ff_cc")
    st = r.sem("ff_st")
    gld = r.sem("ff_gld")

    cin = r.d_cin1
    cout = r.d_cout1
    d_sall = r.d_sall

    with contextlib.ExitStack() as sln:
        emit_ln(r, sln, x_src, r.d_xnbf, r.d_xn32)
    p.barrier(r.sems["bar"])

    sg = scope.enter_context(contextlib.ExitStack())
    wg_sb = r.sb(sg, [128, NFC * NE], F32, base="fgwg")
    xnpool = Pool(r, "p0", r.sb(sg, [128, D], F32, 2, "fgxn"))
    lg_sb = r.sb(sg, [8, 128], F32, base="fglg")
    lgtok = r.sb(sg, [128, 8], F32, base="fglt")
    negm = r.sb(sg, [128, 1], F32, base="fgnm")
    ex = r.sb(sg, [128, 8], F32, base="fgex")
    sumex = r.sb(sg, [128, 1], F32, base="fgse")
    rsum = r.sb(sg, [128, 1], F32, base="fgrs")
    ismax = r.sb(sg, [128, 8], F32, base="fgim")
    c1 = r.sb(sg, [128, 8], F32, base="fgc1")
    c2 = r.sb(sg, [128, 8], F32, base="fgc2")
    onehot = r.sb(sg, [128, 8], F32, base="fgoh")
    oh_bf = r.sb(sg, [128, 8], BF16, base="fgohb")
    exo = r.sb(sg, [128, 1], F32, base="fgexo")
    gown = r.sb(sg, [128, 1], F32, base="fggo")
    scr8 = r.sb(sg, [128, 8], F32, base="fgscr")
    rank_own = r.sb(sg, [128, 1], F32, base="fgro")
    kept = r.sb(sg, [128, 1], F32, base="fgkp")
    scolp = Pool(r, "p8", r.sb(sg, [128, 1], F32, 3, "fgsc"))
    smat = r.sb(sg, [128, 8], F32, base="fgsm")
    oh_all = r.sb(sg, [128, 32 * 8], BF16, base="fgoha")
    ohf_all = r.sb(sg, [128, 32 * 8], F32, base="fgohf")
    gown_all = r.sb(sg, [128, 32], F32, base="fggoa")
    s_all_sb = r.sb(sg, [8, L], F32, base="fgsa")
    cnt_sb = r.sb(sg, [1, 256], F32, base="fgcnt")
    cnt_mat = r.sb(sg, [32, 8], F32, base="fgcm")
    tot_sb = r.sb(sg, [4, 8], F32, base="fgtot")
    cnt_all = r.sb(sg, [32, 8], F32, base="fgca")
    mask_me = r.sb(sg, [32, 1], F32, base="fgmm")
    masked = r.sb(sg, [32, 8], F32, base="fgmk")
    goff_sb = r.sb(sg, [1, 8], F32, base="fggf")
    run_sb = r.sb(sg, [32, 8], F32, base="fgrn")
    runrow = r.sb(sg, [1, 256], F32, base="fgrr")
    ps_lg = r.ps(sg, [8, 128], "fgplg")
    ps_tr = r.ps(sg, [128, 8], "fgptr")
    ps_cnt = r.ps(sg, [1, 256], "fgpc")
    ps_sm8 = r.ps(sg, [8, 128], "fgpsm")
    ps_tot = r.ps(sg, [1, 8], "fgpt")
    ps_run = r.ps(sg, [32, 8], "fgprn")
    ps_r = [r.ps(sg, [128, 8], "fgpr") for _ in range(2)]
    goff_bc = r.sb(sg, [128, 8], F32, base="fggb")
    own_goff = r.sb(sg, [128, 1], F32, base="fgog")
    lrank = r.sb(sg, [128, 1], F32, base="fglr")
    eidc = r.sb(sg, [128, 1], F32, base="fgei")
    slotf = r.sb(sg, [128, 1], F32, base="fgsf")
    slots_sb = r.sb(sg, [128, 32], mybir.dt.int32, base="fgsl")

    p.do("sync", lambda eng: eng.dma_start(
        out=wg_sb[:].rearrange("p (c e) -> p c e", c=NFC),
        in_=r.d_wg.rearrange("(c p) e -> p c e", p=128)), gld, 16)
    p.do("sync", lambda eng: eng.dma_start(out=mask_me[:], in_=r.d_memask[:]), gld, 16)
    wgv = p.wait("tensor", gld)

    for lt in range(32):
        xb, xs = xnpool.acquire(p, "sync")
        xsem, xv = xnpool.dma(p, "sync", xs, lambda eng, xb=xb, lt=lt: eng.dma_start(
            out=xb[:].rearrange("p (c l) -> p c l", c=NFC),
            in_=r.d_xn32[:, 128 * lt:128 * (lt + 1)].rearrange("(c p) l -> p c l", p=128)))
        p.wait("tensor", xsem, xv)
        if lt > 0:
            p.wait("tensor", ev)
        for fc in range(NFC):
            p.do("tensor", lambda eng, xb=xb, fc=fc, f0=(fc == 0), fl=(fc == NFC - 1): eng.matmul(
                ps_lg[:], wg_sb[:, 8 * fc:8 * (fc + 1)], xb[:, 128 * fc:128 * (fc + 1)],
                start=f0, stop=fl), mm, 1)
        p.wait("vector", mm)
        vlg = p.do("vector", lambda eng: eng.tensor_copy(lg_sb[:], ps_lg[:]), ev, 1)
        xnpool.release(xs, ev, vlg)
        p.wait("tensor", ev, vlg)
        vtr = p.do("tensor", lambda eng: eng.transpose(
            ps_tr[:], lg_sb[:], r.ident_f32[0:8, 0:8]), mm, 1)
        p.wait("vector", mm, vtr)
        p.do("vector", lambda eng: eng.tensor_copy(lgtok[:], ps_tr[:]), ev, 1)
        p.wait("vector", ev)
        p.do("vector", lambda eng: eng.reduce_max(
            negm[:], lgtok[:], AX.X, negate=True), ev, 1)
        p.wait("scalar", ev)
        p.do("scalar", lambda eng: eng.activation(
            ex[:], lgtok[:], AF.Exp, bias=negm[:], accum_out=sumex[:]), sq, 1)
        p.wait("vector", ev)
        p.do("vector", lambda eng: eng.tensor_scalar(
            ismax[:], lgtok[:], negm[:], 0.0, ALU.add, ALU.is_ge), ev, 1)
        p.wait("vector", ev)
        p.do("vector", lambda eng: eng.tensor_copy(c1[:, 0:1], ismax[:, 0:1]), ev, 1)
        p.do("vector", lambda eng: eng.tensor_add(
            c1[:, 1:8], ismax[:, 1:8], ismax[:, 0:7]), ev, 1)
        p.wait("vector", ev)
        p.do("vector", lambda eng: eng.tensor_copy(c2[:, 0:2], c1[:, 0:2]), ev, 1)
        p.do("vector", lambda eng: eng.tensor_add(c2[:, 2:8], c1[:, 2:8], c1[:, 0:6]), ev, 1)
        p.wait("vector", ev)
        p.do("vector", lambda eng: eng.tensor_copy(c1[:, 0:4], c2[:, 0:4]), ev, 1)
        p.do("vector", lambda eng: eng.tensor_add(c1[:, 4:8], c2[:, 4:8], c2[:, 0:4]), ev, 1)
        p.wait("vector", ev)
        p.do("vector", lambda eng: eng.tensor_single_scalar(
            c2[:], c1[:], 1.0, ALU.is_equal), ev, 1)
        p.wait("vector", ev)
        p.do("vector", lambda eng, lt=lt: eng.tensor_mul(onehot[:], ismax[:], c2[:]), ev, 1)
        p.wait("vector", ev)
        p.do("vector", lambda eng, lt=lt: eng.tensor_copy(
            oh_all[:, 8 * lt:8 * (lt + 1)], onehot[:]), ev, 1)
        p.do("vector", lambda eng, lt=lt: eng.tensor_copy(
            ohf_all[:, 8 * lt:8 * (lt + 1)], onehot[:]), ev, 1)
        p.do("vector", lambda eng: eng.tensor_copy(oh_bf[:], onehot[:]), ev, 1)
        p.wait("vector", sq)
        p.do("vector", lambda eng: eng.scalar_tensor_tensor(
            scr8[:], ex[:], 1.0, onehot[:], ALU.mult, ALU.mult, accum_out=exo[:]), ev, 1)
        p.do("vector", lambda eng: eng.reciprocal(rsum[:], sumex[:]), ev, 1)
        p.wait("vector", ev)
        vgo = p.do("vector", lambda eng, lt=lt: eng.tensor_mul(
            gown_all[:, lt:lt + 1], exo[:], rsum[:]), ev, 1)
        p.wait("tensor", ev, vgo)
        p.do("tensor", lambda eng, lt=lt: eng.matmul(
            ps_cnt[:, 8 * lt:8 * (lt + 1)], r.ones_col_bf[:], oh_bf[:],
            start=True, stop=True), mm, 1)
    p.wait("vector", mm)
    vc = p.do("vector", lambda eng: eng.tensor_copy(cnt_sb[:], ps_cnt[:]), ev, 1)
    p.wait("gpsimd", ev, vc)
    p.do("gpsimd", lambda eng: eng.dma_start(out=r.d_cnt256[:], in_=cnt_sb[:]), st, 16)
    p.wait("sync", st)
    p.do("sync", lambda eng: eng.dma_start(
        out=cnt_mat[:], in_=r.d_cnt256.rearrange("o (t e) -> (o t) e", e=8)), gld, 16)
    p.wait("tensor", gld)
    p.do("tensor", lambda eng: eng.matmul(
        ps_tot[:], r.ones_col[0:32, :], cnt_mat[:], start=True, stop=True), mm, 1)
    p.wait("vector", mm)
    p.do("vector", lambda eng: eng.memset(tot_sb[:], 0.0), ev, 1)
    p.wait("vector", ev)
    vt = p.do("vector", lambda eng: eng.tensor_copy(tot_sb[0:1, :], ps_tot[:]), ev, 1)
    p.wait("gpsimd", ev)
    p.do("gpsimd", lambda eng: eng.dma_start(out=cin[:], in_=tot_sb[:]), st, 16)
    p.wait("gpsimd", st)
    if getattr(r, "skip_cc", 0) == call_idx:
        p.do("gpsimd", lambda eng: eng.dma_start(out=cout[0:4, :], in_=cin[:]), st, 16)
        p.wait("gpsimd", st)
    else:
        p.do("gpsimd", lambda eng: eng.collective_compute(
            "AllGather", ALU.bypass, replica_groups=[list(range(NCORE))],
            ins=[cin.opt()], outs=[cout.opt()]), cc, 1)
        p.wait("gpsimd", cc)
    p.do("gpsimd", lambda eng: eng.dma_start(out=cnt_all[:], in_=cout[:]), st, 16)
    p.wait("vector", st)
    p.do("vector", lambda eng: eng.tensor_scalar(
        masked[:], cnt_all[:], mask_me[:], None, ALU.mult), ev, 1)
    p.wait("tensor", ev)
    p.do("tensor", lambda eng: eng.matmul(
        ps_tot[:], r.ones_col[0:32, :], masked[:], start=True, stop=True), mm, 1)
    p.wait("vector", mm)
    p.do("vector", lambda eng: eng.tensor_copy(goff_sb[:], ps_tot[:]), ev, 1)
    p.wait("tensor", ev)
    p.do("tensor", lambda eng: eng.matmul(
        ps_run[:], r.su32_f32[:], cnt_mat[:], start=True, stop=False), mm, 1)
    p.do("tensor", lambda eng: eng.matmul(
        ps_run[:], r.ones_row_f32[:, 0:32], goff_sb[:], start=False, stop=True), mm, 1)
    p.wait("vector", mm)
    vr = p.do("vector", lambda eng: eng.tensor_copy(run_sb[:], ps_run[:]), ev, 1)
    vgf = p.do("vector", lambda eng: eng.tensor_copy(tot_sb[0:1, :], goff_sb[:]), ev, 1)
    p.wait("gpsimd", ev, vr)
    p.do("gpsimd", lambda eng: eng.dma_start(out=r.d_run256[:], in_=run_sb[:]), st, 16)
    p.do("gpsimd", lambda eng: eng.dma_start(out=r.d_goffd[:], in_=tot_sb[0:1, :]), st, 16)
    p.wait("sync", st)
    p.do("sync", lambda eng: eng.dma_start(
        out=goff_bc[:], in_=r.d_goffd.broadcast_to((128, 8))), gld, 16)
    p.do("sync", lambda eng: eng.dma_start(
        out=runrow[:], in_=r.d_run256.rearrange("(o t) e -> o (t e)", o=1)), gld, 16)
    rrv = p.wait("tensor", gld)

    for lt in range(32):
        pr = ps_r[lt % 2]
        if lt >= 2:
            p.wait("tensor", ev)
        p.do("tensor", lambda eng, pr=pr, lt=lt: eng.matmul(
            pr[:], r.su_f32[:], ohf_all[:, 8 * lt:8 * (lt + 1)], start=True, stop=False), mm, 1)
        vrk = p.do("tensor", lambda eng, pr=pr, lt=lt: eng.matmul(
            pr[:], r.ones_row_f32[:, 0:128], runrow[:, 8 * lt:8 * (lt + 1)],
            start=False, stop=True), mm, 1)
        p.wait("vector", mm, vrk)
        p.do("vector", lambda eng, pr=pr, lt=lt: eng.scalar_tensor_tensor(
            scr8[:], ohf_all[:, 8 * lt:8 * (lt + 1)], 1.0, pr[:], ALU.mult, ALU.mult,
            accum_out=rank_own[:]), ev, 1)
        p.wait("vector", ev)
        p.do("vector", lambda eng: eng.tensor_single_scalar(
            kept[:], rank_own[:], float(CAP), ALU.is_lt), ev, 1)
        p.wait("vector", ev)
        s_col, scs = scolp.acquire(p, "vector")
        p.do("vector", lambda eng, lt=lt, s_col=s_col: eng.tensor_mul(
            s_col[:], gown_all[:, lt:lt + 1], kept[:]), ev, 1)
        p.wait("vector", ev)
        vsm = p.do("vector", lambda eng, lt=lt, s_col=s_col: eng.tensor_scalar(
            smat[:], ohf_all[:, 8 * lt:8 * (lt + 1)], s_col[:], None, ALU.mult), ev, 1)
        scolp.release(scs, ev, vsm)
        p.wait("vector", gld)
        p.do("vector", lambda eng, lt=lt: eng.scalar_tensor_tensor(
            scr8[:], ohf_all[:, 8 * lt:8 * (lt + 1)], 1.0, goff_bc[:], ALU.mult,
            ALU.mult, accum_out=own_goff[:]), ev, 1)
        p.do("vector", lambda eng, lt=lt: eng.scalar_tensor_tensor(
            scr8[:], ohf_all[:, 8 * lt:8 * (lt + 1)], 1.0, r.iotaE[:], ALU.mult,
            ALU.mult, accum_out=eidc[:]), ev, 1)
        p.wait("vector", ev)
        p.do("vector", lambda eng: eng.tensor_sub(lrank[:], rank_own[:], own_goff[:]), ev, 1)
        p.wait("vector", ev)
        p.do("vector", lambda eng: eng.tensor_scalar(
            slotf[:], eidc[:], float(SLOT), None, ALU.mult), ev, 1)
        p.wait("vector", ev)
        p.do("vector", lambda eng: eng.tensor_add(slotf[:], slotf[:], lrank[:]), ev, 1)
        p.wait("vector", ev)
        vsl = p.do("vector", lambda eng, lt=lt: eng.tensor_copy(
            slots_sb[:, lt:lt + 1], slotf[:]), ev, 1)
        p.wait("gpsimd", ev, vsl)
        scsem, scv = scolp.dma(p, "gpsimd", scs, lambda eng, lt=lt, s_col=s_col: eng.indirect_dma_start(
            r.d_sslot[:], IndirectOffsetOnAxis(slots_sb[:, lt:lt + 1], 0),
            s_col[:], None, oob_is_err=False))
        scolp.release(scs, scsem, scv)
        p.wait("tensor", ev, vsm)
        vtr2 = p.do("tensor", lambda eng, lt=lt: eng.transpose(
            ps_sm8[:], smat[:], r.ident_f32[:]), mm, 1)
        p.wait("vector", mm, vtr2)
        p.do("vector", lambda eng, lt=lt: eng.tensor_copy(
            s_all_sb[:, 128 * lt:128 * (lt + 1)], ps_sm8[:]), ev, 1)
    p.wait("gpsimd", ev)
    p.do("gpsimd", lambda eng: eng.dma_start(out=d_sall[:], in_=s_all_sb[:]), st, 16)
    p.do("gpsimd", lambda eng: eng.dma_start(out=r.d_slots[:], in_=slots_sb[:]), st, 16)
    p.wait("sync", st)
    p.wait("gpsimd", st)
    sg.close()
    p.barrier(r.sems["bar"])

    ssc = scope.enter_context(contextlib.ExitStack())
    xnp2 = Pool(r, "p1", r.sb(ssc, [128, NFC * 128], BF16, 2, "fsxn"))
    tokp = Pool(r, "p2", r.sb(ssc, [128, D], BF16, 2, "fstok"))
    slots2 = r.sb(ssc, [128, 32], mybir.dt.int32, base="fssl")
    ps_tb = [ssc.enter_context(nc.psum_tensor(r.name("fsptb"), [128, 128], BF16))
             for _ in range(2)]
    p.do("sync", lambda eng: eng.dma_start(out=slots2[:], in_=r.d_slots[:]), gld, 16)
    p.wait("gpsimd", gld)
    for lt in range(32):
        xb, xs = xnp2.acquire(p, "sync")
        xsem, xv = xnp2.dma(p, "sync", xs, lambda eng, xb=xb, lt=lt: eng.dma_start(
            out=xb[:].rearrange("p (c l) -> p c l", c=NFC),
            in_=r.d_xnbf[:, 128 * lt:128 * (lt + 1)].rearrange("(c p) l -> p c l", p=128)))
        tb, ts_ = tokp.acquire(p, "tensor")
        p.wait("tensor", xsem, xv)
        lastv = None
        for fc in range(NFC):
            pt = ps_tb[fc % 2]
            if fc >= 2 or lt > 0:
                p.wait("tensor", ev)
            vt = p.do("tensor", lambda eng, xb=xb, fc=fc, pt=pt: eng.transpose(
                pt[:], xb[:, 128 * fc:128 * (fc + 1)], r.ident_bf[:]), mm, 1)
            p.wait("vector", mm, vt)
            lastv = p.do("vector", lambda eng, tb=tb, fc=fc, pt=pt: eng.tensor_copy(
                tb[:, 128 * fc:128 * (fc + 1)], pt[:]), ev, 1)
        xnp2.release(xs, mm, mm.n)
        p.wait("gpsimd", ev, lastv)
        ssem, sv = tokp.dma(p, "gpsimd", ts_, lambda eng, tb=tb, lt=lt: eng.indirect_dma_start(
            r.d_bufx[:], IndirectOffsetOnAxis(slots2[:, lt:lt + 1], 0),
            tb[:], None, oob_is_err=False))
        tokp.release(ts_, ssem, sv)
    for s in tokp.sems:
        p.wait("sync", s)
        p.wait("gpsimd", s)
    ssc.close()
    p.barrier(r.sems["bar"])

def emit_ffn_experts(r, scope, x_src, x_dst):
    p = r.p
    nc = r.nc
    mm = r.sem("fe_mm")
    ev = r.sem("fe_ev")
    st = r.sem("fe_st")
    gld = r.sem("fe_gld")

    se = scope.enter_context(contextlib.ExitStack())
    xnpool = Pool(r, "p3", r.sb(se, [128, NFC * LGT], BF16, 2, "fexn"))
    h1_big = r.sb(se, [128, 32 * LGT], BF16, base="feh1")
    w2e = r.sb(se, [128, 32 * D], BF16, base="few2")
    w1pool = Pool(r, "p0", r.sb(se, [128, NFC * 128], BF16, 3, "few1"))
    sepool = Pool(r, "p1", r.sb(se, [128, LGT], F32, 2, "fese"))
    b1_sb = r.sb(se, [128, 256], F32, base="feb1")
    relu_t = r.sb(se, [128, LGT], F32, base="fert")
    b2_sb = r.sb(se, [8, D], F32, base="feb2")
    s8 = r.sb(se, [8, 512], F32, base="fes8")
    opool = Pool(r, "p2", r.sb(se, [128, LGT], F32, 3, "feo"))
    ps_h = [r.ps(se, [128, LGT], "feph") for _ in range(2)]
    ps_o = [r.ps(se, [128, LGT], "fepo") for _ in range(2)]

    ld0 = r.sem("fe_ld0")
    p.do("sync", lambda eng: eng.dma_start(out=b1_sb[:], in_=r.d_b1[:]), ld0, 16)
    p.do("sync", lambda eng: eng.dma_start(out=b2_sb[:], in_=r.d_b2[:]), ld0, 16)
    p.wait("tensor", ld0)
    p.wait("vector", ld0)

    w2sem = r.sem("fe_w2s")
    w2free = None
    accchain = {}

    for lc in range(NLC):
        sl = slice(LC * lc, LC * (lc + 1))
        p.do("sync", lambda eng, sl=sl: eng.dma_start(
            out=s8[:], in_=r.d_sall[:, sl]), ld0, 16)
        sv8 = p.wait("tensor", ld0)
        for oc in range(NFC):
            po = ps_o[oc % 2]
            if lc > 0 or oc >= 2:
                p.wait("tensor", ev)
            vb = p.do("tensor", lambda eng, po=po, oc=oc: eng.matmul(
                po[:, 0:LC], b2_sb[:, 128 * oc:128 * (oc + 1)], s8[:],
                start=True, stop=True), mm, 1)
            ob, os_ = opool.acquire(p, "vector")
            p.wait("vector", mm, vb)
            vcp = p.do("vector", lambda eng, po=po, ob=ob: eng.tensor_copy(
                ob[:, 0:LC], po[:, 0:LC]), ev, 1)
            p.wait("gpsimd", ev, vcp)
            ssem, sv = opool.dma(p, "gpsimd", os_, lambda eng, ob=ob, oc=oc, sl=sl: eng.dma_start(
                out=r.d_facc[128 * oc:128 * (oc + 1), sl], in_=ob[:, 0:LC]))
            opool.release(os_, ssem, sv)
            accchain[(lc, oc)] = (ssem, sv)

    NLG = L // LGT
    for lg in range(NLG):
        lsl = slice(LGT * lg, LGT * (lg + 1))
        xn_lg, xns = xnpool.acquire(p, "sync")
        xnsem, xnv = xnpool.dma(p, "sync", xns, lambda eng, lsl=lsl, xn_lg=xn_lg: eng.dma_start(
            out=xn_lg[:].rearrange("p (c l) -> p c l", c=NFC),
            in_=r.d_xnbf[:, lsl].rearrange("(c p) l -> p c l", p=128)))
        p.wait("tensor", xnsem, xnv)
        last_hidden_mm = None
        for e in range(NE):
            if w2free is not None:
                p.wait("sync", w2free[0], w2free[1])
            p.do("sync", lambda eng, e=e: eng.dma_start(
                out=w2e[:].rearrange("p (c o) -> p c o", c=32),
                in_=r.d_w2bf[FF * e:FF * (e + 1), :].rearrange(
                    "(c p) o -> p c o", p=128)), w2sem, 16)
            w2v = w2sem.n
            seb, ses = sepool.acquire(p, "sync")
            sesem, sev = sepool.dma(p, "sync", ses, lambda eng, seb=seb, e=e, lsl=lsl: eng.dma_start(
                out=seb[:], in_=r.d_sall[e:e + 1, lsl].broadcast_to((128, LGT))))
            for fo in range(32):
                w1b, w1s = w1pool.acquire(p, "sync")
                w1sem, w1v = w1pool.dma(p, "sync", w1s, lambda eng, w1b=w1b, e=e, fo=fo: eng.dma_start(
                    out=w1b[:].rearrange("p (c f) -> p c f", c=NFC),
                    in_=r.d_w1bf[D * e:D * (e + 1), 128 * fo:128 * (fo + 1)].rearrange(
                        "(c p) f -> p c f", p=128)))
                ph = ps_h[fo % 2]
                p.wait("tensor", w1sem, w1v)
                if fo >= 2 or e > 0 or lg > 0:
                    p.wait("tensor", ev)
                vmm = None
                for half in range(2):
                    for dk in range(NFC):
                        vmm = p.do("tensor", lambda eng, ph=ph, w1b=w1b, dk=dk, half=half,
                                   xn_lg=xn_lg,
                                   f0=(dk == 0), fl=(dk == NFC - 1): eng.matmul(
                            ph[:, 512 * half:512 * (half + 1)],
                            w1b[:, 128 * dk:128 * (dk + 1)],
                            xn_lg[:, LGT * dk + 512 * half:LGT * dk + 512 * half + 512],
                            start=f0, stop=fl), mm, 1)
                last_hidden_mm = vmm
                w1pool.release(w1s, mm, vmm)
                p.wait("vector", mm, vmm)
                p.wait("vector", sesem, sev)
                p.do("vector", lambda eng, ph=ph, e=e, fo=fo: eng.tensor_scalar(
                    relu_t[:], ph[:], b1_sb[:, 32 * e + fo:32 * e + fo + 1], 0.0,
                    ALU.add, ALU.max), ev, 1)
                p.wait("vector", ev)
                p.do("vector", lambda eng, fo=fo, seb=seb: eng.tensor_mul(
                    h1_big[:, LGT * fo:LGT * (fo + 1)], relu_t[:], seb[:]), ev, 1)
            sepool.release(ses, ev, ev.n)
            if getattr(r, "dbg_ffn", False) and lg == 0 and e == 0:
                dsem = r.sem("fedbg")
                p.wait("gpsimd", ev)
                p.do("gpsimd", lambda eng: eng.dma_start(
                    out=r.d_h1dump[:], in_=h1_big[:]), dsem, 16)
                p.wait("gpsimd", dsem)
            p.wait("tensor", w2sem, w2v)
            p.wait("tensor", ev)
            for oc in range(NE):
                po = ps_o[oc % 2]
                if oc >= 2:
                    p.wait("tensor", ev)
                vmm = None
                for half in range(2):
                    for fk in range(32):
                        vmm = p.do("tensor", lambda eng, po=po, fk=fk, oc=oc, half=half,
                                   f0=(fk == 0), fl=(fk == 31): eng.matmul(
                            po[:, 512 * half:512 * (half + 1)],
                            w2e[:, D * fk + 128 * oc: D * fk + 128 * (oc + 1)],
                            h1_big[:, LGT * fk + 512 * half:LGT * fk + 512 * half + 512],
                            start=f0, stop=fl), mm, 1)
                if oc == NE - 1:
                    w2free = (mm, vmm)
                ob, os_ = opool.acquire(p, "vector")
                p.wait("vector", mm, vmm)
                vcp = p.do("vector", lambda eng, po=po, ob=ob: eng.tensor_copy(
                    ob[:], po[:]), ev, 1)
                p.wait("gpsimd", ev, vcp)
                for key in ((2 * lg, oc), (2 * lg + 1, oc)):
                    if key in accchain:
                        asem, av = accchain[key]
                        p.wait("gpsimd", asem, av)
                ssem, sv = opool.dma(p, "gpsimd", os_, lambda eng, ob=ob, oc=oc,
                                     lsl=lsl: eng.dma_start(
                    out=r.d_facc[128 * oc:128 * (oc + 1), lsl], in_=ob[:],
                    accum_op=ALU.add))
                opool.release(os_, ssem, sv)
                accchain[(2 * lg, oc)] = (ssem, sv)
                accchain[(2 * lg + 1, oc)] = (ssem, sv)
        xnpool.release(xns, mm, last_hidden_mm)
    for s in opool.sems:
        p.wait("gpsimd", s)
    se.close()

    sr_ = scope.enter_context(contextlib.ExitStack())
    xpool = Pool(r, "p4", r.sb(sr_, [128, LC], F32, 2, "frx"))
    apool = Pool(r, "p5", r.sb(sr_, [128, LC], F32, 2, "fra"))
    rpool = Pool(r, "p6", r.sb(sr_, [128, LC], F32, 2, "frr"))
    for fc in range(NFC):
        for lc in range(NLC):
            sl = slice(LC * lc, LC * (lc + 1))
            xb, xs = xpool.acquire(p, "sync")
            xsem, xv = xpool.dma(p, "sync", xs, lambda eng, xb=xb, fc=fc, sl=sl: eng.dma_start(
                out=xb[:], in_=x_src[128 * fc:128 * (fc + 1), sl]))
            ab, asl = apool.acquire(p, "sync")
            asem, av = apool.dma(p, "sync", asl, lambda eng, ab=ab, fc=fc, sl=sl: eng.dma_start(
                out=ab[:], in_=r.d_facc[128 * fc:128 * (fc + 1), sl]))
            rb, rs = rpool.acquire(p, "vector")
            p.wait("vector", xsem, xv)
            p.wait("vector", asem, av)
            v = p.do("vector", lambda eng, xb=xb, ab=ab, rb=rb: eng.tensor_add(
                rb[:], xb[:], ab[:]), ev, 1)
            xpool.release(xs, ev, v)
            apool.release(asl, ev, v)
            p.wait("gpsimd", ev, v)
            ssem, sv = rpool.dma(p, "gpsimd", rs, lambda eng, rb=rb, fc=fc, sl=sl: eng.dma_start(
                out=x_dst[128 * fc:128 * (fc + 1), sl], in_=rb[:]))
            rpool.release(rs, ssem, sv)
    for s in rpool.sems:
        p.wait("gpsimd", s)
    sr_.close()


def emit_attn_b(r, scope, x_src, x_dst):
    p = r.p
    nc = r.nc
    mm = r.sem("ab_mm")
    ev = r.sem("ab_ev")
    sq = r.sem("ab_sq")
    st = r.sem("ab_st")
    gld = r.sem("ab_gld")
    bar = r.sems["bar"]

    with contextlib.ExitStack() as sln:
        emit_ln(r, sln, x_src, r.d_xnbf, r.d_xn32)
    p.barrier(bar)

    s1 = scope.enter_context(contextlib.ExitStack())
    xpool = Pool(r, "p0", r.sb(s1, [128, LC], F32, 3, "abx"))
    sqpool = Pool(r, "p1", r.sb(s1, [128, LC], F32, 2, "absq"))
    cells = r.sb(s1, [1, 8], F32, base="abcell")
    rowsc = r.sb(s1, [1, LC], F32, base="abrsc")
    ps_s = r.ps(s1, [1, LC], "abps")
    ps_q = r.ps(s1, [1, LC], "abpq")
    p.do("vector", lambda eng: eng.memset(cells[:], 0.0), ev, 1)
    for lc in range(NLC):
        if lc > 0:
            p.wait("tensor", ev)
        for fc in range(NFC):
            xb, xs = xpool.acquire(p, "sync")
            lsem, lv = xpool.dma(p, "sync", xs, lambda eng, xb=xb, fc=fc, lc=lc: eng.dma_start(
                out=xb[:], in_=r.d_xn32[128 * fc:128 * (fc + 1), LC * lc:LC * (lc + 1)]))
            sb_, ss_ = sqpool.acquire(p, "scalar")
            p.wait("scalar", lsem, lv)
            vq = p.do("scalar", lambda eng, xb=xb, sb_=sb_: eng.activation(
                sb_[:], xb[:], AF.Square), sq, 1)
            p.wait("tensor", lsem, lv)
            v1 = p.do("tensor", lambda eng, xb=xb, f0=(fc == 0), fl=(fc == NFC - 1): eng.matmul(
                ps_s[:], r.ones_col[:], xb[:], start=f0, stop=fl), mm, 1)
            xpool.release(xs, mm, v1)
            xpool.release(xs, sq, vq)
            p.wait("tensor", sq, vq)
            v2 = p.do("tensor", lambda eng, sb_=sb_, f0=(fc == 0), fl=(fc == NFC - 1): eng.matmul(
                ps_q[:], r.ones_col[:], sb_[:], start=f0, stop=fl), mm, 1)
            sqpool.release(ss_, mm, v2)
        p.wait("vector", mm)
        p.do("vector", lambda eng: eng.tensor_reduce(
            rowsc[:, 0:1], ps_s[:], AX.X, ALU.add), ev, 1)
        p.do("vector", lambda eng: eng.tensor_reduce(
            rowsc[:, 1:2], ps_q[:], AX.X, ALU.add), ev, 1)
        p.wait("vector", ev)
        p.do("vector", lambda eng: eng.tensor_add(
            cells[:, 0:1], cells[:, 0:1], rowsc[:, 0:1]), ev, 1)
        p.wait("vector", ev)
        vS = p.do("vector", lambda eng: eng.tensor_add(
            cells[:, 1:2], cells[:, 1:2], rowsc[:, 1:2]), ev, 1)
    NT = float(L * D)
    p.wait("vector", ev)
    p.do("vector", lambda eng: eng.tensor_single_scalar(
        cells[:, 2:3], cells[:, 0:1], 1.0 / NT, ALU.mult), ev, 1)
    p.wait("vector", ev)
    p.do("vector", lambda eng: eng.tensor_single_scalar(
        cells[:, 6:7], cells[:, 1:2], 1.0 / NT, ALU.mult), ev, 1)
    p.wait("vector", ev)
    p.do("vector", lambda eng: eng.tensor_mul(cells[:, 7:8], cells[:, 2:3], cells[:, 2:3]), ev, 1)
    p.wait("vector", ev)
    p.do("vector", lambda eng: eng.tensor_sub(cells[:, 3:4], cells[:, 6:7], cells[:, 7:8]), ev, 1)
    p.wait("scalar", ev)
    p.do("scalar", lambda eng: eng.activation(cells[:, 6:7], cells[:, 3:4], AF.Ln, bias=EPS_B), ev, 1)
    p.wait("scalar", ev)
    p.do("scalar", lambda eng: eng.activation(cells[:, 4:5], cells[:, 6:7], AF.Exp, scale=-0.5), ev, 1)
    p.wait("vector", ev)
    p.do("vector", lambda eng: eng.tensor_mul(cells[:, 5:6], cells[:, 4:5], cells[:, 2:3]), ev, 1)
    p.wait("vector", ev)
    p.do("vector", lambda eng: eng.tensor_single_scalar(
        cells[:, 5:6], cells[:, 5:6], -1.0, ALU.mult), ev, 1)
    p.wait("gpsimd", ev)
    p.do("gpsimd", lambda eng: eng.dma_start(out=r.d_gcells[:], in_=cells[:]), st, 16)
    p.wait("sync", st)
    rg_col = r.sb(s1, [128, 1], F32, base="abrg")
    nmr_col = r.sb(s1, [128, 1], F32, base="abnmr")
    p.do("sync", lambda eng: eng.dma_start(
        out=rg_col[:], in_=r.d_gcells[:, 4:5].broadcast_to((128, 1))), gld, 16)
    p.do("sync", lambda eng: eng.dma_start(
        out=nmr_col[:], in_=r.d_gcells[:, 5:6].broadcast_to((128, 1))), gld, 16)
    p.wait("vector", gld)
    p.wait("tensor", gld)

    winT = r.sb(s1, [128, NFC * D], BF16, base="abwin")
    bin_sb = r.sb(s1, [128, 8], F32, base="abbin")
    colsum = r.sb(s1, [128, 8], F32, base="abcs")
    bias2 = r.sb(s1, [128, 8], F32, base="abb2")
    zrow2 = r.sb(s1, [128, 2], BF16, base="abz2")
    ldw_q = {q: r.sem("ab_ldw_" + q) for q in ("scalar", "gpsimd")}
    ldw_q["sync"] = gld
    for dc in range(NFC):
        q = ("sync", "scalar", "gpsimd")[dc % 3]
        p.do(q, lambda eng, dc=dc: eng.dma_start(
            out=winT[:, D * dc:D * (dc + 1)],
            in_=r.d_winT_bf[128 * dc:128 * (dc + 1), :]), ldw_q[q], 16)
    p.do("sync", lambda eng: eng.dma_start(out=bin_sb[:], in_=r.d_binp[:]), gld, 16)
    p.wait("tensor", gld)
    for q in ("scalar", "gpsimd"):
        p.wait("tensor", ldw_q[q])
    ps_c = r.ps(s1, [128, 8], "abpc")
    for oc in range(NFC):
        for ic in range(NFC):
            p.do("tensor", lambda eng, oc=oc, ic=ic, f0=(ic == 0), fl=(ic == NFC - 1): eng.matmul(
                ps_c[:, oc:oc + 1], winT[:, D * ic + 128 * oc:D * ic + 128 * (oc + 1)],
                r.ones_col_bf[:], start=f0, stop=fl), mm, 1)
    p.wait("vector", mm)
    p.do("vector", lambda eng: eng.tensor_copy(colsum[:], ps_c[:]), ev, 1)
    p.wait("vector", ev)
    p.do("vector", lambda eng: eng.tensor_scalar(
        bias2[:], colsum[:], nmr_col[:], None, ALU.mult), ev, 1)
    p.wait("vector", ev)
    p.do("vector", lambda eng: eng.tensor_add(bias2[:], bias2[:], bin_sb[:]), ev, 1)
    p.wait("vector", ev)
    p.do("vector", lambda eng: eng.memset(zrow2[:], 0.0), ev, 1)
    zg = ev.n
    for fc in range(NFC):
        p.wait("gpsimd", ev, zg)
        p.do("gpsimd", lambda eng, fc=fc: eng.dma_start(
            out=r.d_h0pb[128 * fc:128 * (fc + 1), 0:1], in_=zrow2[:, 0:1],
            allow_slow_non_contiguous=True), st, 16)
        p.do("gpsimd", lambda eng, fc=fc: eng.dma_start(
            out=r.d_h0pb[128 * fc:128 * (fc + 1), L + 1:L + 2], in_=zrow2[:, 1:2],
            allow_slow_non_contiguous=True), st, 16)
    xnpool = Pool(r, "p2", r.sb(s1, [128, NFC * LC], BF16, 2, "abxn"))
    fpool = Pool(r, "p3", r.sb(s1, [128, LC], F32, 2, "abf"))
    bpool = Pool(r, "p4", r.sb(s1, [128, LC], BF16, 2, "abb"))
    ps_w = [r.ps(s1, [128, LC], "abpw") for _ in range(2)]
    v_wev = [None, None]
    k = 0
    for lc in range(NLC):
        sl = slice(LC * lc, LC * (lc + 1))
        xb, xs = xnpool.acquire(p, "sync")
        xsem, xv = xnpool.dma(p, "sync", xs, lambda eng, xb=xb, sl=sl: eng.dma_start(
            out=xb[:].rearrange("p (c l) -> p c l", c=NFC),
            in_=r.d_xnbf[:, sl].rearrange("(c p) l -> p c l", p=128)))
        p.wait("tensor", xsem, xv)
        vwl = None
        for oc in range(NFC):
            pw = ps_w[k % 2]
            if v_wev[k % 2] is not None:
                p.wait("tensor", ev, v_wev[k % 2])
            vw = None
            for ic in range(NFC):
                vw = p.do("tensor", lambda eng, pw=pw, ic=ic, oc=oc, xb=xb,
                          f0=(ic == 0), fl=(ic == NFC - 1): eng.matmul(
                    pw[:], winT[:, D * ic + 128 * oc:D * ic + 128 * (oc + 1)],
                    xb[:, LC * ic:LC * ic + LC], start=f0, stop=fl), mm, 1)
            vwl = vw
            fb, fs = fpool.acquire(p, "vector")
            p.wait("vector", mm, vw)
            v1 = p.do("vector", lambda eng, pw=pw, fb=fb, oc=oc: eng.tensor_scalar(
                fb[:], pw[:], rg_col[:], bias2[:, oc:oc + 1], ALU.mult, ALU.add), ev, 1)
            v_wev[k % 2] = v1
            bb, bs = bpool.acquire(p, "vector")
            p.wait("vector", ev, v1)
            v2 = p.do("vector", lambda eng, fb=fb, bb=bb: eng.tensor_copy(bb[:], fb[:]), ev, 1)
            p.wait("gpsimd", ev, v2)
            s1m, w1_ = fpool.dma(p, "gpsimd", fs, lambda eng, fb=fb, oc=oc, sl=sl: eng.dma_start(
                out=r.d_h0f[128 * oc:128 * (oc + 1), sl], in_=fb[:]))
            fpool.release(fs, s1m, w1_)
            s2m, w2_ = bpool.dma(p, "gpsimd", bs, lambda eng, bb=bb, oc=oc, lc=lc: eng.dma_start(
                out=r.d_h0pb[128 * oc:128 * (oc + 1), 1 + LC * lc:1 + LC * (lc + 1)], in_=bb[:]))
            bpool.release(bs, s2m, w2_)
            k += 1
        xnpool.release(xs, mm, vwl)
    for s in bpool.sems + fpool.sems:
        p.wait("sync", s)
        p.wait("gpsimd", s)
    s1.close()
    p.barrier(bar)

    s3 = scope.enter_context(contextlib.ExitStack())
    cvT = r.sb(s3, [128, 3 * NFC * D], BF16, base="abcv")
    cb_sb = r.sb(s3, [128, 8], F32, base="abcb")
    hpool = Pool(r, "p0", r.sb(s3, [128, NFC * (LC + 2)], BF16, 2, "abhp"))
    fpool3 = Pool(r, "p1", r.sb(s3, [128, LC], F32, 2, "abf3"))
    sum_c = r.sb(s3, [128, 8], F32, base="absc")
    sq_c = r.sb(s3, [128, 8], F32, base="absq2")
    col1 = r.sb(s3, [128, 1], F32, base="abc1")
    col2 = r.sb(s3, [128, 1], F32, base="abc2")
    sqt = r.sb(s3, [128, LC], F32, base="absqt")
    ps_w3 = [r.ps(s3, [128, LC], "abpw3") for _ in range(2)]
    v_w3ev = [None, None]
    ldc_q = {q: r.sem("ab_ldc_" + q) for q in ("scalar", "gpsimd")}
    ldc_q["sync"] = gld
    for kk in range(3):
        for ic in range(NFC):
            q = ("sync", "scalar", "gpsimd")[(kk * NFC + ic) % 3]
            p.do(q, lambda eng, kk=kk, ic=ic: eng.dma_start(
                out=cvT[:, D * (NFC * kk + ic):D * (NFC * kk + ic + 1)],
                in_=r.d_cvT_bf[D * kk + 128 * ic:D * kk + 128 * (ic + 1), :]), ldc_q[q], 16)
    p.do("sync", lambda eng: eng.dma_start(out=cb_sb[:], in_=r.d_cbp[:]), gld, 16)
    p.wait("tensor", gld)
    for q in ("scalar", "gpsimd"):
        p.wait("tensor", ldc_q[q])
    p.do("vector", lambda eng: eng.memset(sum_c[:], 0.0), ev, 1)
    p.do("vector", lambda eng: eng.memset(sq_c[:], 0.0), ev, 1)
    k = 0
    for lc in range(NLC):
        hb, hs_ = hpool.acquire(p, "sync")
        hsem, hv = hpool.dma(p, "sync", hs_, lambda eng, hb=hb, lc=lc: eng.dma_start(
            out=hb[:].rearrange("p (c l) -> p c l", c=NFC),
            in_=r.d_h0pb[:, LC * lc:LC * lc + LC + 2].rearrange(
                "(c p) l -> p c l", p=128)))
        p.wait("tensor", hsem, hv)
        vwl = None
        for oc in range(NFC):
            pw = ps_w3[k % 2]
            if v_w3ev[k % 2] is not None:
                p.wait("tensor", ev, v_w3ev[k % 2])
            vw = None
            for kk in range(3):
                for ic in range(NFC):
                    vw = p.do("tensor", lambda eng, pw=pw, kk=kk, ic=ic, oc=oc, hb=hb,
                              f0=(kk == 0 and ic == 0),
                              fl=(kk == 2 and ic == NFC - 1): eng.matmul(
                        pw[:], cvT[:, D * (NFC * kk + ic) + 128 * oc:
                                   D * (NFC * kk + ic) + 128 * (oc + 1)],
                        hb[:, (LC + 2) * ic + kk:(LC + 2) * ic + kk + LC],
                        start=f0, stop=fl), mm, 1)
            vwl = vw
            fb, fs = fpool3.acquire(p, "vector")
            p.wait("vector", mm, vw)
            v1 = p.do("vector", lambda eng, pw=pw, fb=fb, oc=oc: eng.tensor_scalar(
                fb[:], pw[:], cb_sb[:, oc:oc + 1], None, ALU.add), ev, 1)
            v_w3ev[k % 2] = v1
            p.wait("vector", ev, v1)
            p.do("vector", lambda eng, fb=fb: eng.tensor_reduce(
                col1[:], fb[:], AX.X, ALU.add), ev, 1)
            p.wait("scalar", ev)
            vsq = p.do("scalar", lambda eng, fb=fb: eng.activation(
                sqt[:], fb[:], AF.Square, accum_out=col2[:]), sq, 1)
            p.wait("vector", ev)
            p.do("vector", lambda eng, oc=oc: eng.tensor_add(
                sum_c[:, oc:oc + 1], sum_c[:, oc:oc + 1], col1[:]), ev, 1)
            p.wait("vector", sq, vsq)
            p.wait("vector", ev)
            v3 = p.do("vector", lambda eng, oc=oc: eng.tensor_add(
                sq_c[:, oc:oc + 1], sq_c[:, oc:oc + 1], col2[:]), ev, 1)
            p.wait("gpsimd", ev, v3)
            sm, wv_ = fpool3.dma(p, "gpsimd", fs, lambda eng, fb=fb, oc=oc, lc=lc: eng.dma_start(
                out=r.d_hcf[128 * oc:128 * (oc + 1), LC * lc:LC * (lc + 1)], in_=fb[:]))
            fpool3.release(fs, sm, wv_)
            k += 1
        hpool.release(hs_, mm, vwl)
    for s in fpool3.sems:
        p.wait("sync", s)
    mu_c = r.sb(s3, [128, 8], F32, base="abmc")
    rs_c = r.sb(s3, [128, 8], F32, base="abrc")
    t_c = r.sb(s3, [128, 8], F32, base="abtc")
    p.wait("vector", ev)
    p.do("vector", lambda eng: eng.tensor_single_scalar(
        mu_c[:], sum_c[:], 1.0 / L, ALU.mult), ev, 1)
    p.wait("vector", ev)
    p.do("vector", lambda eng: eng.tensor_single_scalar(
        t_c[:], sq_c[:], 1.0 / L, ALU.mult), ev, 1)
    p.wait("vector", ev)
    p.do("vector", lambda eng: eng.tensor_mul(sq_c[:], mu_c[:], mu_c[:]), ev, 1)
    p.wait("vector", ev)
    p.do("vector", lambda eng: eng.tensor_sub(t_c[:], t_c[:], sq_c[:]), ev, 1)
    p.wait("scalar", ev)
    p.do("scalar", lambda eng: eng.activation(t_c[:], t_c[:], AF.Ln, bias=EPS_B), ev, 1)
    p.wait("scalar", ev)
    p.do("scalar", lambda eng: eng.activation(rs_c[:], t_c[:], AF.Exp, scale=-0.5), ev, 1)
    bpool3 = Pool(r, "p2", r.sb(s3, [128, LC], BF16, 2, "abb3"))
    spool3 = Pool(r, "p3", r.sb(s3, [128, LC], F32, 2, "abs3"))
    zpool3 = Pool(r, "p4", r.sb(s3, [128, LC], F32, 2, "abz3"))
    p.wait("vector", ev)
    for oc in range(NFC):
        for lc in range(NLC):
            fb, fs = fpool3.acquire(p, "sync")
            fsem, fv = fpool3.dma(p, "sync", fs, lambda eng, fb=fb, oc=oc, lc=lc: eng.dma_start(
                out=fb[:], in_=r.d_hcf[128 * oc:128 * (oc + 1), LC * lc:LC * (lc + 1)]))
            sb3, ss3 = spool3.acquire(p, "vector")
            p.wait("vector", fsem, fv)
            v1 = p.do("vector", lambda eng, fb=fb, sb3=sb3, oc=oc: eng.tensor_scalar(
                sb3[:], fb[:], mu_c[:, oc:oc + 1], rs_c[:, oc:oc + 1],
                ALU.subtract, ALU.mult), ev, 1)
            fpool3.release(fs, ev, v1)
            zb, zs = zpool3.acquire(p, "scalar")
            p.wait("scalar", ev, v1)
            v2 = p.do("scalar", lambda eng, sb3=sb3, zb=zb: eng.activation(
                zb[:], sb3[:], AF.Silu), sq, 1)
            spool3.release(ss3, sq, v2)
            bb, bs = bpool3.acquire(p, "vector")
            p.wait("vector", sq, v2)
            v3 = p.do("vector", lambda eng, zb=zb, bb=bb: eng.tensor_copy(bb[:], zb[:]), ev, 1)
            p.wait("gpsimd", sq, v2)
            p.wait("gpsimd", ev, v3)
            sm1, wv1 = zpool3.dma(p, "gpsimd", zs, lambda eng, zb=zb, oc=oc, lc=lc: eng.dma_start(
                out=r.d_h2f[128 * oc:128 * (oc + 1), LC * lc:LC * (lc + 1)], in_=zb[:]))
            zpool3.release(zs, sm1, wv1)
            sm2, wv2 = bpool3.dma(p, "gpsimd", bs, lambda eng, bb=bb, oc=oc, lc=lc: eng.dma_start(
                out=r.d_h2b[128 * oc:128 * (oc + 1), LC * lc:LC * (lc + 1)], in_=bb[:]))
            bpool3.release(bs, sm2, wv2)
    for s in bpool3.sems + zpool3.sems:
        p.wait("sync", s)
        p.wait("gpsimd", s)
    s3.close()
    p.barrier(bar)

    s5 = scope.enter_context(contextlib.ExitStack())
    d1h_bf = r.sb(s5, [64, L], BF16, base="abd1")
    gs_row = r.sb(s5, [1, L], F32, base="abgs")
    gq_row = r.sb(s5, [1, L], F32, base="abgq")
    s5b = contextlib.ExitStack()
    wbcd = r.sb(s5b, [128, NFC * 96], BF16, base="abwb")
    bbcd = r.sb(s5b, [96, 1], F32, base="abbb")
    BCd = r.sb(s5b, [96, L], F32, base="abBC")
    h2pool = Pool(r, "p0", r.sb(s5b, [128, LC], BF16, 3, "abh2"))
    ps_bc = [r.ps(s5b, [96, LC], "abpbc") for _ in range(2)]
    for ic in range(NFC):
        p.do("sync", lambda eng, ic=ic: eng.dma_start(
            out=wbcd[:, 96 * ic:96 * (ic + 1)],
            in_=r.d_wbcdT_bf[128 * ic:128 * (ic + 1), :]), gld, 16)
    p.do("sync", lambda eng: eng.dma_start(out=bbcd[:], in_=r.d_bbcd[:]), gld, 16)
    p.wait("tensor", gld)
    v_bcev = [None, None]
    for lc in range(NLC):
        pb = ps_bc[lc % 2]
        if v_bcev[lc % 2] is not None:
            p.wait("tensor", ev, v_bcev[lc % 2])
        vw = None
        for ic in range(NFC):
            hb, hs_ = h2pool.acquire(p, "sync")
            hsem, hv = h2pool.dma(p, "sync", hs_, lambda eng, hb=hb, ic=ic, lc=lc: eng.dma_start(
                out=hb[:], in_=r.d_h2b[128 * ic:128 * (ic + 1), LC * lc:LC * (lc + 1)]))
            p.wait("tensor", hsem, hv)
            vw = p.do("tensor", lambda eng, pb=pb, ic=ic, hb=hb,
                      f0=(ic == 0), fl=(ic == NFC - 1): eng.matmul(
                pb[:], wbcd[:, 96 * ic:96 * (ic + 1)], hb[:], start=f0, stop=fl), mm, 1)
            h2pool.release(hs_, mm, vw)
        p.wait("vector", mm, vw)
        v1 = p.do("vector", lambda eng, pb=pb, lc=lc: eng.tensor_scalar(
            BCd[:, LC * lc:LC * (lc + 1)], pb[:], bbcd[:], None, ALU.add), ev, 1)
        v_bcev[lc % 2] = v1
        p.wait("vector", ev, v1)
        p.do("vector", lambda eng, lc=lc: eng.tensor_copy(
            d1h_bf[:, LC * lc:LC * (lc + 1)], BCd[0:64, LC * lc:LC * (lc + 1)]), ev, 1)
    p.wait("gpsimd", ev)
    p.do("gpsimd", lambda eng: eng.dma_start(out=r.d_bcrows[:], in_=BCd[64:96, :]), st, 16)
    p.wait("sync", st)
    p.wait("gpsimd", st)
    p.wait("tensor", ev)
    s5b.close()

    s6 = contextlib.ExitStack()
    wd2 = r.sb(s6, [64, D], BF16, base="abwd2")
    bd2_sb = r.sb(s6, [128, 8], F32, base="abbd2")
    D_sb = r.sb(s6, [128, 8], F32, base="abD")
    alog = r.sb(s6, [128, 16], F32, base="abal")
    negA = r.sb(s6, [128, 16], F32, base="abnA")
    delta = r.sb(s6, [128, L], F32, base="abdel")
    dh = r.sb(s6, [128, L], F32, base="abdh")
    abar = r.sb(s6, [128, L], F32, base="abab")
    bx = r.sb(s6, [128, L], F32, base="abbx")
    S_sb = r.sb(s6, [128, L], F32, base="abS")
    pr = r.sb(s6, [128, L], F32, base="abpr")
    bbc = [r.sb(s6, [128, L], F32, base=f"abbbc{i}") for i in range(2)]
    cshared = r.sb(s6, [128, L], F32, base="abcsh")
    cded = r.sb(s6, [128, L], F32, base="abcde")
    mcol = bx[:, 0:LC]
    mcol2 = bx[:, LC:2 * LC]
    ps_y = r.ps(s6, [128, L], "abpy")
    al_s = r.sem("ab_al")
    h0_s = r.sem("ab_h0s")
    bbs = [r.sem("ab_bb0"), r.sem("ab_bb1")]
    cs_s = r.sem("ab_cs")
    cd_s = r.sem("ab_cd")
    cs2_s = r.sem("ab_cs2")
    h02_s = r.sem("ab_h0s2")
    gst_s = r.sem("ab_gst")
    p.do("sync", lambda eng: eng.dma_start(out=wd2[:], in_=r.d_wd2T_bf[:]), gld, 16)
    p.do("sync", lambda eng: eng.dma_start(out=bd2_sb[:], in_=r.d_bd2p[:]), gld, 16)
    p.do("sync", lambda eng: eng.dma_start(out=D_sb[:], in_=r.d_Dp[:]), gld, 16)
    p.wait("tensor", gld)
    p.do("vector", lambda eng: eng.memset(gs_row[:], 0.0), ev, 1)
    p.do("vector", lambda eng: eng.memset(gq_row[:], 0.0), ev, 1)
    QRR = ("sync", "scalar", "gpsimd")
    qk = 0
    vA = vA2 = vg = None
    v_stt_last = None
    v_prod15_prev = None
    v_bx = [None, None]
    for fg in range(NFC):
        if vA is not None:
            p.wait("sync", sq, vA)
        p.do("sync", lambda eng, fg=fg: eng.dma_start(
            out=alog[:], in_=r.d_alog[128 * fg:128 * (fg + 1), :]), al_s, 16)
        p.wait("scalar", al_s)
        if v_stt_last is not None:
            p.wait("sync", ev, v_stt_last)
            p.wait("gpsimd", ev, v_stt_last)
        h2a_v = p.do("sync", lambda eng, fg=fg: eng.dma_start(
            out=cshared[:, 0:HL], in_=r.d_h2f[128 * fg:128 * (fg + 1), 0:HL]), cs_s, 16)
        h2a2_v = p.do("gpsimd", lambda eng, fg=fg: eng.dma_start(
            out=cshared[:, HL:L], in_=r.d_h2f[128 * fg:128 * (fg + 1), HL:L]), cs2_s, 16)
        if fg > 0:
            p.wait("vector", gst_s, 16 * fg)
            p.wait("tensor", ev)
        for lc in range(NLC):
            vw = p.do("tensor", lambda eng, fg=fg, lc=lc: eng.matmul(
                ps_y[:, LC * lc:LC * (lc + 1)], wd2[:, 128 * fg:128 * (fg + 1)],
                d1h_bf[:, LC * lc:LC * (lc + 1)], start=True, stop=True), mm, 1)
            p.wait("scalar", mm, vw)
            p.wait("scalar", ev)
            va = p.do("scalar", lambda eng, fg=fg, lc=lc: eng.activation(
                mcol[:], ps_y[:, LC * lc:LC * (lc + 1)], AF.Abs,
                bias=bd2_sb[:, fg:fg + 1]), sq, 1)
            p.wait("scalar", sq, va)
            ve = p.do("scalar", lambda eng: eng.activation(
                mcol2[:], mcol[:], AF.Exp, scale=-1.0), sq, 1)
            p.wait("scalar", sq, ve)
            vl = p.do("scalar", lambda eng: eng.activation(
                mcol[:], mcol2[:], AF.Ln, bias=1.0), sq, 1)
            p.wait("vector", sq, va)
            vm = p.do("vector", lambda eng, fg=fg, lc=lc: eng.tensor_scalar(
                delta[:, LC * lc:LC * (lc + 1)], ps_y[:, LC * lc:LC * (lc + 1)],
                bd2_sb[:, fg:fg + 1], 0.0, ALU.add, ALU.max), ev, 1)
            p.wait("vector", sq, vl)
            p.wait("vector", ev, vm)
            p.do("vector", lambda eng, lc=lc: eng.tensor_add(
                delta[:, LC * lc:LC * (lc + 1)], delta[:, LC * lc:LC * (lc + 1)],
                mcol[:]), ev, 1)
        p.wait("scalar", sq)
        if vA2 is not None:
            p.wait("scalar", ev, vA2)
        vA = p.do("scalar", lambda eng: eng.activation(
            negA[:], alog[:], AF.Exp), sq, 1)
        p.wait("vector", sq, vA)
        vA2 = p.do("vector", lambda eng: eng.tensor_single_scalar(
            negA[:], negA[:], -1.0, ALU.mult), ev, 1)
        p.wait("vector", cs_s, h2a_v)
        p.wait("vector", cs2_s, h2a2_v)
        vdh = p.do("vector", lambda eng: eng.tensor_mul(dh[:], delta[:], cshared[:]), ev, 1)
        v_scan = [None] * 16
        v_prod = [None] * 16
        v_y = [None] * 16
        v_ab = [None] * 16
        for n in range(16):
            cb = cshared if n % 2 == 0 else cded
            cb_sem = cs_s if n % 2 == 0 else cd_s
            qb = QRR[qk % 3]; qk += 1
            p.wait(qb, ev, v_bx[n % 2] if n >= 2 else vdh)
            bb_v = p.do(qb, lambda eng, n=n: eng.dma_start(
                out=bbc[n % 2][:], in_=r.d_bcrows[n:n + 1, :].broadcast_to((128, L))),
                bbs[n % 2], 16)
            qc = QRR[qk % 3]; qk += 1
            if n >= 2:
                p.wait(qc, ev, v_prod[n - 2])
            elif n == 0:
                p.wait(qc, ev, vdh)
            elif v_prod15_prev is not None:
                p.wait(qc, ev, v_prod15_prev)
            cb_v = p.do(qc, lambda eng, n=n, cb=cb: eng.dma_start(
                out=cb[:], in_=r.d_bcrows[16 + n:16 + n + 1, :].broadcast_to((128, L))),
                cb_sem, 16)
            p.wait("scalar", ev, v_scan[n - 1] if n > 0 else vdh)
            v_ab[n] = p.do("scalar", lambda eng, n=n: eng.activation(
                abar[:], delta[:], AF.Exp, scale=negA[:, n:n + 1]), sq, 1)
            p.wait("vector", bbs[n % 2], bb_v)
            v_bx[n % 2] = p.do("vector", lambda eng, n=n: eng.tensor_mul(
                bx[:], dh[:], bbc[n % 2][:]), ev, 1)
            p.wait("vector", sq, v_ab[n])
            v_scan[n] = p.do("vector", lambda eng: eng.tensor_tensor_scan(
                S_sb[:], abar[:], bx[:], 0.0, ALU.mult, ALU.add), ev, 1)
            p.wait("vector", cb_sem, cb_v)
            if n >= 1:
                p.wait("vector", mm, v_y[n - 1])
            v_prod[n] = p.do("vector", lambda eng, cb=cb: eng.tensor_mul(
                pr[:], S_sb[:], cb[:]), ev, 1)
            p.wait("tensor", ev, v_prod[n])
            for c in range(8):
                v_y[n] = p.do("tensor", lambda eng, n=n, c=c: eng.matmul(
                    ps_y[:, 512 * c:512 * (c + 1)], r.ident_f32[:],
                    pr[:, 512 * c:512 * (c + 1)],
                    start=(n == 0), stop=(n == 15)), mm, 1)
        v_prod15_prev = v_prod[15]
        p.wait("sync", ev, v_prod[14])
        p.wait("gpsimd", ev, v_prod[14])
        h2b_v = p.do("sync", lambda eng, fg=fg: eng.dma_start(
            out=cshared[:, 0:HL], in_=r.d_h2f[128 * fg:128 * (fg + 1), 0:HL]), cs_s, 16)
        h2b2_v = p.do("gpsimd", lambda eng, fg=fg: eng.dma_start(
            out=cshared[:, HL:L], in_=r.d_h2f[128 * fg:128 * (fg + 1), HL:L]), cs2_s, 16)
        p.wait("sync", ev, v_scan[15])
        p.wait("gpsimd", ev, v_scan[15])
        p.do("sync", lambda eng, fg=fg: eng.dma_start(
            out=bx[:, 0:HL], in_=r.d_h0f[128 * fg:128 * (fg + 1), 0:HL]), h0_s, 16)
        p.do("gpsimd", lambda eng, fg=fg: eng.dma_start(
            out=bx[:, HL:L], in_=r.d_h0f[128 * fg:128 * (fg + 1), HL:L]), h02_s, 16)
        p.wait("vector", mm, v_y[15])
        p.wait("vector", cs_s, h2b_v)
        p.wait("vector", cs2_s, h2b2_v)
        for c in range(8):
            v_stt_last = p.do("vector", lambda eng, fg=fg, c=c: eng.scalar_tensor_tensor(
                dh[:, 512 * c:512 * (c + 1)], cshared[:, 512 * c:512 * (c + 1)],
                D_sb[:, fg:fg + 1], ps_y[:, 512 * c:512 * (c + 1)],
                ALU.mult, ALU.add), ev, 1)
        p.wait("scalar", h0_s)
        p.wait("scalar", h02_s)
        p.wait("scalar", mm, v_y[15])
        vsil = p.do("scalar", lambda eng: eng.activation(pr[:], bx[:], AF.Silu), sq, 1)
        p.wait("vector", sq, vsil)
        vg = p.do("vector", lambda eng: eng.tensor_mul(delta[:], dh[:], pr[:]), ev, 1)
        p.wait("gpsimd", ev, vg)
        p.do("gpsimd", lambda eng, fg=fg: eng.dma_start(
            out=r.d_gf[128 * fg:128 * (fg + 1), :], in_=delta[:]), gst_s, 16)
        p.wait("tensor", ev, vg)
        v_gs = [None] * 8
        for c in range(8):
            v_gs[c] = p.do("tensor", lambda eng, c=c: eng.matmul(
                ps_y[0:1, 512 * c:512 * (c + 1)], r.ones_col[:],
                delta[:, 512 * c:512 * (c + 1)], start=True, stop=True), mm, 1)
        vadd_gs = [None] * 8
        for c in range(8):
            p.wait("vector", mm, v_gs[c])
            vadd_gs[c] = p.do("vector", lambda eng, c=c: eng.tensor_add(
                gs_row[:, 512 * c:512 * (c + 1)], gs_row[:, 512 * c:512 * (c + 1)],
                ps_y[0:1, 512 * c:512 * (c + 1)]), ev, 1)
        v_gq_prev = None
        for c in range(8):
            p.wait("scalar", ev, vg)
            if v_gq_prev is not None:
                p.wait("scalar", mm, v_gq_prev)
            vq2 = p.do("scalar", lambda eng, c=c: eng.activation(
                mcol[:], delta[:, 512 * c:512 * (c + 1)], AF.Square), sq, 1)
            p.wait("tensor", sq, vq2)
            p.wait("tensor", ev, vadd_gs[c])
            v_gq_prev = p.do("tensor", lambda eng, c=c: eng.matmul(
                ps_y[0:1, 512 * c:512 * (c + 1)], r.ones_col[:], mcol[:],
                start=True, stop=True), mm, 1)
            p.wait("vector", mm, v_gq_prev)
            p.do("vector", lambda eng, c=c: eng.tensor_add(
                gq_row[:, 512 * c:512 * (c + 1)], gq_row[:, 512 * c:512 * (c + 1)],
                ps_y[0:1, 512 * c:512 * (c + 1)]), ev, 1)
    p.wait("sync", gst_s)
    p.wait("sync", ev)
    p.wait("sync", mm)
    p.wait("tensor", ev)
    p.wait("gpsimd", ev)
    p.wait("gpsimd", cs_s)
    p.wait("gpsimd", cs2_s)
    p.wait("gpsimd", h02_s)
    p.wait("gpsimd", cd_s)
    p.wait("scalar", cs_s)
    p.wait("scalar", cs2_s)
    p.wait("scalar", h02_s)
    p.wait("scalar", cd_s)
    p.wait("scalar", bbs[0])
    p.wait("scalar", bbs[1])
    s6.close()

    mu_row = r.sb(s5, [1, L], F32, base="abmu")
    rs_row = r.sb(s5, [1, L], F32, base="abrs")
    nmu_row = r.sb(s5, [1, L], F32, base="abnm")
    p.wait("vector", ev)
    p.do("vector", lambda eng: eng.tensor_single_scalar(
        mu_row[:], gs_row[:], 1.0 / D, ALU.mult), ev, 1)
    p.wait("vector", ev)
    p.do("vector", lambda eng: eng.tensor_single_scalar(
        gq_row[:], gq_row[:], 1.0 / D, ALU.mult), ev, 1)
    p.wait("vector", ev)
    p.do("vector", lambda eng: eng.tensor_mul(gs_row[:], mu_row[:], mu_row[:]), ev, 1)
    p.wait("vector", ev)
    p.do("vector", lambda eng: eng.tensor_sub(gq_row[:], gq_row[:], gs_row[:]), ev, 1)
    p.wait("scalar", ev)
    p.do("scalar", lambda eng: eng.activation(gq_row[:], gq_row[:], AF.Ln, bias=EPS_B), ev, 1)
    p.wait("scalar", ev)
    p.do("scalar", lambda eng: eng.activation(rs_row[:], gq_row[:], AF.Exp, scale=-0.5), ev, 1)
    p.wait("vector", ev)
    p.do("vector", lambda eng: eng.tensor_single_scalar(
        nmu_row[:], mu_row[:], -1.0, ALU.mult), ev, 1)
    p.wait("gpsimd", ev)
    p.do("gpsimd", lambda eng: eng.dma_start(out=r.d_murow[:], in_=rs_row[:]), st, 16)
    p.do("gpsimd", lambda eng: eng.dma_start(out=r.d_rsrow[:], in_=nmu_row[:]), st, 16)
    p.wait("sync", st)
    gpool = Pool(r, "p1", r.sb(s5, [128, LC], F32, 3, "abgp"))
    rsbcp = Pool(r, "p2", r.sb(s5, [128, LC], F32, 2, "abrsb"))
    nmbcp = Pool(r, "p3", r.sb(s5, [128, LC], F32, 2, "abnmb"))
    t1pool = Pool(r, "p4", r.sb(s5, [128, LC], F32, 2, "abt1"))
    gnpool = Pool(r, "p5", r.sb(s5, [128, LC], BF16, 2, "abgn"))
    for lc in range(NLC):
        sl = slice(LC * lc, LC * (lc + 1))
        rsb, rss = rsbcp.acquire(p, "sync")
        rssem, rsv = rsbcp.dma(p, "sync", rss, lambda eng, rsb=rsb, sl=sl: eng.dma_start(
            out=rsb[:], in_=r.d_murow[:, sl].broadcast_to((128, LC))))
        nmb, nms = nmbcp.acquire(p, "sync")
        nmsem, nmv = nmbcp.dma(p, "sync", nms, lambda eng, nmb=nmb, sl=sl: eng.dma_start(
            out=nmb[:], in_=r.d_rsrow[:, sl].broadcast_to((128, LC))))
        p.wait("vector", rssem, rsv)
        p.wait("vector", nmsem, nmv)
        lastg = None
        for ic in range(NFC):
            gb, gss = gpool.acquire(p, "sync")
            gsem, gv = gpool.dma(p, "sync", gss, lambda eng, gb=gb, ic=ic, sl=sl: eng.dma_start(
                out=gb[:], in_=r.d_gf[128 * ic:128 * (ic + 1), sl]))
            t1, t1s = t1pool.acquire(p, "vector")
            p.wait("vector", gsem, gv)
            v1 = p.do("vector", lambda eng, gb=gb, t1=t1, nmb=nmb: eng.tensor_add(
                t1[:], gb[:], nmb[:]), ev, 1)
            gpool.release(gss, ev, v1)
            gnb, gns = gnpool.acquire(p, "vector")
            p.wait("vector", ev, v1)
            v2 = p.do("vector", lambda eng, t1=t1, gnb=gnb, rsb=rsb: eng.tensor_mul(
                gnb[:], t1[:], rsb[:]), ev, 1)
            t1pool.release(t1s, ev, v2)
            lastg = v2
            p.wait("gpsimd", ev, v2)
            gsm, gwv = gnpool.dma(p, "gpsimd", gns, lambda eng, gnb=gnb, ic=ic, sl=sl: eng.dma_start(
                out=r.d_gnb[128 * ic:128 * (ic + 1), sl], in_=gnb[:]))
            gnpool.release(gns, gsm, gwv)
        rsbcp.release(rss, ev, lastg)
        nmbcp.release(nms, ev, lastg)
    for s in gnpool.sems:
        p.wait("sync", s)
    woT = r.sb(s5, [128, NFC * D], BF16, base="abwo")
    bout_sb = r.sb(s5, [128, 8], F32, base="abbo")
    gbpool = Pool(r, "p6", r.sb(s5, [128, NFC * LC], BF16, 2, "abgb"))
    x0pool = Pool(r, "p7", r.sb(s5, [128, LC], F32, 2, "abx0"))
    x2pool = Pool(r, "p8", r.sb(s5, [128, LC], F32, 2, "abx2"))
    rpool = Pool(r, "p9", r.sb(s5, [128, LC], F32, 2, "abrp"))
    ps_w7 = [r.ps(s5, [128, LC], "abpw7") for _ in range(2)]
    ldo5 = r.sem("ab_ldo5")
    for dc in range(NFC):
        q, s_ = (("sync", gld), ("gpsimd", ldo5))[dc % 2]
        p.do(q, lambda eng, dc=dc: eng.dma_start(
            out=woT[:, D * dc:D * (dc + 1)],
            in_=r.d_woutT_bf[128 * dc:128 * (dc + 1), :]), s_, 16)
    p.do("sync", lambda eng: eng.dma_start(out=bout_sb[:], in_=r.d_boutp[:]), gld, 16)
    p.wait("tensor", gld)
    p.wait("tensor", ldo5)
    k = 0
    for lc in range(NLC):
        sl = slice(LC * lc, LC * (lc + 1))
        gb, gss = gbpool.acquire(p, "sync")
        gsem, gv = gbpool.dma(p, "sync", gss, lambda eng, gb=gb, sl=sl: eng.dma_start(
            out=gb[:].rearrange("p (c l) -> p c l", c=NFC),
            in_=r.d_gnb[:, sl].rearrange("(c p) l -> p c l", p=128)))
        p.wait("tensor", gsem, gv)
        vwl = None
        for oc in range(NFC):
            pw = ps_w7[k % 2]
            x0b, x0s = x0pool.acquire(p, "sync")
            x0sem, x0v = x0pool.dma(p, "sync", x0s, lambda eng, x0b=x0b, oc=oc, sl=sl: eng.dma_start(
                out=x0b[:], in_=r.d_xn32[128 * oc:128 * (oc + 1), sl]))
            x2b, x2s = x2pool.acquire(p, "sync")
            x2sem, x2v = x2pool.dma(p, "sync", x2s, lambda eng, x2b=x2b, oc=oc, sl=sl: eng.dma_start(
                out=x2b[:], in_=x_src[128 * oc:128 * (oc + 1), sl]))
            if k >= 2:
                p.wait("tensor", ev)
            vw = None
            for ic in range(NFC):
                vw = p.do("tensor", lambda eng, pw=pw, ic=ic, oc=oc, gb=gb,
                          f0=(ic == 0), fl=(ic == NFC - 1): eng.matmul(
                    pw[:], woT[:, D * ic + 128 * oc:D * ic + 128 * (oc + 1)],
                    gb[:, LC * ic:LC * ic + LC], start=f0, stop=fl), mm, 1)
            vwl = vw
            rb, rs = rpool.acquire(p, "vector")
            p.wait("vector", mm, vw)
            p.wait("vector", x0sem, x0v)
            p.wait("vector", x2sem, x2v)
            v1 = p.do("vector", lambda eng, pw=pw, rb=rb, x0b=x0b, oc=oc: eng.scalar_tensor_tensor(
                rb[:], pw[:], bout_sb[:, oc:oc + 1], x0b[:], ALU.add, ALU.add), ev, 1)
            x0pool.release(x0s, ev, v1)
            p.wait("vector", ev, v1)
            v2 = p.do("vector", lambda eng, rb=rb, x2b=x2b: eng.tensor_add(
                rb[:], rb[:], x2b[:]), ev, 1)
            x2pool.release(x2s, ev, v2)
            p.wait("gpsimd", ev, v2)
            rsm, rwv = rpool.dma(p, "gpsimd", rs, lambda eng, rb=rb, oc=oc, sl=sl: eng.dma_start(
                out=x_dst[128 * oc:128 * (oc + 1), sl], in_=rb[:]))
            rpool.release(rs, rsm, rwv)
            k += 1
        gbpool.release(gss, mm, vwl)
    for s in rpool.sems:
        p.wait("gpsimd", s)
        p.wait("sync", s)
    s5.close()
    p.barrier(bar)

def emit_ffn_experts_compact(r, scope, x_src, x_dst):
    p = r.p
    nc = r.nc
    mm = r.sem("fe_mm")
    ev = r.sem("fe_ev")
    st = r.sem("fe_st")
    gld = r.sem("fe_gld")
    sq = r.sem("fe_sq")
    SL = SLOT
    NCH = [(0, 512), (512, SL - 512)]

    se = scope.enter_context(contextlib.ExitStack())
    h1_big = r.sb(se, [128, 32 * SL], BF16, base="fch1")
    w2e = r.sb(se, [128, 32 * D], BF16, base="fcw2")
    b1_sb = r.sb(se, [128, 256], F32, base="fcb1")
    relu_t = r.sb(se, [128, SL], F32, base="fcrt")
    xep = Pool(r, "p3", r.sb(se, [128, NFC * SL], BF16, 2, "fcxe"))
    w1pool = Pool(r, "p0", r.sb(se, [128, NFC * 128], BF16, 3, "fcw1"))
    w1fpool = Pool(r, "p4", r.sb(se, [128, NFC * 128], F32, 3, "fcw1f"))
    w2fpool = Pool(r, "p5", r.sb(se, [128, 2 * D], F32, 2, "fcw2f"))
    sepool = Pool(r, "p1", r.sb(se, [128, SL], F32, 2, "fcse"))
    srowp = Pool(r, "p6", r.sb(se, [1, SL], F32, 2, "fcsr"))
    b2rp = Pool(r, "p7", r.sb(se, [1, D], F32, 2, "fcb2"))
    opool = Pool(r, "p2", r.sb(se, [128, D], F32, 2, "fco"))
    ps_h = [r.ps(se, [128, SL], "fcph") for _ in range(2)]
    ps_o = [r.ps(se, [128, D], "fcpo") for _ in range(2)]

    ld0 = r.sem("fe_ld0")
    p.do("sync", lambda eng: eng.dma_start(out=b1_sb[:], in_=r.d_b1[:]), ld0, 16)
    p.wait("vector", ld0)

    sslot_row = r.d_sslot.rearrange("(o s) x -> (o x) s", o=1)
    prev_lastw2 = None
    v_hev = [None, None]
    v_oev = [None, None]
    for e in range(NE):
        xe, xes = xep.acquire(p, "sync")
        xsem = None
        for dk in range(NFC):
            xsem, xv = xep.dma(p, "sync", xes, lambda eng, xe=xe, dk=dk, e=e: eng.dma_start(
                out=xe[:, SL * dk:SL * (dk + 1)],
                in_=r.d_bufx[SL * e:SL * (e + 1), 128 * dk:128 * (dk + 1)],
                transpose=True))
        seb, ses = sepool.acquire(p, "sync")
        sesem, sev = sepool.dma(p, "sync", ses, lambda eng, seb=seb, e=e: eng.dma_start(
            out=seb[:], in_=sslot_row[:, SL * e:SL * (e + 1)].broadcast_to((128, SL))))
        sro, srs = srowp.acquire(p, "sync")
        srsem, srv = srowp.dma(p, "sync", srs, lambda eng, sro=sro, e=e: eng.dma_start(
            out=sro[:], in_=sslot_row[:, SL * e:SL * (e + 1)]))
        b2r, b2s = b2rp.acquire(p, "sync")
        b2sem, b2v = b2rp.dma(p, "sync", b2s, lambda eng, b2r=b2r, e=e: eng.dma_start(
            out=b2r[:], in_=r.d_b2[e:e + 1, :]))
        p.wait("tensor", xsem, xv)
        last_h = None
        for fo in range(32):
            w1f, w1fs = w1fpool.acquire(p, "sync")
            w1fsem, w1fv = w1fpool.dma(p, "sync", w1fs, lambda eng, w1f=w1f, e=e, fo=fo: eng.dma_start(
                out=w1f[:].rearrange("p (c f) -> p c f", c=NFC),
                in_=r.d_w1[D * e:D * (e + 1), 128 * fo:128 * (fo + 1)].rearrange(
                    "(c p) f -> p c f", p=128)))
            w1b, w1s = w1pool.acquire(p, "scalar")
            p.wait("scalar", w1fsem, w1fv)
            vw1c = p.do("scalar", lambda eng, w1b=w1b, w1f=w1f: eng.copy(
                w1b[:], w1f[:]), sq, 1)
            w1fpool.release(w1fs, sq, vw1c)
            ph = ps_h[fo % 2]
            p.wait("tensor", sq, vw1c)
            if v_hev[fo % 2] is not None:
                p.wait("tensor", ev, v_hev[fo % 2])
            vmm = None
            for off, wdt in NCH:
                for dk in range(NFC):
                    vmm = p.do("tensor", lambda eng, ph=ph, w1b=w1b, dk=dk, off=off,
                               wdt=wdt, xe=xe, f0=(dk == 0), fl=(dk == NFC - 1): eng.matmul(
                        ph[:, off:off + wdt],
                        w1b[:, 128 * dk:128 * (dk + 1)],
                        xe[:, SL * dk + off:SL * dk + off + wdt],
                        start=f0, stop=fl), mm, 1)
            last_h = vmm
            w1pool.release(w1s, mm, vmm)
            p.wait("vector", mm, vmm)
            p.wait("vector", ev)
            v1 = p.do("vector", lambda eng, ph=ph, e=e, fo=fo: eng.tensor_scalar(
                relu_t[:], ph[:], b1_sb[:, 32 * e + fo:32 * e + fo + 1], 0.0,
                ALU.add, ALU.max), ev, 1)
            v_hev[fo % 2] = v1
            p.wait("vector", sesem, sev)
            p.wait("vector", ev, v1)
            p.do("vector", lambda eng, fo=fo, seb=seb: eng.tensor_mul(
                h1_big[:, SL * fo:SL * (fo + 1)], relu_t[:], seb[:]), ev, 1)
        xep.release(xes, mm, last_h)
        sepool.release(ses, ev, ev.n)
        if prev_lastw2 is not None:
            p.wait("scalar", mm, prev_lastw2)
        vw2c = None
        for ck in range(16):
            q2 = "gpsimd" if ck % 2 == 0 else "sync"
            w2f, w2fs = w2fpool.acquire(p, q2)
            w2fsem, w2fv = w2fpool.dma(p, q2, w2fs, lambda eng, w2f=w2f, e=e, ck=ck: eng.dma_start(
                out=w2f[:].rearrange("p (c o) -> p c o", c=2),
                in_=r.d_w2[FF * e + 256 * ck:FF * e + 256 * (ck + 1), :].rearrange(
                    "(c p) o -> p c o", p=128)))
            p.wait("scalar", w2fsem, w2fv)
            vw2c = p.do("scalar", lambda eng, w2f=w2f, ck=ck: eng.copy(
                w2e[:, 2 * D * ck:2 * D * (ck + 1)], w2f[:]), sq, 1)
            w2fpool.release(w2fs, sq, vw2c)
        p.wait("tensor", sq, vw2c)
        p.wait("tensor", ev)
        p.wait("tensor", srsem, srv)
        p.wait("tensor", b2sem, b2v)
        lastw2 = None
        for sc in range(SL // 128):
            po = ps_o[sc % 2]
            if v_oev[sc % 2] is not None:
                p.wait("tensor", ev, v_oev[sc % 2])
            vmm = None
            for half in range(2):
                for fk in range(32):
                    vmm = p.do("tensor", lambda eng, po=po, fk=fk, sc=sc, half=half,
                               f0=(fk == 0), fl=(fk == 31): eng.matmul(
                        po[:, 512 * half:512 * (half + 1)],
                        h1_big[:, SL * fk + 128 * sc:SL * fk + 128 * sc + 128],
                        w2e[:, D * fk + 512 * half:D * fk + 512 * half + 512],
                        start=f0, stop=fl), mm, 1)
                vmm = p.do("tensor", lambda eng, po=po, sc=sc, half=half, sro=sro,
                           b2r=b2r: eng.matmul(
                    po[:, 512 * half:512 * (half + 1)],
                    sro[:, 128 * sc:128 * (sc + 1)],
                    b2r[:, 512 * half:512 * (half + 1)], start=False, stop=True), mm, 1)
            lastw2 = vmm
            ob, os_ = opool.acquire(p, "vector")
            p.wait("vector", mm, vmm)
            vcp = p.do("vector", lambda eng, po=po, ob=ob: eng.tensor_copy(ob[:], po[:]), ev, 1)
            v_oev[sc % 2] = vcp
            p.wait("gpsimd", ev, vcp)
            ssem, sv = opool.dma(p, "gpsimd", os_, lambda eng, ob=ob, e=e, sc=sc: eng.dma_start(
                out=r.d_ybuf[SL * e + 128 * sc:SL * e + 128 * (sc + 1), :], in_=ob[:]))
            opool.release(os_, ssem, sv)
        srowp.release(srs, mm, lastw2)
        b2rp.release(b2s, mm, lastw2)
        prev_lastw2 = lastw2
    for s in opool.sems:
        p.wait("sync", s)
        p.wait("gpsimd", s)
    se.close()

    scg = scope.enter_context(contextlib.ExitStack())
    slots3 = r.sb(scg, [128, 32], mybir.dt.int32, base="fcs3")
    gthp = Pool(r, "p4", r.sb(scg, [128, D], F32, 2, "fcgt"))
    x1p = Pool(r, "p5", r.sb(scg, [128, NFC * 128], F32, 2, "fcx1"))
    xoutp = Pool(r, "p1", r.sb(scg, [128, NFC * 128], F32, 2, "fcxo"))
    ps_t = [r.ps(scg, [128, 128], "fcpt") for _ in range(2)]
    v_tev = [None, None]
    p.do("sync", lambda eng: eng.dma_start(out=slots3[:], in_=r.d_slots[:]), gld, 16)
    p.wait("gpsimd", gld)
    for lt in range(32):
        gth, gs_ = gthp.acquire(p, "gpsimd")
        gsem, gv = gthp.dma(p, "gpsimd", gs_, lambda eng, gth=gth, lt=lt: eng.indirect_dma_start(
            gth[:], None, r.d_ybuf[:], IndirectOffsetOnAxis(slots3[:, lt:lt + 1], 0),
            oob_is_err=False))
        x1b, x1s = x1p.acquire(p, "sync")
        x1sem, x1v = x1p.dma(p, "sync", x1s, lambda eng, x1b=x1b, lt=lt: eng.dma_start(
            out=x1b[:].rearrange("p (c l) -> p c l", c=NFC),
            in_=x_src[:, 128 * lt:128 * (lt + 1)].rearrange("(c p) l -> p c l", p=128)))
        xob, xos = xoutp.acquire(p, "vector")
        lastv = None
        for fc in range(NFC):
            pt = ps_t[fc % 2]
            if v_tev[fc % 2] is not None:
                p.wait("tensor", ev, v_tev[fc % 2])
            p.wait("tensor", gsem, gv)
            vt = p.do("tensor", lambda eng, gth=gth, fc=fc, pt=pt: eng.transpose(
                pt[:], gth[:, 128 * fc:128 * (fc + 1)], r.ident_f32[:]), mm, 1)
            p.wait("vector", mm, vt)
            p.wait("vector", x1sem, x1v)
            lastv = p.do("vector", lambda eng, pt=pt, x1b=x1b, xob=xob, fc=fc: eng.tensor_add(
                xob[:, 128 * fc:128 * (fc + 1)], pt[:], x1b[:, 128 * fc:128 * (fc + 1)]),
                ev, 1)
            v_tev[fc % 2] = lastv
        gthp.release(gs_, mm, mm.n)
        x1p.release(x1s, ev, lastv)
        p.wait("gpsimd", ev, lastv)
        ssem, sv = xoutp.dma(p, "gpsimd", xos, lambda eng, xob=xob, lt=lt: eng.dma_start(
            out=x_dst[:, 128 * lt:128 * (lt + 1)].rearrange("(c p) l -> p c l", p=128),
            in_=xob[:].rearrange("p (c l) -> p c l", c=NFC)))
        xoutp.release(xos, ssem, sv)
    for s in xoutp.sems:
        p.wait("gpsimd", s)
        p.wait("sync", s)
    scg.close()



def kernel(**inputs):
    nc = build_nc("all")
    x = np.asarray(inputs["L0_attn_a_x"], np.float32)
    w = {k: np.asarray(v, np.float32) for k, v in inputs.items()}
    col = lambda v: np.ascontiguousarray(v.reshape(8, 128).T)
    shared = {
        "wqT": np.ascontiguousarray(w["L0_attn_a_Wq"].T),
        "wkT": np.ascontiguousarray(w["L0_attn_a_Wk"].T),
        "wvT": np.ascontiguousarray(w["L0_attn_a_Wv"].T),
        "woT": np.ascontiguousarray(w["L0_attn_a_Wo"].T),
        "wg": np.ascontiguousarray(w["L0_ffn_Wg"]),
        "w1": np.ascontiguousarray(w["L0_ffn_W1"].reshape(NE * D, FF)),
        "b1p": np.ascontiguousarray(
            w["L0_ffn_b1"].reshape(NE, 32, 128).transpose(2, 0, 1).reshape(128, 256)),
        "w2": np.ascontiguousarray(w["L0_ffn_W2"].reshape(NE * FF, D)),
        "b2m": np.ascontiguousarray(w["L0_ffn_b2"]),
        "winT": np.ascontiguousarray(w["L0_attn_b_w_in"].T),
        "binp": col(w["L0_attn_b_b_in"]),
        "cvT": np.ascontiguousarray(np.concatenate(
            [w["L0_attn_b_conv_w"][:, :, kk].T for kk in range(3)], axis=0)),
        "cbp": col(w["L0_attn_b_conv_b"]),
        "alogd": np.ascontiguousarray(w["L0_attn_b_A_log"]),
        "Dp": col(w["L0_attn_b_D"]),
        "wbcdT": np.ascontiguousarray(np.concatenate(
            [w["L0_attn_b_w_d1"].T, w["L0_attn_b_w_B"].T, w["L0_attn_b_w_C"].T], axis=1)),
        "bbcd": np.ascontiguousarray(np.concatenate(
            [w["L0_attn_b_b_d1"], w["L0_attn_b_b_B"], w["L0_attn_b_b_C"]])[:, None]),
        "wd2T": np.ascontiguousarray(w["L0_attn_b_w_d2"].T),
        "bd2p": col(w["L0_attn_b_b_d2"]),
        "woutT": np.ascontiguousarray(w["L0_attn_b_w_out"].T),
        "boutp": col(w["L0_attn_b_b_out"]),
    }
    in_maps = []
    for c in range(NCORE):
        mask = np.zeros((32, 1), np.float32)
        for rr_ in range(c):
            mask[4 * rr_, 0] = 1.0
        m = dict(shared)
        m["xT"] = np.ascontiguousarray(x[c].T)
        m["memask"] = mask
        in_maps.append(m)
    res = run_bass_kernel_spmd(nc, in_maps, core_ids=list(range(NCORE)))
    out = np.stack([np.ascontiguousarray(res.results[c]["outT"].T)
                    for c in range(NCORE)], axis=0)
    return out


# version 22 (final)
# speedup vs baseline: 1.0058x; 1.0018x over previous
import contextlib
import numpy as np
import ml_dtypes
import concourse.bass as bass
import concourse.mybir as mybir
from concourse.bass_utils import run_bass_kernel_spmd
from concourse.bass_core_rust import IndirectOffsetOnAxis

F32 = mybir.dt.float32
BF16 = mybir.dt.bfloat16
ALU = mybir.AluOpType
AF = mybir.ActivationFunctionType
AX = mybir.AxisListType

D = 1024
L = 4096
NH = 16
DH = 64
NE = 8
FF = 4096
NCORE = 8
EPS_LN = 1e-5
EPS_B = 1e-5
EPS_ATT = 1e-6
CAP = 4096
LC = 512
NLC = L // LC
NFC = D // 128
HL = L // 2
SLOT = 768

ENGINES = ("sync", "vector", "scalar", "tensor", "gpsimd")


class Sem:
    __slots__ = ("h", "n")

    def __init__(self, h):
        self.h = h
        self.n = 0


class Prog:
    def __init__(self, nc):
        self.nc = nc
        self.cmds = {e: [] for e in ENGINES}

    def do(self, engine, fn, sem=None, k=1):
        if sem is None:
            self.cmds[engine].append(fn)
            return None
        sem.n += k
        v = sem.n
        self.cmds[engine].append(lambda eng: fn(eng).then_inc(sem.h, k))
        return v

    def wait(self, engine, sem, val=None):
        v = sem.n if val is None else val
        self.cmds[engine].append(lambda eng: eng.wait_ge(sem.h, v))
        return v

    def barrier(self, bar):
        for e in ENGINES:
            bar.n += 1
            self.cmds[e].append(lambda eng, s=bar.h: eng.drain().then_inc(s, 1))
        tgt = bar.n
        for e in ENGINES:
            self.cmds[e].append(lambda eng, s=bar.h, v=tgt: eng.wait_ge(s, v))

    def replay(self, block):
        def run(eng, cmds):
            for c in cmds:
                c(eng)

        @block.sync
        def _(eng):
            run(eng, self.cmds["sync"])

        @block.vector
        def _(eng):
            run(eng, self.cmds["vector"])

        @block.scalar
        def _(eng):
            run(eng, self.cmds["scalar"])

        @block.tensor
        def _(eng):
            run(eng, self.cmds["tensor"])

        @block.gpsimd
        def _(eng):
            run(eng, self.cmds["gpsimd"])


class Pool:

    def __init__(self, r, tag, bufs):
        self.bufs = bufs
        self.sems = [r.sem(f"{tag}_s{i}") for i in range(len(bufs))]
        self.free_at = [[] for _ in bufs]
        self.k = 0

    def acquire(self, p, writer_engine):
        s = self.k % len(self.bufs)
        self.k += 1
        if self.free_at[s]:
            for sem, v in self.free_at[s]:
                p.wait(writer_engine, sem, v)
        self.free_at[s] = []
        return self.bufs[s], s

    def dma(self, p, queue, slot, fn):
        v = p.do(queue, fn, self.sems[slot], 16)
        return self.sems[slot], v

    def release(self, slot, sem, val):
        if self.free_at[slot] is None:
            self.free_at[slot] = []
        self.free_at[slot].append((sem, val))


class R:
    def __init__(self, nc, prog, stack):
        self.nc = nc
        self.p = prog
        self.stack = stack
        self.sems = {}
        self.uid = 0

    def sem(self, name):
        if name not in self.sems:
            h = self.stack.enter_context(self.nc.semaphore(name))
            self.sems[name] = Sem(h)
        return self.sems[name]

    def name(self, base):
        self.uid += 1
        return f"{base}{self.uid}"

    def sb(self, scope, shape, dt, n=1, base="t"):
        ts = [scope.enter_context(self.nc.sbuf_tensor(self.name(base), shape, dt))
              for _ in range(n)]
        return ts[0] if n == 1 else ts

    def ps(self, scope, shape, base="ps"):
        return scope.enter_context(self.nc.psum_tensor(self.name(base), shape, F32))



def emit_cast(r, scope, pairs):
    p = r.p
    cv = r.sem("c_cv")
    CC = 2048
    fpool = Pool(r, "p0", r.sb(scope, [128, CC], F32, 3, "castf"))
    bpool = Pool(r, "p1", r.sb(scope, [128, CC], BF16, 3, "castb"))
    for (src, dst, rows, cols) in pairs:
        cc = min(cols, CC)
        assert cols % cc == 0, (rows, cols)
        for i in range((rows + 127) // 128):
            pr = min(128, rows - 128 * i)
            for j in range(max(cols // CC, 1)):
                fb, fs = fpool.acquire(p, "sync")
                lsem, lv = fpool.dma(p, "sync", fs, lambda eng, fb=fb, src=src, i=i, j=j,
                                     cc=cc, pr=pr: eng.dma_start(
                    out=fb[0:pr, 0:cc], in_=src[128 * i:128 * i + pr, cc * j:cc * (j + 1)]))
                bb, bs = bpool.acquire(p, "vector")
                p.wait("vector", lsem, lv)
                v = p.do("vector", lambda eng, fb=fb, bb=bb, cc=cc, pr=pr: eng.tensor_copy(
                    bb[0:pr, 0:cc], fb[0:pr, 0:cc]), cv, 1)
                fpool.release(fs, cv, v)
                p.wait("gpsimd", cv)
                ssem, sv = bpool.dma(p, "gpsimd", bs, lambda eng, bb=bb, dst=dst, i=i, j=j,
                                     cc=cc, pr=pr: eng.dma_start(
                    out=dst[128 * i:128 * i + pr, cc * j:cc * (j + 1)], in_=bb[0:pr, 0:cc]))
                bpool.release(bs, ssem, sv)
    for s in bpool.sems:
        p.wait("gpsimd", s)



def emit_ln(r, scope, src, xn_bf, xn_f32=None):
    p = r.p
    sq = r.sem("ln_sq")
    mm = r.sem("ln_mm")
    ev = r.sem("ln_ev")
    st = r.sem("ln_st")

    xpool = Pool(r, "p0", r.sb(scope, [128, LC], F32, 3, "lnx"))
    sqpool = Pool(r, "p1", r.sb(scope, [128, LC], F32, 2, "lnsq"))
    mu_row = r.sb(scope, [1, L], F32, base="lnmu")
    rs_row = r.sb(scope, [1, L], F32, base="lnrs")
    srow = r.sb(scope, [1, LC], F32, base="lns")
    qrow = r.sb(scope, [1, LC], F32, base="lnq")
    trow = r.sb(scope, [1, LC], F32, base="lnt")
    ps_s = r.ps(scope, [1, LC], "lnps")
    ps_q = r.ps(scope, [1, LC], "lnpq")

    for lc in range(NLC):
        if lc > 0:
            p.wait("tensor", ev)
        for fc in range(NFC):
            xb, xs = xpool.acquire(p, "sync")
            lsem, lv = xpool.dma(p, "sync", xs, lambda eng, xb=xb, fc=fc, lc=lc: eng.dma_start(
                out=xb[:], in_=src[128 * fc:128 * (fc + 1), LC * lc:LC * (lc + 1)]))
            sb_, ss_ = sqpool.acquire(p, "scalar")
            p.wait("scalar", lsem, lv)
            vq = p.do("scalar", lambda eng, xb=xb, sb_=sb_: eng.activation(
                sb_[:], xb[:], AF.Square), sq, 1)
            p.wait("tensor", lsem, lv)
            v1 = p.do("tensor", lambda eng, xb=xb, f0=(fc == 0), fl=(fc == NFC - 1): eng.matmul(
                ps_s[:], r.ones_col[:], xb[:], start=f0, stop=fl), mm, 1)
            xpool.release(xs, mm, v1)
            xpool.release(xs, sq, vq)
            p.wait("tensor", sq, vq)
            v2 = p.do("tensor", lambda eng, sb_=sb_, f0=(fc == 0), fl=(fc == NFC - 1): eng.matmul(
                ps_q[:], r.ones_col[:], sb_[:], start=f0, stop=fl), mm, 1)
            sqpool.release(ss_, mm, v2)
        sl = slice(LC * lc, LC * (lc + 1))
        p.wait("vector", mm)
        p.do("vector", lambda eng, sl=sl: eng.tensor_single_scalar(
            mu_row[:, sl], ps_s[:], 1.0 / D, ALU.mult), ev, 1)
        p.do("vector", lambda eng: eng.tensor_single_scalar(
            qrow[:], ps_q[:], 1.0 / D, ALU.mult), ev, 1)
        p.wait("vector", ev)
        p.do("vector", lambda eng, sl=sl: eng.tensor_mul(
            trow[:], mu_row[:, sl], mu_row[:, sl]), ev, 1)
        p.wait("vector", ev)
        p.do("vector", lambda eng: eng.tensor_sub(srow[:], qrow[:], trow[:]), ev, 1)
        p.wait("scalar", ev)
        p.do("scalar", lambda eng: eng.activation(trow[:], srow[:], AF.Ln, bias=EPS_LN), ev, 1)
        p.wait("scalar", ev)
        p.do("scalar", lambda eng, sl=sl: eng.activation(
            rs_row[:, sl], trow[:], AF.Exp, scale=-0.5), ev, 1)
    p.wait("gpsimd", ev)
    p.do("gpsimd", lambda eng: eng.dma_start(out=r.d_murow[:], in_=mu_row[:]), st, 16)
    p.do("gpsimd", lambda eng: eng.dma_start(out=r.d_rsrow[:], in_=rs_row[:]), st, 16)
    p.wait("sync", st)

    mupool = Pool(r, "p2", r.sb(scope, [128, LC], F32, 2, "lnmub"))
    rspool = Pool(r, "p3", r.sb(scope, [128, LC], F32, 2, "lnrsb"))
    tpool = Pool(r, "p4", r.sb(scope, [128, LC], F32, 2, "lntb"))
    opool = Pool(r, "p5", r.sb(scope, [128, LC], BF16, 2, "lnob"))
    fpool = Pool(r, "p6", r.sb(scope, [128, LC], F32, 2, "lnfb")) if xn_f32 is not None else None
    for lc in range(NLC):
        sl = slice(LC * lc, LC * (lc + 1))
        mu_bc, mus = mupool.acquire(p, "sync")
        msem, mv = mupool.dma(p, "sync", mus, lambda eng, mu_bc=mu_bc, sl=sl: eng.dma_start(
            out=mu_bc[:], in_=r.d_murow[:, sl].broadcast_to((128, LC))))
        rs_bc, rss = rspool.acquire(p, "sync")
        rsem, rv = rspool.dma(p, "sync", rss, lambda eng, rs_bc=rs_bc, sl=sl: eng.dma_start(
            out=rs_bc[:], in_=r.d_rsrow[:, sl].broadcast_to((128, LC))))
        p.wait("vector", msem, mv)
        p.wait("vector", rsem, rv)
        last = None
        for fc in range(NFC):
            xb, xs = xpool.acquire(p, "sync")
            lsem, lv = xpool.dma(p, "sync", xs, lambda eng, xb=xb, fc=fc, sl=sl: eng.dma_start(
                out=xb[:], in_=src[128 * fc:128 * (fc + 1), sl]))
            tb, ts = tpool.acquire(p, "vector")
            p.wait("vector", lsem, lv)
            v = p.do("vector", lambda eng, xb=xb, tb=tb, mu_bc=mu_bc: eng.tensor_sub(
                tb[:], xb[:], mu_bc[:]), ev, 1)
            xpool.release(xs, ev, v)
            if xn_f32 is not None:
                fb, fs = fpool.acquire(p, "vector")
                p.wait("vector", ev, v)
                v2 = p.do("vector", lambda eng, tb=tb, fb=fb, rs_bc=rs_bc: eng.tensor_mul(
                    fb[:], tb[:], rs_bc[:]), ev, 1)
                tpool.release(ts, ev, v2)
                ob, os_ = opool.acquire(p, "vector")
                p.wait("vector", ev, v2)
                v3 = p.do("vector", lambda eng, fb=fb, ob=ob: eng.tensor_copy(
                    ob[:], fb[:]), ev, 1)
                last = v3
                p.wait("gpsimd", ev, v3)
                s1, w1 = opool.dma(p, "gpsimd", os_, lambda eng, ob=ob, fc=fc, sl=sl: eng.dma_start(
                    out=xn_bf[128 * fc:128 * (fc + 1), sl], in_=ob[:]))
                opool.release(os_, s1, w1)
                s2, w2 = fpool.dma(p, "gpsimd", fs, lambda eng, fb=fb, fc=fc, sl=sl: eng.dma_start(
                    out=xn_f32[128 * fc:128 * (fc + 1), sl], in_=fb[:]))
                fpool.release(fs, s2, w2)
            else:
                ob, os_ = opool.acquire(p, "vector")
                p.wait("vector", ev, v)
                v2 = p.do("vector", lambda eng, tb=tb, ob=ob, rs_bc=rs_bc: eng.tensor_mul(
                    ob[:], tb[:], rs_bc[:]), ev, 1)
                tpool.release(ts, ev, v2)
                last = v2
                p.wait("gpsimd", ev, v2)
                s1, w1 = opool.dma(p, "gpsimd", os_, lambda eng, ob=ob, fc=fc, sl=sl: eng.dma_start(
                    out=xn_bf[128 * fc:128 * (fc + 1), sl], in_=ob[:]))
                opool.release(os_, s1, w1)
        mupool.release(mus, ev, last)
        rspool.release(rss, ev, last)
    for pool in (opool, fpool) if fpool is not None else (opool,):
        for s in pool.sems:
            p.wait("gpsimd", s)



def emit_attn_a(r, scope, x_src, xn_bf, x_dst, stop_stage=None):
    p = r.p
    nc = r.nc
    ld = r.sem("aa_ld")
    mm = r.sem("aa_mm")
    ev = r.sem("aa_ev")
    st = r.sem("aa_st")
    sp = r.sem("aa_sp")

    s1 = scope.enter_context(contextlib.ExitStack())
    WT = r.sb(s1, [128, NFC * 3 * D], BF16, base="aaWT")
    ldq = {q: r.sem("aa_ld_" + q) for q in ("sync", "scalar", "gpsimd")}
    ldq["sync"] = ld
    for m, mat in enumerate((r.d_wqT_bf, r.d_wkT_bf, r.d_wvT_bf)):
        for dc in range(NFC):
            q = ("sync", "scalar", "gpsimd")[(m * NFC + dc) % 3]
            p.do(q, lambda eng, mat=mat, dc=dc, m=m: eng.dma_start(
                out=WT[:, 3 * D * dc + D * m: 3 * D * dc + D * (m + 1)],
                in_=mat[128 * dc:128 * (dc + 1), :]), ldq[q], 16)
    xnpool = Pool(r, "p0", r.sb(s1, [128, D], BF16, 2, "aaxn"))
    NEN = 6
    pss = [r.ps(s1, [128, 512], "aapq") for _ in range(NEN)]
    m_sb = r.sb(s1, [128, 512], F32, base="aam")
    a_sb = r.sb(s1, [128, 512], F32, base="aaa")
    e_sb = r.sb(s1, [128, 512], F32, base="aae")
    l_sb = r.sb(s1, [128, 512], F32, base="aal")
    opool = Pool(r, "p1", r.sb(s1, [128, 512], BF16, 3, "aao"))
    wld = p.wait("tensor", ld)
    for q in ("scalar", "gpsimd"):
        p.wait("tensor", ldq[q])
    for lt in range(L // 128):
        xb, xs = xnpool.acquire(p, "sync")
        xsem, xv = xnpool.dma(p, "sync", xs, lambda eng, xb=xb, lt=lt: eng.dma_start(
            out=xb[:].rearrange("p (c l) -> p c l", c=NFC),
            in_=xn_bf[:, 128 * lt:128 * (lt + 1)].rearrange(
                "(c p) l -> p c l", p=128)))
        p.wait("tensor", xsem, xv)
        if lt > 0:
            p.wait("tensor", ev)
        vmm = None
        for dc in range(NFC):
            for en in range(NEN):
                vmm = p.do("tensor", lambda eng, xb=xb, dc=dc, en=en,
                           f0=(dc == 0), fl=(dc == NFC - 1): eng.matmul(
                    pss[en][:], xb[:, 128 * dc:128 * (dc + 1)],
                    WT[:, 3 * D * dc + 512 * en: 3 * D * dc + 512 * (en + 1)],
                    start=f0, stop=fl), mm, 1)
        xnpool.release(xs, mm, vmm)
        for en in range(NEN):
            dst = (r.d_qtok, r.d_ktok, r.d_vtok)[en // 2]
            csl = slice(512 * (en % 2), 512 * (en % 2 + 1))
            ob, os_ = opool.acquire(p, "vector")
            if en < 4:
                p.wait("scalar", ev)
                p.wait("scalar", mm)
                vab = p.do("scalar", lambda eng, en=en: eng.activation(
                    a_sb[:], pss[en][:], AF.Abs), sp, 1)
                p.wait("vector", ev)
                p.wait("vector", sp, vab)
                p.do("vector", lambda eng, en=en: eng.tensor_scalar_max(
                    m_sb[:], pss[en][:], 0.0), ev, 1)
                p.wait("scalar", sp)
                p.do("scalar", lambda eng: eng.activation(
                    e_sb[:], a_sb[:], AF.Exp, scale=-1.0), sp, 1)
                p.wait("scalar", sp)
                p.do("scalar", lambda eng: eng.activation(
                    l_sb[:], e_sb[:], AF.Ln, bias=1.0), sp, 1)
                p.wait("vector", sp)
                p.wait("vector", ev)
                v = p.do("vector", lambda eng, ob=ob: eng.scalar_tensor_tensor(
                    ob[:], m_sb[:], EPS_ATT, l_sb[:], ALU.add, ALU.add), ev, 1)
            else:
                p.wait("vector", mm)
                v = p.do("vector", lambda eng, ob=ob, en=en: eng.tensor_copy(
                    ob[:], pss[en][:]), ev, 1)
            p.wait("gpsimd", ev, v)
            wsem, wv = opool.dma(p, "gpsimd", os_, lambda eng, ob=ob, dst=dst, lt=lt, csl=csl: eng.dma_start(
                out=dst[128 * lt:128 * (lt + 1), csl], in_=ob[:]))
            opool.release(os_, wsem, wv)
    for s in opool.sems:
        p.wait("gpsimd", s)

    for s in opool.sems:
        p.wait("sync", s)
    if stop_stage == 1:
        s1.close()
        return
    s1.close()
    p.wait("tensor", ev)

    ao_full = r.sb(scope, [128, 32 * D], BF16, base="aaao")
    s2 = scope.enter_context(contextlib.ExitStack())
    qpool = Pool(r, "p2", r.sb(s2, [128, D], BF16, 2, "aaq"))
    kpool = Pool(r, "p3", r.sb(s2, [128, D], BF16, 2, "aak"))
    vpool = Pool(r, "p4", r.sb(s2, [128, D], BF16, 2, "aav"))
    kfT = r.sb(s2, [64, 128], BF16, base="aakfT")
    qfT = r.sb(s2, [64, 128], BF16, base="aaqfT")
    Sb = r.sb(s2, [128, 128], BF16, base="aaSb")
    zi_bf = r.sb(s2, [128, 64], BF16, base="aazi")
    den_sc = r.sb(s2, [128, 64], BF16, base="aadsc")
    den = r.sb(s2, [128, 1], F32, base="aaden")
    den2 = r.sb(s2, [128, 1], F32, base="aaden2")
    rcol = r.sb(s2, [128, 1], F32, base="aar")
    KVbf = r.sb(s2, [64, D], BF16, base="aaKV")
    zrow = r.sb(s2, [1, D], BF16, base="aaz")
    KV32 = r.sb(s2, [64, D], F32, 2, base="aaKV32")
    z32 = r.sb(s2, [1, D], F32, 2, base="aaz32")
    ps_kv = r.ps(s2, [64, D], "aapkv")
    ps_z = [r.ps(s2, [1, 512], "aapz") for _ in range(2)]
    ps_tr = s2.enter_context(nc.psum_tensor(r.name("aaptr"), [64, 256], BF16))
    ps_S = r.ps(s2, [128, 128], "aapS")
    ps_zi = r.ps(s2, [128, 64], "aapzi")
    ps_o = r.ps(s2, [128, 64], "aapo")

    p.do("vector", lambda eng: eng.memset(KVbf[:], 0.0), ev, 1)
    p.do("vector", lambda eng: eng.memset(zrow[:], 0.0), ev, 1)
    p.do("vector", lambda eng: eng.memset(KV32[0][:], 0.0), ev, 1)
    p.do("vector", lambda eng: eng.memset(z32[0][:], 0.0), ev, 1)
    for tc in range(L // 128):
        qb, qs = qpool.acquire(p, "sync")
        qsem, qv = qpool.dma(p, "sync", qs, lambda eng, qb=qb, tc=tc: eng.dma_start(
            out=qb[:], in_=r.d_qtok[128 * tc:128 * (tc + 1), :]))
        kb, ks = kpool.acquire(p, "sync")
        ksem, kv_ = kpool.dma(p, "sync", ks, lambda eng, kb=kb, tc=tc: eng.dma_start(
            out=kb[:], in_=r.d_ktok[128 * tc:128 * (tc + 1), :]))
        vb, vs = vpool.acquire(p, "sync")
        vsem, vv_ = vpool.dma(p, "sync", vs, lambda eng, vb=vb, tc=tc: eng.dma_start(
            out=vb[:], in_=r.d_vtok[128 * tc:128 * (tc + 1), :]))
        p.wait("tensor", qsem, qv)
        p.wait("tensor", ksem, kv_)
        p.wait("tensor", vsem, vv_)
        lastv = None
        for h in range(NH):
            hs = slice(64 * h, 64 * (h + 1))
            p.wait("tensor", ev)
            p.do("tensor", lambda eng, kb=kb, hs=hs: eng.transpose(
                ps_tr[:, 0:128], kb[:, hs], r.ident_bf[:]), mm, 1)
            p.do("tensor", lambda eng, qb=qb, hs=hs: eng.transpose(
                ps_tr[:, 128:256], qb[:, hs], r.ident_bf[:]), mm, 1)
            p.wait("vector", mm)
            p.do("vector", lambda eng: eng.tensor_copy(kfT[:], ps_tr[:, 0:128]), ev, 1)
            vtr = p.do("vector", lambda eng: eng.tensor_copy(qfT[:], ps_tr[:, 128:256]), ev, 1)
            p.wait("tensor", ev, vtr)
            vS = p.do("tensor", lambda eng: eng.matmul(
                ps_S[:], kfT[:], qfT[:], start=True, stop=True), mm, 1)
            p.wait("vector", mm, vS)
            vSb = p.do("vector", lambda eng: eng.tensor_mul(
                Sb[:], ps_S[:], r.mask_f32[:]), ev, 1)
            vzi = p.do("tensor", lambda eng, kb=kb, hs=hs: eng.matmul(
                ps_zi[:], r.mask_bf[:], kb[:, hs], start=True, stop=False), mm, 1)
            vzi = p.do("tensor", lambda eng, hs=hs: eng.matmul(
                ps_zi[:], r.ones_row_bf[:, 0:128], zrow[:, hs], start=False, stop=True), mm, 1)
            p.wait("vector", mm, vzi)
            p.wait("vector", ev)
            vzb = p.do("vector", lambda eng: eng.tensor_copy(zi_bf[:], ps_zi[:]), ev, 1)
            p.wait("vector", ev, vzb)
            p.do("vector", lambda eng, qb=qb, hs=hs: eng.scalar_tensor_tensor(
                den_sc[:], qb[:, hs], 1.0, zi_bf[:], ALU.mult, ALU.mult,
                accum_out=den[:]), ev, 1)
            p.wait("vector", ev)
            p.do("vector", lambda eng: eng.tensor_scalar_add(den2[:], den[:], EPS_ATT), ev, 1)
            p.wait("vector", ev)
            vr = p.do("vector", lambda eng: eng.reciprocal(rcol[:], den2[:]), ev, 1)
            p.wait("tensor", ev, vSb)
            p.do("tensor", lambda eng, vb=vb, hs=hs: eng.matmul(
                ps_o[:], Sb[:], vb[:, hs], start=True, stop=False), mm, 1)
            vo = p.do("tensor", lambda eng, hs=hs: eng.matmul(
                ps_o[:], qfT[:], KVbf[:, hs], start=False, stop=True), mm, 1)
            vkv = p.do("tensor", lambda eng, kb=kb, vb=vb, hs=hs: eng.matmul(
                ps_kv[:, hs], kb[:, hs], vb[:, hs], start=True, stop=True), mm, 1)
            vzu = p.do("tensor", lambda eng, kb=kb, hs=hs, h=h: eng.matmul(
                ps_z[h // 8][:, 64 * (h % 8):64 * (h % 8 + 1)], r.ones_col_bf[:],
                kb[:, hs], start=True, stop=True), mm, 1)
            p.wait("vector", mm, vo)
            p.wait("vector", ev, vr)
            lastv = p.do("vector", lambda eng, tc=tc, h=h: eng.tensor_scalar(
                ao_full[:, 1024 * tc + 64 * h: 1024 * tc + 64 * (h + 1)], ps_o[:],
                rcol[:], None, ALU.mult), ev, 1)
            cur, prv = (tc + 1) % 2, tc % 2
            p.wait("vector", mm, vkv)
            va = p.do("vector", lambda eng, hs=hs, cur=cur, prv=prv: eng.tensor_add(
                KV32[cur][:, hs], KV32[prv][:, hs], ps_kv[:, hs]), ev, 1)
            p.wait("vector", ev, va)
            p.do("vector", lambda eng, hs=hs, cur=cur: eng.tensor_copy(
                KVbf[:, hs], KV32[cur][:, hs]), ev, 1)
            p.wait("vector", mm, vzu)
            vz2 = p.do("vector", lambda eng, hs=hs, h=h, cur=cur, prv=prv: eng.tensor_add(
                z32[cur][:, hs], z32[prv][:, hs],
                ps_z[h // 8][:, 64 * (h % 8):64 * (h % 8 + 1)]), ev, 1)
            p.wait("vector", ev, vz2)
            lastz = p.do("vector", lambda eng, hs=hs, cur=cur: eng.tensor_copy(
                zrow[:, hs], z32[cur][:, hs]), ev, 1)
        qpool.release(qs, ev, lastv)
        kpool.release(ks, ev, ev.n)
        vpool.release(vs, ev, lastv)

    if stop_stage == 2:
        dmp = r.nc.dram_tensor("aodump", [128, 32 * D], BF16, kind="ExternalOutput").ap()
        dsem = r.sem("aadmp")
        p.wait("gpsimd", ev)
        for tcd in range(32):
            p.do("gpsimd", lambda eng, tcd=tcd: eng.dma_start(
                out=dmp[:, 1024 * tcd:1024 * (tcd + 1)],
                in_=ao_full[:, 1024 * tcd:1024 * (tcd + 1)]), dsem, 16)
        p.wait("gpsimd", dsem)
        s2.close()
        return
    s2.close()
    p.wait("tensor", ev)

    aoT = r.sb(scope, [128, NFC * L], BF16, base="aaaoT")
    s3 = scope.enter_context(contextlib.ExitStack())
    ps_t2 = [s3.enter_context(nc.psum_tensor(r.name("aapt2"), [128, 128], BF16))
             for _ in range(2)]
    v_t2ev = [None, None]
    for tc in range(L // 128):
        for fc in range(NFC):
            half = (tc * NFC + fc) % 2
            if v_t2ev[half] is not None:
                p.wait("tensor", ev, v_t2ev[half])
            vt = p.do("tensor", lambda eng, tc=tc, fc=fc, half=half: eng.transpose(
                ps_t2[half][:],
                ao_full[:, 1024 * tc + 128 * fc: 1024 * tc + 128 * (fc + 1)],
                r.ident_bf[:]), mm, 1)
            p.wait("vector", mm, vt)
            v_t2ev[half] = p.do("vector", lambda eng, tc=tc, fc=fc, half=half: eng.tensor_copy(
                aoT[:, L * fc + 128 * tc: L * fc + 128 * (tc + 1)],
                ps_t2[half][:]), ev, 1)
    s3.close()
    p.wait("tensor", ev)

    s4 = scope.enter_context(contextlib.ExitStack())
    WoT = r.sb(s4, [128, NFC * D], BF16, base="aaWo")
    ld5 = r.sem("aa_ld5")
    for dc in range(NFC):
        q, s_ = (("sync", ld), ("gpsimd", ld5))[dc % 2]
        p.do(q, lambda eng, dc=dc: eng.dma_start(
            out=WoT[:, D * dc:D * (dc + 1)],
            in_=r.d_woT_bf[128 * dc:128 * (dc + 1), :]), s_, 16)
    xpool2 = Pool(r, "p5", r.sb(s4, [128, LC], F32, 2, "aax2"))
    rpool = Pool(r, "p6", r.sb(s4, [128, LC], F32, 2, "aares"))
    ps_w = [r.ps(s4, [128, 512], "aapw") for _ in range(2)]
    p.wait("tensor", ld)
    p.wait("tensor", ld5)
    p.wait("tensor", ev)
    k = 0
    v_w4ev = [None, None]
    for ec in range(NFC):
        for lc in range(NLC):
            pw = ps_w[k % 2]
            xb, xs = xpool2.acquire(p, "sync")
            x2sem, x2v = xpool2.dma(p, "sync", xs, lambda eng, xb=xb, ec=ec, lc=lc: eng.dma_start(
                out=xb[:], in_=x_src[128 * ec:128 * (ec + 1), LC * lc:LC * (lc + 1)]))
            if v_w4ev[k % 2] is not None:
                p.wait("tensor", ev, v_w4ev[k % 2])
            vw = None
            for dc in range(NFC):
                vw = p.do("tensor", lambda eng, pw=pw, dc=dc, ec=ec, lc=lc,
                          f0=(dc == 0), fl=(dc == NFC - 1): eng.matmul(
                    pw[:], WoT[:, D * dc + 128 * ec: D * dc + 128 * (ec + 1)],
                    aoT[:, L * dc + LC * lc: L * dc + LC * (lc + 1)],
                    start=f0, stop=fl), mm, 1)
            rb, rs = rpool.acquire(p, "vector")
            p.wait("vector", mm, vw)
            p.wait("vector", x2sem, x2v)
            v = p.do("vector", lambda eng, pw=pw, xb=xb, rb=rb: eng.tensor_add(
                rb[:], pw[:], xb[:]), ev, 1)
            v_w4ev[k % 2] = v
            xpool2.release(xs, ev, v)
            p.wait("gpsimd", ev, v)
            wsem, wv = rpool.dma(p, "gpsimd", rs, lambda eng, rb=rb, ec=ec, lc=lc: eng.dma_start(
                out=x_dst[128 * ec:128 * (ec + 1), LC * lc:LC * (lc + 1)], in_=rb[:]))
            rpool.release(rs, wsem, wv)
            k += 1
    for s in rpool.sems:
        p.wait("gpsimd", s)


def build_nc(stop_after="all", taps=(), skip_cc=0):
    nc = bass.Bass("TRN2", target_bir_lowering=False)
    with contextlib.ExitStack() as stack:
        prog = Prog(nc)
        r = R(nc, prog, stack)
        r.skip_cc = skip_cc
        p = prog

        r.d_xT = nc.dram_tensor("xT", [D, L], F32, kind="ExternalInput").ap()
        r.d_wqT = nc.dram_tensor("wqT", [D, D], F32, kind="ExternalInput").ap()
        r.d_wkT = nc.dram_tensor("wkT", [D, D], F32, kind="ExternalInput").ap()
        r.d_wvT = nc.dram_tensor("wvT", [D, D], F32, kind="ExternalInput").ap()
        r.d_woT = nc.dram_tensor("woT", [D, D], F32, kind="ExternalInput").ap()
        r.d_wg = nc.dram_tensor("wg", [D, NE], F32, kind="ExternalInput").ap()
        r.d_w1 = nc.dram_tensor("w1", [NE * D, FF], F32, kind="ExternalInput").ap()
        r.d_b1 = nc.dram_tensor("b1p", [128, 256], F32, kind="ExternalInput").ap()
        r.d_w2 = nc.dram_tensor("w2", [NE * FF, D], F32, kind="ExternalInput").ap()
        r.d_b2 = nc.dram_tensor("b2m", [NE, D], F32, kind="ExternalInput").ap()
        r.d_memask = nc.dram_tensor("memask", [32, 1], F32, kind="ExternalInput").ap()
        r.d_winT = nc.dram_tensor("winT", [D, D], F32, kind="ExternalInput").ap()
        r.d_binp = nc.dram_tensor("binp", [128, 8], F32, kind="ExternalInput").ap()
        r.d_cvT = nc.dram_tensor("cvT", [3 * D, D], F32, kind="ExternalInput").ap()
        r.d_cbp = nc.dram_tensor("cbp", [128, 8], F32, kind="ExternalInput").ap()
        r.d_alog = nc.dram_tensor("alogd", [D, 16], F32, kind="ExternalInput").ap()
        r.d_Dp = nc.dram_tensor("Dp", [128, 8], F32, kind="ExternalInput").ap()
        r.d_wbcdT = nc.dram_tensor("wbcdT", [D, 96], F32, kind="ExternalInput").ap()
        r.d_bbcd = nc.dram_tensor("bbcd", [96, 1], F32, kind="ExternalInput").ap()
        r.d_wd2T = nc.dram_tensor("wd2T", [64, D], F32, kind="ExternalInput").ap()
        r.d_bd2p = nc.dram_tensor("bd2p", [128, 8], F32, kind="ExternalInput").ap()
        r.d_woutT = nc.dram_tensor("woutT", [D, D], F32, kind="ExternalInput").ap()
        r.d_boutp = nc.dram_tensor("boutp", [128, 8], F32, kind="ExternalInput").ap()

        r.d_out = nc.dram_tensor("outT", [D, L], F32, kind="ExternalOutput").ap()
        r.d_x1 = nc.dram_tensor("x1", [D, L], F32).ap()
        r.d_xnbf = nc.dram_tensor("xnbf", [D, L], BF16).ap()
        r.d_murow = nc.dram_tensor("murow", [1, L], F32).ap()
        r.d_rsrow = nc.dram_tensor("rsrow", [1, L], F32).ap()
        r.d_qtok = nc.dram_tensor("qtok", [L, D], BF16).ap()
        r.d_ktok = nc.dram_tensor("ktok", [L, D], BF16).ap()
        r.d_vtok = nc.dram_tensor("vtok", [L, D], BF16).ap()
        r.d_x2 = nc.dram_tensor("x2", [D, L], F32).ap()
        r.d_xn32 = nc.dram_tensor("xn32", [D, L], F32).ap()
        r.d_w1bf = nc.dram_tensor("w1b", [NE * D, FF], BF16).ap()
        r.d_w2bf = nc.dram_tensor("w2b", [NE * FF, D], BF16).ap()
        r.d_sall = nc.dram_tensor("sall", [NE, L], F32).ap()
        r.d_facc = nc.dram_tensor("facc", [D, L], F32).ap()
        r.d_cnt256 = nc.dram_tensor("cnt256", [1, 256], F32).ap()
        r.d_goffd = nc.dram_tensor("goffd", [1, 8], F32).ap()
        r.d_slots = nc.dram_tensor("slots", [128, 32], mybir.dt.int32).ap()
        r.d_sslot = nc.dram_tensor("sslot", [NE * SLOT, 1], F32).ap()
        r.d_bufx = nc.dram_tensor("bufx", [NE * SLOT, D], BF16).ap()
        r.d_ybuf = nc.dram_tensor("ybuf", [NE * SLOT, D], F32).ap()
        if stop_after == "ffn1_dbg":
            r.dbg_ffn = True
            r.d_h1dump = nc.dram_tensor("h1dump", [128, 32 * LGT], BF16,
                                        kind="ExternalOutput").ap()
        r.d_run256 = nc.dram_tensor("run256", [32, 8], F32).ap()
        r.d_x3 = nc.dram_tensor("x3", [D, L], F32).ap()
        r.d_winT_bf = nc.dram_tensor("winTb", [D, D], BF16).ap()
        r.d_cvT_bf = nc.dram_tensor("cvTb", [3 * D, D], BF16).ap()
        r.d_wbcdT_bf = nc.dram_tensor("wbcdTb", [D, 96], BF16).ap()
        r.d_wd2T_bf = nc.dram_tensor("wd2Tb", [64, D], BF16).ap()
        r.d_woutT_bf = nc.dram_tensor("woutTb", [D, D], BF16).ap()
        r.d_h0f = nc.dram_tensor("h0f", [D, L], F32).ap()
        r.d_h0pb = nc.dram_tensor("h0pb", [D, L + 2], BF16).ap()
        r.d_hcf = nc.dram_tensor("hcf", [D, L], F32).ap()
        r.d_h2f = nc.dram_tensor("h2f", [D, L], F32).ap()
        r.d_h2b = nc.dram_tensor("h2b", [D, L], BF16).ap()
        r.d_gf = nc.dram_tensor("gf", [D, L], F32).ap()
        r.d_gnb = nc.dram_tensor("gnb", [D, L], BF16).ap()
        r.d_bcrows = nc.dram_tensor("bcrows", [32, L], F32).ap()
        r.d_gcells = nc.dram_tensor("gcells", [1, 8], F32).ap()
        r.d_cin1 = nc.dram_tensor("cin1", [4, 8], F32).ap()
        r.d_cout1 = nc.dram_tensor("cout1", [32, 8], F32).ap()
        r.d_cin2 = nc.dram_tensor("cin2", [4, 8], F32).ap()
        r.d_cout2 = nc.dram_tensor("cout2", [32, 8], F32).ap()
        r.d_wqT_bf = nc.dram_tensor("wqTb", [D, D], BF16).ap()
        r.d_wkT_bf = nc.dram_tensor("wkTb", [D, D], BF16).ap()
        r.d_wvT_bf = nc.dram_tensor("wvTb", [D, D], BF16).ap()
        r.d_woT_bf = nc.dram_tensor("woTb", [D, D], BF16).ap()

        tri = np.triu(np.ones((128, 128), np.float32))
        sut = np.triu(np.ones((128, 128), np.float32), 1)
        su32 = np.triu(np.ones((32, 32), np.float32), 1)
        r.c_ident = nc.inline_tensor(np.eye(128, dtype=ml_dtypes.bfloat16), name="idbf").ap()
        r.c_mask_bf = nc.inline_tensor(tri.astype(ml_dtypes.bfloat16), name="maskbf").ap()
        r.c_mask_f32 = nc.inline_tensor(tri, name="maskf").ap()
        r.c_ones_col = nc.inline_tensor(np.ones((128, 1), np.float32), name="ones_c").ap()
        r.c_ones_col_bf = nc.inline_tensor(np.ones((128, 1), ml_dtypes.bfloat16), name="ones_cb").ap()
        r.c_ones_row_bf = nc.inline_tensor(np.ones((1, 512), ml_dtypes.bfloat16), name="ones_rb").ap()
        r.c_ident_f32 = nc.inline_tensor(np.eye(128, dtype=np.float32), name="idf").ap()
        r.c_su_f32 = nc.inline_tensor(sut, name="suf").ap()
        r.c_su32_f32 = nc.inline_tensor(su32, name="su32f").ap()
        r.c_ones_row_f32 = nc.inline_tensor(np.ones((1, 512), np.float32), name="ones_rf").ap()
        r.c_iotaE = nc.inline_tensor(
            np.tile(np.arange(8, dtype=np.float32), (128, 1)), name="iotaE").ap()

        gscope = stack
        r.ident_bf = r.sb(gscope, [128, 128], BF16, base="identbf")
        r.mask_bf = r.sb(gscope, [128, 128], BF16, base="maskbf")
        r.mask_f32 = r.sb(gscope, [128, 128], F32, base="maskf")
        r.ones_col = r.sb(gscope, [128, 1], F32, base="onesc")
        r.ones_col_bf = r.sb(gscope, [128, 1], BF16, base="onescb")
        r.ones_row_bf = r.sb(gscope, [1, 512], BF16, base="onesrb")
        r.ident_f32 = r.sb(gscope, [128, 128], F32, base="identf")
        r.su_f32 = r.sb(gscope, [128, 128], F32, base="suf")
        r.su32_f32 = r.sb(gscope, [32, 32], F32, base="su32f")
        r.ones_row_f32 = r.sb(gscope, [1, 512], F32, base="onesrf")
        r.iotaE = r.sb(gscope, [128, 8], F32, base="iotaE")
        cl = r.sem("const_ld")
        for dst, srcc in ((r.ident_bf, r.c_ident), (r.mask_bf, r.c_mask_bf),
                          (r.mask_f32, r.c_mask_f32), (r.ones_col, r.c_ones_col),
                          (r.ones_col_bf, r.c_ones_col_bf), (r.ones_row_bf, r.c_ones_row_bf),
                          (r.ident_f32, r.c_ident_f32), (r.su_f32, r.c_su_f32),
                          (r.su32_f32, r.c_su32_f32), (r.ones_row_f32, r.c_ones_row_f32),
                          (r.iotaE, r.c_iotaE)):
            p.do("sync", lambda eng, d=dst, s=srcc: eng.dma_start(out=d[:], in_=s[:]), cl, 16)
        p.wait("sync", cl)

        cvals = [EPS_LN, 1.0]
        carr = np.array(cvals, np.float32)[None, :].repeat(128, 0)
        c_biases = nc.inline_tensor(np.ascontiguousarray(carr), name="biasconsts").ap()
        bias_sb = r.sb(gscope, [128, len(cvals)], F32, base="biasc")
        p.do("sync", lambda eng: eng.dma_start(out=bias_sb[:], in_=c_biases[:]), cl, 16)
        p.wait("sync", cl)
        for i, v in enumerate(cvals):
            nc.const_aps.register(F32, v, bias_sb[:, i:i + 1])

        bar = r.sem("bar")

        with contextlib.ExitStack() as sc:
            pairs = [
                (r.d_wqT, r.d_wqT_bf, D, D), (r.d_wkT, r.d_wkT_bf, D, D),
                (r.d_wvT, r.d_wvT_bf, D, D), (r.d_woT, r.d_woT_bf, D, D),
            ]
            if stop_after in ("attn_b", "all"):
                pairs += [(r.d_winT, r.d_winT_bf, D, D), (r.d_cvT, r.d_cvT_bf, 3 * D, D),
                          (r.d_wbcdT, r.d_wbcdT_bf, D, 96), (r.d_wd2T, r.d_wd2T_bf, 64, D),
                          (r.d_woutT, r.d_woutT_bf, D, D)]

            emit_cast(r, sc, pairs)
            if stop_after in ("ffn1", "ffn1_dbg", "attn_b", "all"):
                zi = r.sem("zinit")
                zt = r.sb(sc, [128, D], BF16, base="zinit")
                ztf = r.sb(sc, [128, NE * SLOT // 128], F32, base="zinitf")
                p.do("vector", lambda eng: eng.memset(zt[:], 0.0), zi, 1)
                p.do("vector", lambda eng: eng.memset(ztf[:], 0.0), zi, 1)
                p.wait("gpsimd", zi)
                for i_ in range(NE * SLOT // 128):
                    p.do("gpsimd", lambda eng, i_=i_: eng.dma_start(
                        out=r.d_bufx[128 * i_:128 * (i_ + 1), :], in_=zt[:]), zi, 16)
                p.do("gpsimd", lambda eng: eng.dma_start(
                    out=r.d_sslot.rearrange("(o s) x -> o (s x)", o=128), in_=ztf[:]), zi, 16)
                p.wait("gpsimd", zi)
            p.barrier(bar)
        if stop_after != "cast":
            with contextlib.ExitStack() as sc:
                emit_ln(r, sc, r.d_xT, r.d_xnbf)
                p.barrier(bar)
        if stop_after not in ("cast", "ln"):
            dst_a = r.d_out if stop_after in ("attn_a", "attn_a_s2", "attn_a_s1") else r.d_x1
            with contextlib.ExitStack() as sc:
                emit_attn_a(r, sc, r.d_xT, r.d_xnbf, dst_a,
                            stop_stage={"attn_a_s2": 2, "attn_a_s1": 1}.get(stop_after))
                p.barrier(bar)
        if stop_after not in ("cast", "ln", "attn_a", "attn_a_s1", "attn_a_s2"):
            dst_f = r.d_out if stop_after in ("ffn1", "ffn1_dbg") else r.d_x2
            with contextlib.ExitStack() as sc:
                emit_ffn_gating(r, sc, r.d_x1, 1)
            with contextlib.ExitStack() as sc:
                emit_ffn_experts_compact(r, sc, r.d_x1, dst_f)
                p.barrier(bar)
        if stop_after in ("attn_b", "all"):
            dst_b = r.d_out if stop_after == "attn_b" else r.d_x3
            with contextlib.ExitStack() as sc:
                emit_attn_b(r, sc, r.d_x2, dst_b)
        if stop_after == "all":
            with contextlib.ExitStack() as sc:
                emit_ffn_gating(r, sc, r.d_x3, 2)
            with contextlib.ExitStack() as sc:
                emit_ffn_experts_compact(r, sc, r.d_x3, r.d_out)
                p.barrier(bar)

        if taps:
            tsem = r.sem("tap")
            for i, name in enumerate(taps):
                srcd = getattr(r, "d_" + name)
                shp = [srcd.shape[0], srcd.shape[1]]
                dt = srcd.dtype
                dout = nc.dram_tensor(f"tap_{name}", shp, dt, kind="ExternalOutput").ap()
                p.do("sync", lambda eng, dout=dout, srcd=srcd: eng.dma_start(
                    out=dout[:], in_=srcd[:]), tsem, 16)
            p.wait("sync", tsem)

        with nc.Block() as block:
            prog.replay(block)
    return nc



LGT = 1024


def emit_ffn_gating(r, scope, x_src, call_idx):
    p = r.p
    nc = r.nc
    mm = r.sem("ff_mm")
    ev = r.sem("ff_ev")
    sq = r.sem("ff_sq")
    cc = r.sem("ff_cc")
    st = r.sem("ff_st")
    gld = r.sem("ff_gld")

    cin = r.d_cin1
    cout = r.d_cout1
    d_sall = r.d_sall

    with contextlib.ExitStack() as sln:
        emit_ln(r, sln, x_src, r.d_xnbf, r.d_xn32)
    p.barrier(r.sems["bar"])

    sg = scope.enter_context(contextlib.ExitStack())
    wg_sb = r.sb(sg, [128, NFC * NE], F32, base="fgwg")
    xnpool = Pool(r, "p0", r.sb(sg, [128, D], F32, 2, "fgxn"))
    lg_sb = r.sb(sg, [8, 128], F32, base="fglg")
    lgtok = r.sb(sg, [128, 8], F32, base="fglt")
    negm = r.sb(sg, [128, 1], F32, base="fgnm")
    ex = r.sb(sg, [128, 8], F32, base="fgex")
    sumex = r.sb(sg, [128, 1], F32, base="fgse")
    rsum = r.sb(sg, [128, 1], F32, base="fgrs")
    ismax = r.sb(sg, [128, 8], F32, base="fgim")
    c1 = r.sb(sg, [128, 8], F32, base="fgc1")
    c2 = r.sb(sg, [128, 8], F32, base="fgc2")
    onehot = r.sb(sg, [128, 8], F32, base="fgoh")
    oh_bf = r.sb(sg, [128, 8], BF16, base="fgohb")
    exo = r.sb(sg, [128, 1], F32, base="fgexo")
    gown = r.sb(sg, [128, 1], F32, base="fggo")
    scr8 = r.sb(sg, [128, 8], F32, base="fgscr")
    rank_own = r.sb(sg, [128, 1], F32, base="fgro")
    kept = r.sb(sg, [128, 1], F32, base="fgkp")
    scolp = Pool(r, "p8", r.sb(sg, [128, 1], F32, 3, "fgsc"))
    smat = r.sb(sg, [128, 8], F32, base="fgsm")
    oh_all = r.sb(sg, [128, 32 * 8], BF16, base="fgoha")
    ohf_all = r.sb(sg, [128, 32 * 8], F32, base="fgohf")
    gown_all = r.sb(sg, [128, 32], F32, base="fggoa")
    s_all_sb = r.sb(sg, [8, L], F32, base="fgsa")
    cnt_sb = r.sb(sg, [1, 256], F32, base="fgcnt")
    cnt_mat = r.sb(sg, [32, 8], F32, base="fgcm")
    tot_sb = r.sb(sg, [4, 8], F32, base="fgtot")
    cnt_all = r.sb(sg, [32, 8], F32, base="fgca")
    mask_me = r.sb(sg, [32, 1], F32, base="fgmm")
    masked = r.sb(sg, [32, 8], F32, base="fgmk")
    goff_sb = r.sb(sg, [1, 8], F32, base="fggf")
    run_sb = r.sb(sg, [32, 8], F32, base="fgrn")
    runrow = r.sb(sg, [1, 256], F32, base="fgrr")
    ps_lg = r.ps(sg, [8, 128], "fgplg")
    ps_tr = r.ps(sg, [128, 8], "fgptr")
    ps_cnt = r.ps(sg, [1, 256], "fgpc")
    ps_sm8 = r.ps(sg, [8, 128], "fgpsm")
    ps_tot = r.ps(sg, [1, 8], "fgpt")
    ps_run = r.ps(sg, [32, 8], "fgprn")
    ps_r = [r.ps(sg, [128, 8], "fgpr") for _ in range(2)]
    goff_bc = r.sb(sg, [128, 8], F32, base="fggb")
    own_goff = r.sb(sg, [128, 1], F32, base="fgog")
    lrank = r.sb(sg, [128, 1], F32, base="fglr")
    eidc = r.sb(sg, [128, 1], F32, base="fgei")
    slotf = r.sb(sg, [128, 1], F32, base="fgsf")
    slots_sb = r.sb(sg, [128, 32], mybir.dt.int32, base="fgsl")

    p.do("sync", lambda eng: eng.dma_start(
        out=wg_sb[:].rearrange("p (c e) -> p c e", c=NFC),
        in_=r.d_wg.rearrange("(c p) e -> p c e", p=128)), gld, 16)
    p.do("sync", lambda eng: eng.dma_start(out=mask_me[:], in_=r.d_memask[:]), gld, 16)
    wgv = p.wait("tensor", gld)

    for lt in range(32):
        xb, xs = xnpool.acquire(p, "sync")
        xsem, xv = xnpool.dma(p, "sync", xs, lambda eng, xb=xb, lt=lt: eng.dma_start(
            out=xb[:].rearrange("p (c l) -> p c l", c=NFC),
            in_=r.d_xn32[:, 128 * lt:128 * (lt + 1)].rearrange("(c p) l -> p c l", p=128)))
        p.wait("tensor", xsem, xv)
        if lt > 0:
            p.wait("tensor", ev)
        for fc in range(NFC):
            p.do("tensor", lambda eng, xb=xb, fc=fc, f0=(fc == 0), fl=(fc == NFC - 1): eng.matmul(
                ps_lg[:], wg_sb[:, 8 * fc:8 * (fc + 1)], xb[:, 128 * fc:128 * (fc + 1)],
                start=f0, stop=fl), mm, 1)
        p.wait("vector", mm)
        vlg = p.do("vector", lambda eng: eng.tensor_copy(lg_sb[:], ps_lg[:]), ev, 1)
        xnpool.release(xs, ev, vlg)
        p.wait("tensor", ev, vlg)
        vtr = p.do("tensor", lambda eng: eng.transpose(
            ps_tr[:], lg_sb[:], r.ident_f32[0:8, 0:8]), mm, 1)
        p.wait("vector", mm, vtr)
        p.do("vector", lambda eng: eng.tensor_copy(lgtok[:], ps_tr[:]), ev, 1)
        p.wait("vector", ev)
        p.do("vector", lambda eng: eng.reduce_max(
            negm[:], lgtok[:], AX.X, negate=True), ev, 1)
        p.wait("scalar", ev)
        p.do("scalar", lambda eng: eng.activation(
            ex[:], lgtok[:], AF.Exp, bias=negm[:], accum_out=sumex[:]), sq, 1)
        p.wait("vector", ev)
        p.do("vector", lambda eng: eng.tensor_scalar(
            ismax[:], lgtok[:], negm[:], 0.0, ALU.add, ALU.is_ge), ev, 1)
        p.wait("vector", ev)
        p.do("vector", lambda eng: eng.tensor_copy(c1[:, 0:1], ismax[:, 0:1]), ev, 1)
        p.do("vector", lambda eng: eng.tensor_add(
            c1[:, 1:8], ismax[:, 1:8], ismax[:, 0:7]), ev, 1)
        p.wait("vector", ev)
        p.do("vector", lambda eng: eng.tensor_copy(c2[:, 0:2], c1[:, 0:2]), ev, 1)
        p.do("vector", lambda eng: eng.tensor_add(c2[:, 2:8], c1[:, 2:8], c1[:, 0:6]), ev, 1)
        p.wait("vector", ev)
        p.do("vector", lambda eng: eng.tensor_copy(c1[:, 0:4], c2[:, 0:4]), ev, 1)
        p.do("vector", lambda eng: eng.tensor_add(c1[:, 4:8], c2[:, 4:8], c2[:, 0:4]), ev, 1)
        p.wait("vector", ev)
        p.do("vector", lambda eng: eng.tensor_single_scalar(
            c2[:], c1[:], 1.0, ALU.is_equal), ev, 1)
        p.wait("vector", ev)
        p.do("vector", lambda eng, lt=lt: eng.tensor_mul(onehot[:], ismax[:], c2[:]), ev, 1)
        p.wait("vector", ev)
        p.do("vector", lambda eng, lt=lt: eng.tensor_copy(
            oh_all[:, 8 * lt:8 * (lt + 1)], onehot[:]), ev, 1)
        p.do("vector", lambda eng, lt=lt: eng.tensor_copy(
            ohf_all[:, 8 * lt:8 * (lt + 1)], onehot[:]), ev, 1)
        p.do("vector", lambda eng: eng.tensor_copy(oh_bf[:], onehot[:]), ev, 1)
        p.wait("vector", sq)
        p.do("vector", lambda eng: eng.scalar_tensor_tensor(
            scr8[:], ex[:], 1.0, onehot[:], ALU.mult, ALU.mult, accum_out=exo[:]), ev, 1)
        p.do("vector", lambda eng: eng.reciprocal(rsum[:], sumex[:]), ev, 1)
        p.wait("vector", ev)
        vgo = p.do("vector", lambda eng, lt=lt: eng.tensor_mul(
            gown_all[:, lt:lt + 1], exo[:], rsum[:]), ev, 1)
        p.wait("tensor", ev, vgo)
        p.do("tensor", lambda eng, lt=lt: eng.matmul(
            ps_cnt[:, 8 * lt:8 * (lt + 1)], r.ones_col_bf[:], oh_bf[:],
            start=True, stop=True), mm, 1)
    p.wait("vector", mm)
    vc = p.do("vector", lambda eng: eng.tensor_copy(cnt_sb[:], ps_cnt[:]), ev, 1)
    p.wait("gpsimd", ev, vc)
    p.do("gpsimd", lambda eng: eng.dma_start(out=r.d_cnt256[:], in_=cnt_sb[:]), st, 16)
    p.wait("sync", st)
    p.do("sync", lambda eng: eng.dma_start(
        out=cnt_mat[:], in_=r.d_cnt256.rearrange("o (t e) -> (o t) e", e=8)), gld, 16)
    p.wait("tensor", gld)
    p.do("tensor", lambda eng: eng.matmul(
        ps_tot[:], r.ones_col[0:32, :], cnt_mat[:], start=True, stop=True), mm, 1)
    p.wait("vector", mm)
    p.do("vector", lambda eng: eng.memset(tot_sb[:], 0.0), ev, 1)
    p.wait("vector", ev)
    vt = p.do("vector", lambda eng: eng.tensor_copy(tot_sb[0:1, :], ps_tot[:]), ev, 1)
    p.wait("gpsimd", ev)
    p.do("gpsimd", lambda eng: eng.dma_start(out=cin[:], in_=tot_sb[:]), st, 16)
    p.wait("gpsimd", st)
    if getattr(r, "skip_cc", 0) == call_idx:
        p.do("gpsimd", lambda eng: eng.dma_start(out=cout[0:4, :], in_=cin[:]), st, 16)
        p.wait("gpsimd", st)
    else:
        p.do("gpsimd", lambda eng: eng.collective_compute(
            "AllGather", ALU.bypass, replica_groups=[list(range(NCORE))],
            ins=[cin.opt()], outs=[cout.opt()]), cc, 1)
        p.wait("gpsimd", cc)
    p.do("gpsimd", lambda eng: eng.dma_start(out=cnt_all[:], in_=cout[:]), st, 16)
    p.wait("vector", st)
    p.do("vector", lambda eng: eng.tensor_scalar(
        masked[:], cnt_all[:], mask_me[:], None, ALU.mult), ev, 1)
    p.wait("tensor", ev)
    p.do("tensor", lambda eng: eng.matmul(
        ps_tot[:], r.ones_col[0:32, :], masked[:], start=True, stop=True), mm, 1)
    p.wait("vector", mm)
    p.do("vector", lambda eng: eng.tensor_copy(goff_sb[:], ps_tot[:]), ev, 1)
    p.wait("tensor", ev)
    p.do("tensor", lambda eng: eng.matmul(
        ps_run[:], r.su32_f32[:], cnt_mat[:], start=True, stop=False), mm, 1)
    p.do("tensor", lambda eng: eng.matmul(
        ps_run[:], r.ones_row_f32[:, 0:32], goff_sb[:], start=False, stop=True), mm, 1)
    p.wait("vector", mm)
    vr = p.do("vector", lambda eng: eng.tensor_copy(run_sb[:], ps_run[:]), ev, 1)
    vgf = p.do("vector", lambda eng: eng.tensor_copy(tot_sb[0:1, :], goff_sb[:]), ev, 1)
    p.wait("gpsimd", ev, vr)
    p.do("gpsimd", lambda eng: eng.dma_start(out=r.d_run256[:], in_=run_sb[:]), st, 16)
    p.do("gpsimd", lambda eng: eng.dma_start(out=r.d_goffd[:], in_=tot_sb[0:1, :]), st, 16)
    p.wait("sync", st)
    p.do("sync", lambda eng: eng.dma_start(
        out=goff_bc[:], in_=r.d_goffd.broadcast_to((128, 8))), gld, 16)
    p.do("sync", lambda eng: eng.dma_start(
        out=runrow[:], in_=r.d_run256.rearrange("(o t) e -> o (t e)", o=1)), gld, 16)
    rrv = p.wait("tensor", gld)

    for lt in range(32):
        pr = ps_r[lt % 2]
        if lt >= 2:
            p.wait("tensor", ev)
        p.do("tensor", lambda eng, pr=pr, lt=lt: eng.matmul(
            pr[:], r.su_f32[:], ohf_all[:, 8 * lt:8 * (lt + 1)], start=True, stop=False), mm, 1)
        vrk = p.do("tensor", lambda eng, pr=pr, lt=lt: eng.matmul(
            pr[:], r.ones_row_f32[:, 0:128], runrow[:, 8 * lt:8 * (lt + 1)],
            start=False, stop=True), mm, 1)
        p.wait("vector", mm, vrk)
        p.do("vector", lambda eng, pr=pr, lt=lt: eng.scalar_tensor_tensor(
            scr8[:], ohf_all[:, 8 * lt:8 * (lt + 1)], 1.0, pr[:], ALU.mult, ALU.mult,
            accum_out=rank_own[:]), ev, 1)
        p.wait("vector", ev)
        p.do("vector", lambda eng: eng.tensor_single_scalar(
            kept[:], rank_own[:], float(CAP), ALU.is_lt), ev, 1)
        p.wait("vector", ev)
        s_col, scs = scolp.acquire(p, "vector")
        p.do("vector", lambda eng, lt=lt, s_col=s_col: eng.tensor_mul(
            s_col[:], gown_all[:, lt:lt + 1], kept[:]), ev, 1)
        p.wait("vector", ev)
        vsm = p.do("vector", lambda eng, lt=lt, s_col=s_col: eng.tensor_scalar(
            smat[:], ohf_all[:, 8 * lt:8 * (lt + 1)], s_col[:], None, ALU.mult), ev, 1)
        scolp.release(scs, ev, vsm)
        p.wait("vector", gld)
        p.do("vector", lambda eng, lt=lt: eng.scalar_tensor_tensor(
            scr8[:], ohf_all[:, 8 * lt:8 * (lt + 1)], 1.0, goff_bc[:], ALU.mult,
            ALU.mult, accum_out=own_goff[:]), ev, 1)
        p.do("vector", lambda eng, lt=lt: eng.scalar_tensor_tensor(
            scr8[:], ohf_all[:, 8 * lt:8 * (lt + 1)], 1.0, r.iotaE[:], ALU.mult,
            ALU.mult, accum_out=eidc[:]), ev, 1)
        p.wait("vector", ev)
        p.do("vector", lambda eng: eng.tensor_sub(lrank[:], rank_own[:], own_goff[:]), ev, 1)
        p.wait("vector", ev)
        p.do("vector", lambda eng: eng.tensor_scalar(
            slotf[:], eidc[:], float(SLOT), None, ALU.mult), ev, 1)
        p.wait("vector", ev)
        p.do("vector", lambda eng: eng.tensor_add(slotf[:], slotf[:], lrank[:]), ev, 1)
        p.wait("vector", ev)
        vsl = p.do("vector", lambda eng, lt=lt: eng.tensor_copy(
            slots_sb[:, lt:lt + 1], slotf[:]), ev, 1)
        p.wait("gpsimd", ev, vsl)
        scsem, scv = scolp.dma(p, "gpsimd", scs, lambda eng, lt=lt, s_col=s_col: eng.indirect_dma_start(
            r.d_sslot[:], IndirectOffsetOnAxis(slots_sb[:, lt:lt + 1], 0),
            s_col[:], None, oob_is_err=False))
        scolp.release(scs, scsem, scv)
        p.wait("tensor", ev, vsm)
        vtr2 = p.do("tensor", lambda eng, lt=lt: eng.transpose(
            ps_sm8[:], smat[:], r.ident_f32[:]), mm, 1)
        p.wait("vector", mm, vtr2)
        p.do("vector", lambda eng, lt=lt: eng.tensor_copy(
            s_all_sb[:, 128 * lt:128 * (lt + 1)], ps_sm8[:]), ev, 1)
    p.wait("gpsimd", ev)
    p.do("gpsimd", lambda eng: eng.dma_start(out=d_sall[:], in_=s_all_sb[:]), st, 16)
    p.do("gpsimd", lambda eng: eng.dma_start(out=r.d_slots[:], in_=slots_sb[:]), st, 16)
    p.wait("sync", st)
    p.wait("gpsimd", st)
    sg.close()
    p.barrier(r.sems["bar"])

    ssc = scope.enter_context(contextlib.ExitStack())
    xnp2 = Pool(r, "p1", r.sb(ssc, [128, NFC * 128], BF16, 2, "fsxn"))
    tokp = Pool(r, "p2", r.sb(ssc, [128, D], BF16, 2, "fstok"))
    slots2 = r.sb(ssc, [128, 32], mybir.dt.int32, base="fssl")
    ps_tb = [ssc.enter_context(nc.psum_tensor(r.name("fsptb"), [128, 128], BF16))
             for _ in range(2)]
    p.do("sync", lambda eng: eng.dma_start(out=slots2[:], in_=r.d_slots[:]), gld, 16)
    p.wait("gpsimd", gld)
    for lt in range(32):
        xb, xs = xnp2.acquire(p, "sync")
        xsem, xv = xnp2.dma(p, "sync", xs, lambda eng, xb=xb, lt=lt: eng.dma_start(
            out=xb[:].rearrange("p (c l) -> p c l", c=NFC),
            in_=r.d_xnbf[:, 128 * lt:128 * (lt + 1)].rearrange("(c p) l -> p c l", p=128)))
        tb, ts_ = tokp.acquire(p, "tensor")
        p.wait("tensor", xsem, xv)
        lastv = None
        for fc in range(NFC):
            pt = ps_tb[fc % 2]
            if fc >= 2 or lt > 0:
                p.wait("tensor", ev)
            vt = p.do("tensor", lambda eng, xb=xb, fc=fc, pt=pt: eng.transpose(
                pt[:], xb[:, 128 * fc:128 * (fc + 1)], r.ident_bf[:]), mm, 1)
            p.wait("vector", mm, vt)
            lastv = p.do("vector", lambda eng, tb=tb, fc=fc, pt=pt: eng.tensor_copy(
                tb[:, 128 * fc:128 * (fc + 1)], pt[:]), ev, 1)
        xnp2.release(xs, mm, mm.n)
        p.wait("gpsimd", ev, lastv)
        ssem, sv = tokp.dma(p, "gpsimd", ts_, lambda eng, tb=tb, lt=lt: eng.indirect_dma_start(
            r.d_bufx[:], IndirectOffsetOnAxis(slots2[:, lt:lt + 1], 0),
            tb[:], None, oob_is_err=False))
        tokp.release(ts_, ssem, sv)
    for s in tokp.sems:
        p.wait("sync", s)
        p.wait("gpsimd", s)
    ssc.close()
    p.barrier(r.sems["bar"])

def emit_ffn_experts(r, scope, x_src, x_dst):
    p = r.p
    nc = r.nc
    mm = r.sem("fe_mm")
    ev = r.sem("fe_ev")
    st = r.sem("fe_st")
    gld = r.sem("fe_gld")

    se = scope.enter_context(contextlib.ExitStack())
    xnpool = Pool(r, "p3", r.sb(se, [128, NFC * LGT], BF16, 2, "fexn"))
    h1_big = r.sb(se, [128, 32 * LGT], BF16, base="feh1")
    w2e = r.sb(se, [128, 32 * D], BF16, base="few2")
    w1pool = Pool(r, "p0", r.sb(se, [128, NFC * 128], BF16, 3, "few1"))
    sepool = Pool(r, "p1", r.sb(se, [128, LGT], F32, 2, "fese"))
    b1_sb = r.sb(se, [128, 256], F32, base="feb1")
    relu_t = r.sb(se, [128, LGT], F32, base="fert")
    b2_sb = r.sb(se, [8, D], F32, base="feb2")
    s8 = r.sb(se, [8, 512], F32, base="fes8")
    opool = Pool(r, "p2", r.sb(se, [128, LGT], F32, 3, "feo"))
    ps_h = [r.ps(se, [128, LGT], "feph") for _ in range(2)]
    ps_o = [r.ps(se, [128, LGT], "fepo") for _ in range(2)]

    ld0 = r.sem("fe_ld0")
    p.do("sync", lambda eng: eng.dma_start(out=b1_sb[:], in_=r.d_b1[:]), ld0, 16)
    p.do("sync", lambda eng: eng.dma_start(out=b2_sb[:], in_=r.d_b2[:]), ld0, 16)
    p.wait("tensor", ld0)
    p.wait("vector", ld0)

    w2sem = r.sem("fe_w2s")
    w2free = None
    accchain = {}

    for lc in range(NLC):
        sl = slice(LC * lc, LC * (lc + 1))
        p.do("sync", lambda eng, sl=sl: eng.dma_start(
            out=s8[:], in_=r.d_sall[:, sl]), ld0, 16)
        sv8 = p.wait("tensor", ld0)
        for oc in range(NFC):
            po = ps_o[oc % 2]
            if lc > 0 or oc >= 2:
                p.wait("tensor", ev)
            vb = p.do("tensor", lambda eng, po=po, oc=oc: eng.matmul(
                po[:, 0:LC], b2_sb[:, 128 * oc:128 * (oc + 1)], s8[:],
                start=True, stop=True), mm, 1)
            ob, os_ = opool.acquire(p, "vector")
            p.wait("vector", mm, vb)
            vcp = p.do("vector", lambda eng, po=po, ob=ob: eng.tensor_copy(
                ob[:, 0:LC], po[:, 0:LC]), ev, 1)
            p.wait("gpsimd", ev, vcp)
            ssem, sv = opool.dma(p, "gpsimd", os_, lambda eng, ob=ob, oc=oc, sl=sl: eng.dma_start(
                out=r.d_facc[128 * oc:128 * (oc + 1), sl], in_=ob[:, 0:LC]))
            opool.release(os_, ssem, sv)
            accchain[(lc, oc)] = (ssem, sv)

    NLG = L // LGT
    for lg in range(NLG):
        lsl = slice(LGT * lg, LGT * (lg + 1))
        xn_lg, xns = xnpool.acquire(p, "sync")
        xnsem, xnv = xnpool.dma(p, "sync", xns, lambda eng, lsl=lsl, xn_lg=xn_lg: eng.dma_start(
            out=xn_lg[:].rearrange("p (c l) -> p c l", c=NFC),
            in_=r.d_xnbf[:, lsl].rearrange("(c p) l -> p c l", p=128)))
        p.wait("tensor", xnsem, xnv)
        last_hidden_mm = None
        for e in range(NE):
            if w2free is not None:
                p.wait("sync", w2free[0], w2free[1])
            p.do("sync", lambda eng, e=e: eng.dma_start(
                out=w2e[:].rearrange("p (c o) -> p c o", c=32),
                in_=r.d_w2bf[FF * e:FF * (e + 1), :].rearrange(
                    "(c p) o -> p c o", p=128)), w2sem, 16)
            w2v = w2sem.n
            seb, ses = sepool.acquire(p, "sync")
            sesem, sev = sepool.dma(p, "sync", ses, lambda eng, seb=seb, e=e, lsl=lsl: eng.dma_start(
                out=seb[:], in_=r.d_sall[e:e + 1, lsl].broadcast_to((128, LGT))))
            for fo in range(32):
                w1b, w1s = w1pool.acquire(p, "sync")
                w1sem, w1v = w1pool.dma(p, "sync", w1s, lambda eng, w1b=w1b, e=e, fo=fo: eng.dma_start(
                    out=w1b[:].rearrange("p (c f) -> p c f", c=NFC),
                    in_=r.d_w1bf[D * e:D * (e + 1), 128 * fo:128 * (fo + 1)].rearrange(
                        "(c p) f -> p c f", p=128)))
                ph = ps_h[fo % 2]
                p.wait("tensor", w1sem, w1v)
                if fo >= 2 or e > 0 or lg > 0:
                    p.wait("tensor", ev)
                vmm = None
                for half in range(2):
                    for dk in range(NFC):
                        vmm = p.do("tensor", lambda eng, ph=ph, w1b=w1b, dk=dk, half=half,
                                   xn_lg=xn_lg,
                                   f0=(dk == 0), fl=(dk == NFC - 1): eng.matmul(
                            ph[:, 512 * half:512 * (half + 1)],
                            w1b[:, 128 * dk:128 * (dk + 1)],
                            xn_lg[:, LGT * dk + 512 * half:LGT * dk + 512 * half + 512],
                            start=f0, stop=fl), mm, 1)
                last_hidden_mm = vmm
                w1pool.release(w1s, mm, vmm)
                p.wait("vector", mm, vmm)
                p.wait("vector", sesem, sev)
                p.do("vector", lambda eng, ph=ph, e=e, fo=fo: eng.tensor_scalar(
                    relu_t[:], ph[:], b1_sb[:, 32 * e + fo:32 * e + fo + 1], 0.0,
                    ALU.add, ALU.max), ev, 1)
                p.wait("vector", ev)
                p.do("vector", lambda eng, fo=fo, seb=seb: eng.tensor_mul(
                    h1_big[:, LGT * fo:LGT * (fo + 1)], relu_t[:], seb[:]), ev, 1)
            sepool.release(ses, ev, ev.n)
            if getattr(r, "dbg_ffn", False) and lg == 0 and e == 0:
                dsem = r.sem("fedbg")
                p.wait("gpsimd", ev)
                p.do("gpsimd", lambda eng: eng.dma_start(
                    out=r.d_h1dump[:], in_=h1_big[:]), dsem, 16)
                p.wait("gpsimd", dsem)
            p.wait("tensor", w2sem, w2v)
            p.wait("tensor", ev)
            for oc in range(NE):
                po = ps_o[oc % 2]
                if oc >= 2:
                    p.wait("tensor", ev)
                vmm = None
                for half in range(2):
                    for fk in range(32):
                        vmm = p.do("tensor", lambda eng, po=po, fk=fk, oc=oc, half=half,
                                   f0=(fk == 0), fl=(fk == 31): eng.matmul(
                            po[:, 512 * half:512 * (half + 1)],
                            w2e[:, D * fk + 128 * oc: D * fk + 128 * (oc + 1)],
                            h1_big[:, LGT * fk + 512 * half:LGT * fk + 512 * half + 512],
                            start=f0, stop=fl), mm, 1)
                if oc == NE - 1:
                    w2free = (mm, vmm)
                ob, os_ = opool.acquire(p, "vector")
                p.wait("vector", mm, vmm)
                vcp = p.do("vector", lambda eng, po=po, ob=ob: eng.tensor_copy(
                    ob[:], po[:]), ev, 1)
                p.wait("gpsimd", ev, vcp)
                for key in ((2 * lg, oc), (2 * lg + 1, oc)):
                    if key in accchain:
                        asem, av = accchain[key]
                        p.wait("gpsimd", asem, av)
                ssem, sv = opool.dma(p, "gpsimd", os_, lambda eng, ob=ob, oc=oc,
                                     lsl=lsl: eng.dma_start(
                    out=r.d_facc[128 * oc:128 * (oc + 1), lsl], in_=ob[:],
                    accum_op=ALU.add))
                opool.release(os_, ssem, sv)
                accchain[(2 * lg, oc)] = (ssem, sv)
                accchain[(2 * lg + 1, oc)] = (ssem, sv)
        xnpool.release(xns, mm, last_hidden_mm)
    for s in opool.sems:
        p.wait("gpsimd", s)
    se.close()

    sr_ = scope.enter_context(contextlib.ExitStack())
    xpool = Pool(r, "p4", r.sb(sr_, [128, LC], F32, 2, "frx"))
    apool = Pool(r, "p5", r.sb(sr_, [128, LC], F32, 2, "fra"))
    rpool = Pool(r, "p6", r.sb(sr_, [128, LC], F32, 2, "frr"))
    for fc in range(NFC):
        for lc in range(NLC):
            sl = slice(LC * lc, LC * (lc + 1))
            xb, xs = xpool.acquire(p, "sync")
            xsem, xv = xpool.dma(p, "sync", xs, lambda eng, xb=xb, fc=fc, sl=sl: eng.dma_start(
                out=xb[:], in_=x_src[128 * fc:128 * (fc + 1), sl]))
            ab, asl = apool.acquire(p, "sync")
            asem, av = apool.dma(p, "sync", asl, lambda eng, ab=ab, fc=fc, sl=sl: eng.dma_start(
                out=ab[:], in_=r.d_facc[128 * fc:128 * (fc + 1), sl]))
            rb, rs = rpool.acquire(p, "vector")
            p.wait("vector", xsem, xv)
            p.wait("vector", asem, av)
            v = p.do("vector", lambda eng, xb=xb, ab=ab, rb=rb: eng.tensor_add(
                rb[:], xb[:], ab[:]), ev, 1)
            xpool.release(xs, ev, v)
            apool.release(asl, ev, v)
            p.wait("gpsimd", ev, v)
            ssem, sv = rpool.dma(p, "gpsimd", rs, lambda eng, rb=rb, fc=fc, sl=sl: eng.dma_start(
                out=x_dst[128 * fc:128 * (fc + 1), sl], in_=rb[:]))
            rpool.release(rs, ssem, sv)
    for s in rpool.sems:
        p.wait("gpsimd", s)
    sr_.close()


def emit_attn_b(r, scope, x_src, x_dst):
    p = r.p
    nc = r.nc
    mm = r.sem("ab_mm")
    ev = r.sem("ab_ev")
    sq = r.sem("ab_sq")
    st = r.sem("ab_st")
    gld = r.sem("ab_gld")
    bar = r.sems["bar"]

    with contextlib.ExitStack() as sln:
        emit_ln(r, sln, x_src, r.d_xnbf, r.d_xn32)
    p.barrier(bar)

    s1 = scope.enter_context(contextlib.ExitStack())
    xpool = Pool(r, "p0", r.sb(s1, [128, LC], F32, 3, "abx"))
    sqpool = Pool(r, "p1", r.sb(s1, [128, LC], F32, 2, "absq"))
    cells = r.sb(s1, [1, 8], F32, base="abcell")
    rowsc = r.sb(s1, [1, LC], F32, base="abrsc")
    ps_s = r.ps(s1, [1, LC], "abps")
    ps_q = r.ps(s1, [1, LC], "abpq")
    p.do("vector", lambda eng: eng.memset(cells[:], 0.0), ev, 1)
    for lc in range(NLC):
        if lc > 0:
            p.wait("tensor", ev)
        for fc in range(NFC):
            xb, xs = xpool.acquire(p, "sync")
            lsem, lv = xpool.dma(p, "sync", xs, lambda eng, xb=xb, fc=fc, lc=lc: eng.dma_start(
                out=xb[:], in_=r.d_xn32[128 * fc:128 * (fc + 1), LC * lc:LC * (lc + 1)]))
            sb_, ss_ = sqpool.acquire(p, "scalar")
            p.wait("scalar", lsem, lv)
            vq = p.do("scalar", lambda eng, xb=xb, sb_=sb_: eng.activation(
                sb_[:], xb[:], AF.Square), sq, 1)
            p.wait("tensor", lsem, lv)
            v1 = p.do("tensor", lambda eng, xb=xb, f0=(fc == 0), fl=(fc == NFC - 1): eng.matmul(
                ps_s[:], r.ones_col[:], xb[:], start=f0, stop=fl), mm, 1)
            xpool.release(xs, mm, v1)
            xpool.release(xs, sq, vq)
            p.wait("tensor", sq, vq)
            v2 = p.do("tensor", lambda eng, sb_=sb_, f0=(fc == 0), fl=(fc == NFC - 1): eng.matmul(
                ps_q[:], r.ones_col[:], sb_[:], start=f0, stop=fl), mm, 1)
            sqpool.release(ss_, mm, v2)
        p.wait("vector", mm)
        p.do("vector", lambda eng: eng.tensor_reduce(
            rowsc[:, 0:1], ps_s[:], AX.X, ALU.add), ev, 1)
        p.do("vector", lambda eng: eng.tensor_reduce(
            rowsc[:, 1:2], ps_q[:], AX.X, ALU.add), ev, 1)
        p.wait("vector", ev)
        p.do("vector", lambda eng: eng.tensor_add(
            cells[:, 0:1], cells[:, 0:1], rowsc[:, 0:1]), ev, 1)
        p.wait("vector", ev)
        vS = p.do("vector", lambda eng: eng.tensor_add(
            cells[:, 1:2], cells[:, 1:2], rowsc[:, 1:2]), ev, 1)
    NT = float(L * D)
    p.wait("vector", ev)
    p.do("vector", lambda eng: eng.tensor_single_scalar(
        cells[:, 2:3], cells[:, 0:1], 1.0 / NT, ALU.mult), ev, 1)
    p.wait("vector", ev)
    p.do("vector", lambda eng: eng.tensor_single_scalar(
        cells[:, 6:7], cells[:, 1:2], 1.0 / NT, ALU.mult), ev, 1)
    p.wait("vector", ev)
    p.do("vector", lambda eng: eng.tensor_mul(cells[:, 7:8], cells[:, 2:3], cells[:, 2:3]), ev, 1)
    p.wait("vector", ev)
    p.do("vector", lambda eng: eng.tensor_sub(cells[:, 3:4], cells[:, 6:7], cells[:, 7:8]), ev, 1)
    p.wait("scalar", ev)
    p.do("scalar", lambda eng: eng.activation(cells[:, 6:7], cells[:, 3:4], AF.Ln, bias=EPS_B), ev, 1)
    p.wait("scalar", ev)
    p.do("scalar", lambda eng: eng.activation(cells[:, 4:5], cells[:, 6:7], AF.Exp, scale=-0.5), ev, 1)
    p.wait("vector", ev)
    p.do("vector", lambda eng: eng.tensor_mul(cells[:, 5:6], cells[:, 4:5], cells[:, 2:3]), ev, 1)
    p.wait("vector", ev)
    p.do("vector", lambda eng: eng.tensor_single_scalar(
        cells[:, 5:6], cells[:, 5:6], -1.0, ALU.mult), ev, 1)
    p.wait("gpsimd", ev)
    p.do("gpsimd", lambda eng: eng.dma_start(out=r.d_gcells[:], in_=cells[:]), st, 16)
    p.wait("sync", st)
    rg_col = r.sb(s1, [128, 1], F32, base="abrg")
    nmr_col = r.sb(s1, [128, 1], F32, base="abnmr")
    p.do("sync", lambda eng: eng.dma_start(
        out=rg_col[:], in_=r.d_gcells[:, 4:5].broadcast_to((128, 1))), gld, 16)
    p.do("sync", lambda eng: eng.dma_start(
        out=nmr_col[:], in_=r.d_gcells[:, 5:6].broadcast_to((128, 1))), gld, 16)
    p.wait("vector", gld)
    p.wait("tensor", gld)

    winT = r.sb(s1, [128, NFC * D], BF16, base="abwin")
    bin_sb = r.sb(s1, [128, 8], F32, base="abbin")
    colsum = r.sb(s1, [128, 8], F32, base="abcs")
    bias2 = r.sb(s1, [128, 8], F32, base="abb2")
    zrow2 = r.sb(s1, [128, 2], BF16, base="abz2")
    ldw_q = {q: r.sem("ab_ldw_" + q) for q in ("scalar", "gpsimd")}
    ldw_q["sync"] = gld
    for dc in range(NFC):
        q = ("sync", "scalar", "gpsimd")[dc % 3]
        p.do(q, lambda eng, dc=dc: eng.dma_start(
            out=winT[:, D * dc:D * (dc + 1)],
            in_=r.d_winT_bf[128 * dc:128 * (dc + 1), :]), ldw_q[q], 16)
    p.do("sync", lambda eng: eng.dma_start(out=bin_sb[:], in_=r.d_binp[:]), gld, 16)
    p.wait("tensor", gld)
    for q in ("scalar", "gpsimd"):
        p.wait("tensor", ldw_q[q])
    ps_c = r.ps(s1, [128, 8], "abpc")
    for oc in range(NFC):
        for ic in range(NFC):
            p.do("tensor", lambda eng, oc=oc, ic=ic, f0=(ic == 0), fl=(ic == NFC - 1): eng.matmul(
                ps_c[:, oc:oc + 1], winT[:, D * ic + 128 * oc:D * ic + 128 * (oc + 1)],
                r.ones_col_bf[:], start=f0, stop=fl), mm, 1)
    p.wait("vector", mm)
    p.do("vector", lambda eng: eng.tensor_copy(colsum[:], ps_c[:]), ev, 1)
    p.wait("vector", ev)
    p.do("vector", lambda eng: eng.tensor_scalar(
        bias2[:], colsum[:], nmr_col[:], None, ALU.mult), ev, 1)
    p.wait("vector", ev)
    p.do("vector", lambda eng: eng.tensor_add(bias2[:], bias2[:], bin_sb[:]), ev, 1)
    p.wait("vector", ev)
    p.do("vector", lambda eng: eng.memset(zrow2[:], 0.0), ev, 1)
    zg = ev.n
    for fc in range(NFC):
        p.wait("gpsimd", ev, zg)
        p.do("gpsimd", lambda eng, fc=fc: eng.dma_start(
            out=r.d_h0pb[128 * fc:128 * (fc + 1), 0:1], in_=zrow2[:, 0:1],
            allow_slow_non_contiguous=True), st, 16)
        p.do("gpsimd", lambda eng, fc=fc: eng.dma_start(
            out=r.d_h0pb[128 * fc:128 * (fc + 1), L + 1:L + 2], in_=zrow2[:, 1:2],
            allow_slow_non_contiguous=True), st, 16)
    xnpool = Pool(r, "p2", r.sb(s1, [128, NFC * LC], BF16, 2, "abxn"))
    fpool = Pool(r, "p3", r.sb(s1, [128, LC], F32, 2, "abf"))
    bpool = Pool(r, "p4", r.sb(s1, [128, LC], BF16, 2, "abb"))
    ps_w = [r.ps(s1, [128, LC], "abpw") for _ in range(2)]
    v_wev = [None, None]
    k = 0
    for lc in range(NLC):
        sl = slice(LC * lc, LC * (lc + 1))
        xb, xs = xnpool.acquire(p, "sync")
        xsem, xv = xnpool.dma(p, "sync", xs, lambda eng, xb=xb, sl=sl: eng.dma_start(
            out=xb[:].rearrange("p (c l) -> p c l", c=NFC),
            in_=r.d_xnbf[:, sl].rearrange("(c p) l -> p c l", p=128)))
        p.wait("tensor", xsem, xv)
        vwl = None
        for oc in range(NFC):
            pw = ps_w[k % 2]
            if v_wev[k % 2] is not None:
                p.wait("tensor", ev, v_wev[k % 2])
            vw = None
            for ic in range(NFC):
                vw = p.do("tensor", lambda eng, pw=pw, ic=ic, oc=oc, xb=xb,
                          f0=(ic == 0), fl=(ic == NFC - 1): eng.matmul(
                    pw[:], winT[:, D * ic + 128 * oc:D * ic + 128 * (oc + 1)],
                    xb[:, LC * ic:LC * ic + LC], start=f0, stop=fl), mm, 1)
            vwl = vw
            fb, fs = fpool.acquire(p, "vector")
            p.wait("vector", mm, vw)
            v1 = p.do("vector", lambda eng, pw=pw, fb=fb, oc=oc: eng.tensor_scalar(
                fb[:], pw[:], rg_col[:], bias2[:, oc:oc + 1], ALU.mult, ALU.add), ev, 1)
            v_wev[k % 2] = v1
            bb, bs = bpool.acquire(p, "vector")
            p.wait("vector", ev, v1)
            v2 = p.do("vector", lambda eng, fb=fb, bb=bb: eng.tensor_copy(bb[:], fb[:]), ev, 1)
            p.wait("gpsimd", ev, v2)
            s1m, w1_ = fpool.dma(p, "gpsimd", fs, lambda eng, fb=fb, oc=oc, sl=sl: eng.dma_start(
                out=r.d_h0f[128 * oc:128 * (oc + 1), sl], in_=fb[:]))
            fpool.release(fs, s1m, w1_)
            s2m, w2_ = bpool.dma(p, "gpsimd", bs, lambda eng, bb=bb, oc=oc, lc=lc: eng.dma_start(
                out=r.d_h0pb[128 * oc:128 * (oc + 1), 1 + LC * lc:1 + LC * (lc + 1)], in_=bb[:]))
            bpool.release(bs, s2m, w2_)
            k += 1
        xnpool.release(xs, mm, vwl)
    for s in bpool.sems + fpool.sems:
        p.wait("sync", s)
        p.wait("gpsimd", s)
    s1.close()
    p.barrier(bar)

    s3 = scope.enter_context(contextlib.ExitStack())
    cvT = r.sb(s3, [128, 3 * NFC * D], BF16, base="abcv")
    cb_sb = r.sb(s3, [128, 8], F32, base="abcb")
    hpool = Pool(r, "p0", r.sb(s3, [128, NFC * (LC + 2)], BF16, 2, "abhp"))
    fpool3 = Pool(r, "p1", r.sb(s3, [128, LC], F32, 2, "abf3"))
    sum_c = r.sb(s3, [128, 8], F32, base="absc")
    sq_c = r.sb(s3, [128, 8], F32, base="absq2")
    col1 = r.sb(s3, [128, 1], F32, base="abc1")
    col2 = r.sb(s3, [128, 1], F32, base="abc2")
    sqt = r.sb(s3, [128, LC], F32, base="absqt")
    ps_w3 = [r.ps(s3, [128, LC], "abpw3") for _ in range(2)]
    v_w3ev = [None, None]
    ldc_q = {q: r.sem("ab_ldc_" + q) for q in ("scalar", "gpsimd")}
    ldc_q["sync"] = gld
    for kk in range(3):
        for ic in range(NFC):
            q = ("sync", "scalar", "gpsimd")[(kk * NFC + ic) % 3]
            p.do(q, lambda eng, kk=kk, ic=ic: eng.dma_start(
                out=cvT[:, D * (NFC * kk + ic):D * (NFC * kk + ic + 1)],
                in_=r.d_cvT_bf[D * kk + 128 * ic:D * kk + 128 * (ic + 1), :]), ldc_q[q], 16)
    p.do("sync", lambda eng: eng.dma_start(out=cb_sb[:], in_=r.d_cbp[:]), gld, 16)
    p.wait("tensor", gld)
    for q in ("scalar", "gpsimd"):
        p.wait("tensor", ldc_q[q])
    p.do("vector", lambda eng: eng.memset(sum_c[:], 0.0), ev, 1)
    p.do("vector", lambda eng: eng.memset(sq_c[:], 0.0), ev, 1)
    k = 0
    for lc in range(NLC):
        hb, hs_ = hpool.acquire(p, "sync")
        hsem, hv = hpool.dma(p, "sync", hs_, lambda eng, hb=hb, lc=lc: eng.dma_start(
            out=hb[:].rearrange("p (c l) -> p c l", c=NFC),
            in_=r.d_h0pb[:, LC * lc:LC * lc + LC + 2].rearrange(
                "(c p) l -> p c l", p=128)))
        p.wait("tensor", hsem, hv)
        vwl = None
        for oc in range(NFC):
            pw = ps_w3[k % 2]
            if v_w3ev[k % 2] is not None:
                p.wait("tensor", ev, v_w3ev[k % 2])
            vw = None
            for kk in range(3):
                for ic in range(NFC):
                    vw = p.do("tensor", lambda eng, pw=pw, kk=kk, ic=ic, oc=oc, hb=hb,
                              f0=(kk == 0 and ic == 0),
                              fl=(kk == 2 and ic == NFC - 1): eng.matmul(
                        pw[:], cvT[:, D * (NFC * kk + ic) + 128 * oc:
                                   D * (NFC * kk + ic) + 128 * (oc + 1)],
                        hb[:, (LC + 2) * ic + kk:(LC + 2) * ic + kk + LC],
                        start=f0, stop=fl), mm, 1)
            vwl = vw
            fb, fs = fpool3.acquire(p, "vector")
            p.wait("vector", mm, vw)
            v1 = p.do("vector", lambda eng, pw=pw, fb=fb, oc=oc: eng.tensor_scalar(
                fb[:], pw[:], cb_sb[:, oc:oc + 1], None, ALU.add), ev, 1)
            v_w3ev[k % 2] = v1
            p.wait("vector", ev, v1)
            p.do("vector", lambda eng, fb=fb: eng.tensor_reduce(
                col1[:], fb[:], AX.X, ALU.add), ev, 1)
            p.wait("scalar", ev)
            vsq = p.do("scalar", lambda eng, fb=fb: eng.activation(
                sqt[:], fb[:], AF.Square, accum_out=col2[:]), sq, 1)
            p.wait("vector", ev)
            p.do("vector", lambda eng, oc=oc: eng.tensor_add(
                sum_c[:, oc:oc + 1], sum_c[:, oc:oc + 1], col1[:]), ev, 1)
            p.wait("vector", sq, vsq)
            p.wait("vector", ev)
            v3 = p.do("vector", lambda eng, oc=oc: eng.tensor_add(
                sq_c[:, oc:oc + 1], sq_c[:, oc:oc + 1], col2[:]), ev, 1)
            p.wait("gpsimd", ev, v3)
            sm, wv_ = fpool3.dma(p, "gpsimd", fs, lambda eng, fb=fb, oc=oc, lc=lc: eng.dma_start(
                out=r.d_hcf[128 * oc:128 * (oc + 1), LC * lc:LC * (lc + 1)], in_=fb[:]))
            fpool3.release(fs, sm, wv_)
            k += 1
        hpool.release(hs_, mm, vwl)
    for s in fpool3.sems:
        p.wait("sync", s)
    mu_c = r.sb(s3, [128, 8], F32, base="abmc")
    rs_c = r.sb(s3, [128, 8], F32, base="abrc")
    t_c = r.sb(s3, [128, 8], F32, base="abtc")
    p.wait("vector", ev)
    p.do("vector", lambda eng: eng.tensor_single_scalar(
        mu_c[:], sum_c[:], 1.0 / L, ALU.mult), ev, 1)
    p.wait("vector", ev)
    p.do("vector", lambda eng: eng.tensor_single_scalar(
        t_c[:], sq_c[:], 1.0 / L, ALU.mult), ev, 1)
    p.wait("vector", ev)
    p.do("vector", lambda eng: eng.tensor_mul(sq_c[:], mu_c[:], mu_c[:]), ev, 1)
    p.wait("vector", ev)
    p.do("vector", lambda eng: eng.tensor_sub(t_c[:], t_c[:], sq_c[:]), ev, 1)
    p.wait("scalar", ev)
    p.do("scalar", lambda eng: eng.activation(t_c[:], t_c[:], AF.Ln, bias=EPS_B), ev, 1)
    p.wait("scalar", ev)
    p.do("scalar", lambda eng: eng.activation(rs_c[:], t_c[:], AF.Exp, scale=-0.5), ev, 1)
    bpool3 = Pool(r, "p2", r.sb(s3, [128, LC], BF16, 2, "abb3"))
    spool3 = Pool(r, "p3", r.sb(s3, [128, LC], F32, 2, "abs3"))
    zpool3 = Pool(r, "p4", r.sb(s3, [128, LC], F32, 2, "abz3"))
    p.wait("vector", ev)
    for oc in range(NFC):
        for lc in range(NLC):
            fb, fs = fpool3.acquire(p, "sync")
            fsem, fv = fpool3.dma(p, "sync", fs, lambda eng, fb=fb, oc=oc, lc=lc: eng.dma_start(
                out=fb[:], in_=r.d_hcf[128 * oc:128 * (oc + 1), LC * lc:LC * (lc + 1)]))
            sb3, ss3 = spool3.acquire(p, "vector")
            p.wait("vector", fsem, fv)
            v1 = p.do("vector", lambda eng, fb=fb, sb3=sb3, oc=oc: eng.tensor_scalar(
                sb3[:], fb[:], mu_c[:, oc:oc + 1], rs_c[:, oc:oc + 1],
                ALU.subtract, ALU.mult), ev, 1)
            fpool3.release(fs, ev, v1)
            zb, zs = zpool3.acquire(p, "scalar")
            p.wait("scalar", ev, v1)
            v2 = p.do("scalar", lambda eng, sb3=sb3, zb=zb: eng.activation(
                zb[:], sb3[:], AF.Silu), sq, 1)
            spool3.release(ss3, sq, v2)
            bb, bs = bpool3.acquire(p, "vector")
            p.wait("vector", sq, v2)
            v3 = p.do("vector", lambda eng, zb=zb, bb=bb: eng.tensor_copy(bb[:], zb[:]), ev, 1)
            p.wait("gpsimd", sq, v2)
            p.wait("gpsimd", ev, v3)
            sm1, wv1 = zpool3.dma(p, "gpsimd", zs, lambda eng, zb=zb, oc=oc, lc=lc: eng.dma_start(
                out=r.d_h2f[128 * oc:128 * (oc + 1), LC * lc:LC * (lc + 1)], in_=zb[:]))
            zpool3.release(zs, sm1, wv1)
            sm2, wv2 = bpool3.dma(p, "gpsimd", bs, lambda eng, bb=bb, oc=oc, lc=lc: eng.dma_start(
                out=r.d_h2b[128 * oc:128 * (oc + 1), LC * lc:LC * (lc + 1)], in_=bb[:]))
            bpool3.release(bs, sm2, wv2)
    for s in bpool3.sems + zpool3.sems:
        p.wait("sync", s)
        p.wait("gpsimd", s)
    s3.close()
    p.barrier(bar)

    s5 = scope.enter_context(contextlib.ExitStack())
    d1h_bf = r.sb(s5, [64, L], BF16, base="abd1")
    gs_row = r.sb(s5, [1, L], F32, base="abgs")
    gq_row = r.sb(s5, [1, L], F32, base="abgq")
    s5b = contextlib.ExitStack()
    wbcd = r.sb(s5b, [128, NFC * 96], BF16, base="abwb")
    bbcd = r.sb(s5b, [96, 1], F32, base="abbb")
    BCd = r.sb(s5b, [96, L], F32, base="abBC")
    h2pool = Pool(r, "p0", r.sb(s5b, [128, LC], BF16, 3, "abh2"))
    ps_bc = [r.ps(s5b, [96, LC], "abpbc") for _ in range(2)]
    for ic in range(NFC):
        p.do("sync", lambda eng, ic=ic: eng.dma_start(
            out=wbcd[:, 96 * ic:96 * (ic + 1)],
            in_=r.d_wbcdT_bf[128 * ic:128 * (ic + 1), :]), gld, 16)
    p.do("sync", lambda eng: eng.dma_start(out=bbcd[:], in_=r.d_bbcd[:]), gld, 16)
    p.wait("tensor", gld)
    v_bcev = [None, None]
    for lc in range(NLC):
        pb = ps_bc[lc % 2]
        if v_bcev[lc % 2] is not None:
            p.wait("tensor", ev, v_bcev[lc % 2])
        vw = None
        for ic in range(NFC):
            hb, hs_ = h2pool.acquire(p, "sync")
            hsem, hv = h2pool.dma(p, "sync", hs_, lambda eng, hb=hb, ic=ic, lc=lc: eng.dma_start(
                out=hb[:], in_=r.d_h2b[128 * ic:128 * (ic + 1), LC * lc:LC * (lc + 1)]))
            p.wait("tensor", hsem, hv)
            vw = p.do("tensor", lambda eng, pb=pb, ic=ic, hb=hb,
                      f0=(ic == 0), fl=(ic == NFC - 1): eng.matmul(
                pb[:], wbcd[:, 96 * ic:96 * (ic + 1)], hb[:], start=f0, stop=fl), mm, 1)
            h2pool.release(hs_, mm, vw)
        p.wait("vector", mm, vw)
        v1 = p.do("vector", lambda eng, pb=pb, lc=lc: eng.tensor_scalar(
            BCd[:, LC * lc:LC * (lc + 1)], pb[:], bbcd[:], None, ALU.add), ev, 1)
        v_bcev[lc % 2] = v1
        p.wait("vector", ev, v1)
        p.do("vector", lambda eng, lc=lc: eng.tensor_copy(
            d1h_bf[:, LC * lc:LC * (lc + 1)], BCd[0:64, LC * lc:LC * (lc + 1)]), ev, 1)
    p.wait("gpsimd", ev)
    p.do("gpsimd", lambda eng: eng.dma_start(out=r.d_bcrows[:], in_=BCd[64:96, :]), st, 16)
    p.wait("sync", st)
    p.wait("gpsimd", st)
    p.wait("tensor", ev)
    s5b.close()

    s6 = contextlib.ExitStack()
    wd2 = r.sb(s6, [64, D], BF16, base="abwd2")
    bd2_sb = r.sb(s6, [128, 8], F32, base="abbd2")
    D_sb = r.sb(s6, [128, 8], F32, base="abD")
    alog = r.sb(s6, [128, 16], F32, base="abal")
    negA = r.sb(s6, [128, 16], F32, base="abnA")
    delta = r.sb(s6, [128, L], F32, base="abdel")
    dh = r.sb(s6, [128, L], F32, base="abdh")
    abar = r.sb(s6, [128, L], F32, base="abab")
    bx = r.sb(s6, [128, L], F32, base="abbx")
    S_sb = r.sb(s6, [128, L], F32, base="abS")
    pr = r.sb(s6, [128, L], F32, base="abpr")
    bbc = [r.sb(s6, [128, L], F32, base=f"abbbc{i}") for i in range(2)]
    cshared = r.sb(s6, [128, L], F32, base="abcsh")
    cded = r.sb(s6, [128, L], F32, base="abcde")
    mcol = bx[:, 0:LC]
    mcol2 = bx[:, LC:2 * LC]
    ps_y = r.ps(s6, [128, L], "abpy")
    al_s = r.sem("ab_al")
    h0_s = r.sem("ab_h0s")
    bbs = [r.sem("ab_bb0"), r.sem("ab_bb1")]
    cs_s = r.sem("ab_cs")
    cd_s = r.sem("ab_cd")
    cs2_s = r.sem("ab_cs2")
    h02_s = r.sem("ab_h0s2")
    gst_s = r.sem("ab_gst")
    p.do("sync", lambda eng: eng.dma_start(out=wd2[:], in_=r.d_wd2T_bf[:]), gld, 16)
    p.do("sync", lambda eng: eng.dma_start(out=bd2_sb[:], in_=r.d_bd2p[:]), gld, 16)
    p.do("sync", lambda eng: eng.dma_start(out=D_sb[:], in_=r.d_Dp[:]), gld, 16)
    p.wait("tensor", gld)
    p.do("vector", lambda eng: eng.memset(gs_row[:], 0.0), ev, 1)
    p.do("vector", lambda eng: eng.memset(gq_row[:], 0.0), ev, 1)
    QRR = ("sync", "scalar", "gpsimd")
    qk = 0
    vA = vA2 = vg = None
    v_stt_last = None
    v_prod15_prev = None
    v_bx = [None, None]
    for fg in range(NFC):
        if vA is not None:
            p.wait("sync", sq, vA)
        p.do("sync", lambda eng, fg=fg: eng.dma_start(
            out=alog[:], in_=r.d_alog[128 * fg:128 * (fg + 1), :]), al_s, 16)
        p.wait("scalar", al_s)
        if v_stt_last is not None:
            p.wait("sync", ev, v_stt_last)
            p.wait("gpsimd", ev, v_stt_last)
        h2a_v = p.do("sync", lambda eng, fg=fg: eng.dma_start(
            out=cshared[:, 0:HL], in_=r.d_h2f[128 * fg:128 * (fg + 1), 0:HL]), cs_s, 16)
        h2a2_v = p.do("gpsimd", lambda eng, fg=fg: eng.dma_start(
            out=cshared[:, HL:L], in_=r.d_h2f[128 * fg:128 * (fg + 1), HL:L]), cs2_s, 16)
        dl = delta
        if fg > 0:
            p.wait("vector", gst_s, 32 * fg)
            p.wait("tensor", ev)
        for lc in range(NLC):
            vw = p.do("tensor", lambda eng, fg=fg, lc=lc: eng.matmul(
                ps_y[:, LC * lc:LC * (lc + 1)], wd2[:, 128 * fg:128 * (fg + 1)],
                d1h_bf[:, LC * lc:LC * (lc + 1)], start=True, stop=True), mm, 1)
            p.wait("scalar", mm, vw)
            p.wait("scalar", ev)
            va = p.do("scalar", lambda eng, fg=fg, lc=lc: eng.activation(
                mcol[:], ps_y[:, LC * lc:LC * (lc + 1)], AF.Abs,
                bias=bd2_sb[:, fg:fg + 1]), sq, 1)
            p.wait("scalar", sq, va)
            ve = p.do("scalar", lambda eng: eng.activation(
                mcol2[:], mcol[:], AF.Exp, scale=-1.0), sq, 1)
            p.wait("scalar", sq, ve)
            vl = p.do("scalar", lambda eng: eng.activation(
                mcol[:], mcol2[:], AF.Ln, bias=1.0), sq, 1)
            p.wait("vector", sq, va)
            vm = p.do("vector", lambda eng, fg=fg, lc=lc, dl=dl: eng.tensor_scalar(
                dl[:, LC * lc:LC * (lc + 1)], ps_y[:, LC * lc:LC * (lc + 1)],
                bd2_sb[:, fg:fg + 1], 0.0, ALU.add, ALU.max), ev, 1)
            p.wait("vector", sq, vl)
            p.wait("vector", ev, vm)
            p.do("vector", lambda eng, lc=lc, dl=dl: eng.tensor_add(
                dl[:, LC * lc:LC * (lc + 1)], dl[:, LC * lc:LC * (lc + 1)],
                mcol[:]), ev, 1)
        p.wait("scalar", sq)
        if vA2 is not None:
            p.wait("scalar", ev, vA2)
        vA = p.do("scalar", lambda eng: eng.activation(
            negA[:], alog[:], AF.Exp), sq, 1)
        p.wait("vector", sq, vA)
        vA2 = p.do("vector", lambda eng: eng.tensor_single_scalar(
            negA[:], negA[:], -1.0, ALU.mult), ev, 1)
        p.wait("vector", cs_s, h2a_v)
        p.wait("vector", cs2_s, h2a2_v)
        vdh = p.do("vector", lambda eng, dl=dl: eng.tensor_mul(dh[:], dl[:], cshared[:]), ev, 1)
        v_scan = [None] * 16
        v_prod = [None] * 16
        v_y = [None] * 16
        v_ab = [None] * 16
        for n in range(16):
            cb = cshared if n % 2 == 0 else cded
            cb_sem = cs_s if n % 2 == 0 else cd_s
            qb = QRR[qk % 3]; qk += 1
            p.wait(qb, ev, v_bx[n % 2] if n >= 2 else vdh)
            bb_v = p.do(qb, lambda eng, n=n: eng.dma_start(
                out=bbc[n % 2][:], in_=r.d_bcrows[n:n + 1, :].broadcast_to((128, L))),
                bbs[n % 2], 16)
            qc = QRR[qk % 3]; qk += 1
            if n >= 2:
                p.wait(qc, ev, v_prod[n - 2])
            elif n == 0:
                p.wait(qc, ev, vdh)
            elif v_prod15_prev is not None:
                p.wait(qc, ev, v_prod15_prev)
            cb_v = p.do(qc, lambda eng, n=n, cb=cb: eng.dma_start(
                out=cb[:], in_=r.d_bcrows[16 + n:16 + n + 1, :].broadcast_to((128, L))),
                cb_sem, 16)
            p.wait("scalar", ev, v_scan[n - 1] if n > 0 else vdh)
            v_ab[n] = p.do("scalar", lambda eng, n=n, dl=dl: eng.activation(
                abar[:], dl[:], AF.Exp, scale=negA[:, n:n + 1]), sq, 1)
            p.wait("vector", bbs[n % 2], bb_v)
            v_bx[n % 2] = p.do("vector", lambda eng, n=n: eng.tensor_mul(
                bx[:], dh[:], bbc[n % 2][:]), ev, 1)
            p.wait("vector", sq, v_ab[n])
            v_scan[n] = p.do("vector", lambda eng: eng.tensor_tensor_scan(
                S_sb[:], abar[:], bx[:], 0.0, ALU.mult, ALU.add), ev, 1)
            p.wait("vector", cb_sem, cb_v)
            if n >= 1:
                p.wait("vector", mm, v_y[n - 1])
            v_prod[n] = p.do("vector", lambda eng, cb=cb: eng.tensor_mul(
                pr[:], S_sb[:], cb[:]), ev, 1)
            p.wait("tensor", ev, v_prod[n])
            for c in range(8):
                v_y[n] = p.do("tensor", lambda eng, n=n, c=c: eng.matmul(
                    ps_y[:, 512 * c:512 * (c + 1)], r.ident_f32[:],
                    pr[:, 512 * c:512 * (c + 1)],
                    start=(n == 0), stop=(n == 15)), mm, 1)
        v_prod15_prev = v_prod[15]
        p.wait("sync", ev, v_prod[14])
        p.wait("gpsimd", ev, v_prod[14])
        h2b_v = p.do("sync", lambda eng, fg=fg: eng.dma_start(
            out=cshared[:, 0:HL], in_=r.d_h2f[128 * fg:128 * (fg + 1), 0:HL]), cs_s, 16)
        h2b2_v = p.do("gpsimd", lambda eng, fg=fg: eng.dma_start(
            out=cshared[:, HL:L], in_=r.d_h2f[128 * fg:128 * (fg + 1), HL:L]), cs2_s, 16)
        p.wait("sync", ev, v_scan[15])
        p.wait("gpsimd", ev, v_scan[15])
        p.do("sync", lambda eng, fg=fg: eng.dma_start(
            out=bx[:, 0:HL], in_=r.d_h0f[128 * fg:128 * (fg + 1), 0:HL]), h0_s, 16)
        p.do("gpsimd", lambda eng, fg=fg: eng.dma_start(
            out=bx[:, HL:L], in_=r.d_h0f[128 * fg:128 * (fg + 1), HL:L]), h02_s, 16)
        p.wait("vector", mm, v_y[15])
        p.wait("vector", cs_s, h2b_v)
        p.wait("vector", cs2_s, h2b2_v)
        for c in range(8):
            v_stt_last = p.do("vector", lambda eng, fg=fg, c=c: eng.scalar_tensor_tensor(
                dh[:, 512 * c:512 * (c + 1)], cshared[:, 512 * c:512 * (c + 1)],
                D_sb[:, fg:fg + 1], ps_y[:, 512 * c:512 * (c + 1)],
                ALU.mult, ALU.add), ev, 1)
        p.wait("scalar", h0_s)
        p.wait("scalar", h02_s)
        p.wait("scalar", mm, v_y[15])
        vsil = p.do("scalar", lambda eng: eng.activation(pr[:], bx[:], AF.Silu), sq, 1)
        p.wait("vector", sq, vsil)
        vg = p.do("vector", lambda eng, dl=dl: eng.tensor_mul(dl[:], dh[:], pr[:]), ev, 1)
        p.wait("gpsimd", ev, vg)
        p.wait("sync", ev, vg)
        p.do("gpsimd", lambda eng, fg=fg, dl=dl: eng.dma_start(
            out=r.d_gf[128 * fg:128 * (fg + 1), 0:HL], in_=dl[:, 0:HL]), gst_s, 16)
        p.do("sync", lambda eng, fg=fg, dl=dl: eng.dma_start(
            out=r.d_gf[128 * fg:128 * (fg + 1), HL:L], in_=dl[:, HL:L]), gst_s, 16)
        p.wait("tensor", ev, vg)
        v_gs = [None] * 8
        for c in range(8):
            v_gs[c] = p.do("tensor", lambda eng, c=c, dl=dl: eng.matmul(
                ps_y[0:1, 512 * c:512 * (c + 1)], r.ones_col[:],
                dl[:, 512 * c:512 * (c + 1)], start=True, stop=True), mm, 1)
        vadd_gs = [None] * 8
        for c in range(8):
            p.wait("vector", mm, v_gs[c])
            vadd_gs[c] = p.do("vector", lambda eng, c=c: eng.tensor_add(
                gs_row[:, 512 * c:512 * (c + 1)], gs_row[:, 512 * c:512 * (c + 1)],
                ps_y[0:1, 512 * c:512 * (c + 1)]), ev, 1)
        v_gq_prev = None
        for c in range(8):
            p.wait("scalar", ev, vg)
            if v_gq_prev is not None:
                p.wait("scalar", mm, v_gq_prev)
            vq2 = p.do("scalar", lambda eng, c=c, dl=dl: eng.activation(
                mcol[:], dl[:, 512 * c:512 * (c + 1)], AF.Square), sq, 1)
            p.wait("tensor", sq, vq2)
            p.wait("tensor", ev, vadd_gs[c])
            v_gq_prev = p.do("tensor", lambda eng, c=c: eng.matmul(
                ps_y[0:1, 512 * c:512 * (c + 1)], r.ones_col[:], mcol[:],
                start=True, stop=True), mm, 1)
            p.wait("vector", mm, v_gq_prev)
            p.do("vector", lambda eng, c=c: eng.tensor_add(
                gq_row[:, 512 * c:512 * (c + 1)], gq_row[:, 512 * c:512 * (c + 1)],
                ps_y[0:1, 512 * c:512 * (c + 1)]), ev, 1)
    p.wait("sync", gst_s)
    p.wait("sync", ev)
    p.wait("sync", mm)
    p.wait("tensor", ev)
    p.wait("gpsimd", ev)
    p.wait("gpsimd", cs_s)
    p.wait("gpsimd", cs2_s)
    p.wait("gpsimd", h02_s)
    p.wait("gpsimd", cd_s)
    p.wait("scalar", cs_s)
    p.wait("scalar", cs2_s)
    p.wait("scalar", h02_s)
    p.wait("scalar", cd_s)
    p.wait("scalar", bbs[0])
    p.wait("scalar", bbs[1])
    s6.close()

    mu_row = r.sb(s5, [1, L], F32, base="abmu")
    rs_row = r.sb(s5, [1, L], F32, base="abrs")
    nmu_row = r.sb(s5, [1, L], F32, base="abnm")
    p.wait("vector", ev)
    p.do("vector", lambda eng: eng.tensor_single_scalar(
        mu_row[:], gs_row[:], 1.0 / D, ALU.mult), ev, 1)
    p.wait("vector", ev)
    p.do("vector", lambda eng: eng.tensor_single_scalar(
        gq_row[:], gq_row[:], 1.0 / D, ALU.mult), ev, 1)
    p.wait("vector", ev)
    p.do("vector", lambda eng: eng.tensor_mul(gs_row[:], mu_row[:], mu_row[:]), ev, 1)
    p.wait("vector", ev)
    p.do("vector", lambda eng: eng.tensor_sub(gq_row[:], gq_row[:], gs_row[:]), ev, 1)
    p.wait("scalar", ev)
    p.do("scalar", lambda eng: eng.activation(gq_row[:], gq_row[:], AF.Ln, bias=EPS_B), ev, 1)
    p.wait("scalar", ev)
    p.do("scalar", lambda eng: eng.activation(rs_row[:], gq_row[:], AF.Exp, scale=-0.5), ev, 1)
    p.wait("vector", ev)
    p.do("vector", lambda eng: eng.tensor_single_scalar(
        nmu_row[:], mu_row[:], -1.0, ALU.mult), ev, 1)
    p.wait("gpsimd", ev)
    p.do("gpsimd", lambda eng: eng.dma_start(out=r.d_murow[:], in_=rs_row[:]), st, 16)
    p.do("gpsimd", lambda eng: eng.dma_start(out=r.d_rsrow[:], in_=nmu_row[:]), st, 16)
    p.wait("sync", st)
    gpool = Pool(r, "p1", r.sb(s5, [128, LC], F32, 3, "abgp"))
    rsbcp = Pool(r, "p2", r.sb(s5, [128, LC], F32, 2, "abrsb"))
    nmbcp = Pool(r, "p3", r.sb(s5, [128, LC], F32, 2, "abnmb"))
    t1pool = Pool(r, "p4", r.sb(s5, [128, LC], F32, 2, "abt1"))
    gnpool = Pool(r, "p5", r.sb(s5, [128, LC], BF16, 2, "abgn"))
    for lc in range(NLC):
        sl = slice(LC * lc, LC * (lc + 1))
        rsb, rss = rsbcp.acquire(p, "sync")
        rssem, rsv = rsbcp.dma(p, "sync", rss, lambda eng, rsb=rsb, sl=sl: eng.dma_start(
            out=rsb[:], in_=r.d_murow[:, sl].broadcast_to((128, LC))))
        nmb, nms = nmbcp.acquire(p, "sync")
        nmsem, nmv = nmbcp.dma(p, "sync", nms, lambda eng, nmb=nmb, sl=sl: eng.dma_start(
            out=nmb[:], in_=r.d_rsrow[:, sl].broadcast_to((128, LC))))
        p.wait("vector", rssem, rsv)
        p.wait("vector", nmsem, nmv)
        lastg = None
        for ic in range(NFC):
            gb, gss = gpool.acquire(p, "sync")
            gsem, gv = gpool.dma(p, "sync", gss, lambda eng, gb=gb, ic=ic, sl=sl: eng.dma_start(
                out=gb[:], in_=r.d_gf[128 * ic:128 * (ic + 1), sl]))
            t1, t1s = t1pool.acquire(p, "vector")
            p.wait("vector", gsem, gv)
            v1 = p.do("vector", lambda eng, gb=gb, t1=t1, nmb=nmb: eng.tensor_add(
                t1[:], gb[:], nmb[:]), ev, 1)
            gpool.release(gss, ev, v1)
            gnb, gns = gnpool.acquire(p, "vector")
            p.wait("vector", ev, v1)
            v2 = p.do("vector", lambda eng, t1=t1, gnb=gnb, rsb=rsb: eng.tensor_mul(
                gnb[:], t1[:], rsb[:]), ev, 1)
            t1pool.release(t1s, ev, v2)
            lastg = v2
            p.wait("gpsimd", ev, v2)
            gsm, gwv = gnpool.dma(p, "gpsimd", gns, lambda eng, gnb=gnb, ic=ic, sl=sl: eng.dma_start(
                out=r.d_gnb[128 * ic:128 * (ic + 1), sl], in_=gnb[:]))
            gnpool.release(gns, gsm, gwv)
        rsbcp.release(rss, ev, lastg)
        nmbcp.release(nms, ev, lastg)
    for s in gnpool.sems:
        p.wait("sync", s)
    woT = r.sb(s5, [128, NFC * D], BF16, base="abwo")
    bout_sb = r.sb(s5, [128, 8], F32, base="abbo")
    gbpool = Pool(r, "p6", r.sb(s5, [128, NFC * LC], BF16, 2, "abgb"))
    x0pool = Pool(r, "p7", r.sb(s5, [128, LC], F32, 2, "abx0"))
    x2pool = Pool(r, "p8", r.sb(s5, [128, LC], F32, 2, "abx2"))
    rpool = Pool(r, "p9", r.sb(s5, [128, LC], F32, 2, "abrp"))
    ps_w7 = [r.ps(s5, [128, LC], "abpw7") for _ in range(2)]
    ldo5 = r.sem("ab_ldo5")
    for dc in range(NFC):
        q, s_ = (("sync", gld), ("gpsimd", ldo5))[dc % 2]
        p.do(q, lambda eng, dc=dc: eng.dma_start(
            out=woT[:, D * dc:D * (dc + 1)],
            in_=r.d_woutT_bf[128 * dc:128 * (dc + 1), :]), s_, 16)
    p.do("sync", lambda eng: eng.dma_start(out=bout_sb[:], in_=r.d_boutp[:]), gld, 16)
    p.wait("tensor", gld)
    p.wait("tensor", ldo5)
    k = 0
    for lc in range(NLC):
        sl = slice(LC * lc, LC * (lc + 1))
        gb, gss = gbpool.acquire(p, "sync")
        gsem, gv = gbpool.dma(p, "sync", gss, lambda eng, gb=gb, sl=sl: eng.dma_start(
            out=gb[:].rearrange("p (c l) -> p c l", c=NFC),
            in_=r.d_gnb[:, sl].rearrange("(c p) l -> p c l", p=128)))
        p.wait("tensor", gsem, gv)
        vwl = None
        for oc in range(NFC):
            pw = ps_w7[k % 2]
            x0b, x0s = x0pool.acquire(p, "sync")
            x0sem, x0v = x0pool.dma(p, "sync", x0s, lambda eng, x0b=x0b, oc=oc, sl=sl: eng.dma_start(
                out=x0b[:], in_=r.d_xn32[128 * oc:128 * (oc + 1), sl]))
            x2b, x2s = x2pool.acquire(p, "sync")
            x2sem, x2v = x2pool.dma(p, "sync", x2s, lambda eng, x2b=x2b, oc=oc, sl=sl: eng.dma_start(
                out=x2b[:], in_=x_src[128 * oc:128 * (oc + 1), sl]))
            if k >= 2:
                p.wait("tensor", ev)
            vw = None
            for ic in range(NFC):
                vw = p.do("tensor", lambda eng, pw=pw, ic=ic, oc=oc, gb=gb,
                          f0=(ic == 0), fl=(ic == NFC - 1): eng.matmul(
                    pw[:], woT[:, D * ic + 128 * oc:D * ic + 128 * (oc + 1)],
                    gb[:, LC * ic:LC * ic + LC], start=f0, stop=fl), mm, 1)
            vwl = vw
            rb, rs = rpool.acquire(p, "vector")
            p.wait("vector", mm, vw)
            p.wait("vector", x0sem, x0v)
            p.wait("vector", x2sem, x2v)
            v1 = p.do("vector", lambda eng, pw=pw, rb=rb, x0b=x0b, oc=oc: eng.scalar_tensor_tensor(
                rb[:], pw[:], bout_sb[:, oc:oc + 1], x0b[:], ALU.add, ALU.add), ev, 1)
            x0pool.release(x0s, ev, v1)
            p.wait("vector", ev, v1)
            v2 = p.do("vector", lambda eng, rb=rb, x2b=x2b: eng.tensor_add(
                rb[:], rb[:], x2b[:]), ev, 1)
            x2pool.release(x2s, ev, v2)
            p.wait("gpsimd", ev, v2)
            rsm, rwv = rpool.dma(p, "gpsimd", rs, lambda eng, rb=rb, oc=oc, sl=sl: eng.dma_start(
                out=x_dst[128 * oc:128 * (oc + 1), sl], in_=rb[:]))
            rpool.release(rs, rsm, rwv)
            k += 1
        gbpool.release(gss, mm, vwl)
    for s in rpool.sems:
        p.wait("gpsimd", s)
        p.wait("sync", s)
    s5.close()
    p.barrier(bar)

def emit_ffn_experts_compact(r, scope, x_src, x_dst):
    p = r.p
    nc = r.nc
    mm = r.sem("fe_mm")
    ev = r.sem("fe_ev")
    st = r.sem("fe_st")
    gld = r.sem("fe_gld")
    sq = r.sem("fe_sq")
    SL = SLOT
    NCH = [(0, 512), (512, SL - 512)]

    se = scope.enter_context(contextlib.ExitStack())
    h1_big = r.sb(se, [128, 32 * SL], BF16, base="fch1")
    w2e = r.sb(se, [128, 32 * D], BF16, base="fcw2")
    b1_sb = r.sb(se, [128, 256], F32, base="fcb1")
    relu_t = r.sb(se, [128, SL], F32, base="fcrt")
    xep = Pool(r, "p3", [r.sb(se, [128, NFC * SL], BF16, 1, "fcxe")])
    w1pool = Pool(r, "p0", r.sb(se, [128, NFC * 128], BF16, 3, "fcw1"))
    w1fpool = Pool(r, "p4", r.sb(se, [128, NFC * 128], F32, 3, "fcw1f"))
    w2fpool = Pool(r, "p5", r.sb(se, [128, 2 * D], F32, 2, "fcw2f"))
    sepool = Pool(r, "p1", r.sb(se, [128, SL], F32, 2, "fcse"))
    srowp = Pool(r, "p6", r.sb(se, [1, SL], F32, 2, "fcsr"))
    b2rp = Pool(r, "p7", r.sb(se, [1, D], F32, 2, "fcb2"))
    opool = Pool(r, "p2", r.sb(se, [128, D], F32, 2, "fco"))
    ps_h = [r.ps(se, [128, SL], "fcph") for _ in range(2)]
    ps_o = [r.ps(se, [128, D], "fcpo") for _ in range(2)]

    ld0 = r.sem("fe_ld0")
    p.do("sync", lambda eng: eng.dma_start(out=b1_sb[:], in_=r.d_b1[:]), ld0, 16)
    p.wait("vector", ld0)

    sslot_row = r.d_sslot.rearrange("(o s) x -> (o x) s", o=1)
    prev_lastw2 = None
    v_hev = [None, None]
    v_oev = [None, None]
    for e in range(NE):
        xe, xes = xep.acquire(p, "sync")
        xsem = None
        for dk in range(NFC):
            xsem, xv = xep.dma(p, "sync", xes, lambda eng, xe=xe, dk=dk, e=e: eng.dma_start(
                out=xe[:, SL * dk:SL * (dk + 1)],
                in_=r.d_bufx[SL * e:SL * (e + 1), 128 * dk:128 * (dk + 1)],
                transpose=True))
        seb, ses = sepool.acquire(p, "sync")
        sesem, sev = sepool.dma(p, "sync", ses, lambda eng, seb=seb, e=e: eng.dma_start(
            out=seb[:], in_=sslot_row[:, SL * e:SL * (e + 1)].broadcast_to((128, SL))))
        sro, srs = srowp.acquire(p, "sync")
        srsem, srv = srowp.dma(p, "sync", srs, lambda eng, sro=sro, e=e: eng.dma_start(
            out=sro[:], in_=sslot_row[:, SL * e:SL * (e + 1)]))
        b2r, b2s = b2rp.acquire(p, "sync")
        b2sem, b2v = b2rp.dma(p, "sync", b2s, lambda eng, b2r=b2r, e=e: eng.dma_start(
            out=b2r[:], in_=r.d_b2[e:e + 1, :]))
        p.wait("tensor", xsem, xv)
        last_h = None
        for fo in range(32):
            w1f, w1fs = w1fpool.acquire(p, "sync")
            w1fsem, w1fv = w1fpool.dma(p, "sync", w1fs, lambda eng, w1f=w1f, e=e, fo=fo: eng.dma_start(
                out=w1f[:].rearrange("p (c f) -> p c f", c=NFC),
                in_=r.d_w1[D * e:D * (e + 1), 128 * fo:128 * (fo + 1)].rearrange(
                    "(c p) f -> p c f", p=128)))
            w1b, w1s = w1pool.acquire(p, "scalar")
            p.wait("scalar", w1fsem, w1fv)
            vw1c = p.do("scalar", lambda eng, w1b=w1b, w1f=w1f: eng.copy(
                w1b[:], w1f[:]), sq, 1)
            w1fpool.release(w1fs, sq, vw1c)
            ph = ps_h[fo % 2]
            p.wait("tensor", sq, vw1c)
            if v_hev[fo % 2] is not None:
                p.wait("tensor", ev, v_hev[fo % 2])
            vmm = None
            for off, wdt in NCH:
                for dk in range(NFC):
                    vmm = p.do("tensor", lambda eng, ph=ph, w1b=w1b, dk=dk, off=off,
                               wdt=wdt, xe=xe, f0=(dk == 0), fl=(dk == NFC - 1): eng.matmul(
                        ph[:, off:off + wdt],
                        w1b[:, 128 * dk:128 * (dk + 1)],
                        xe[:, SL * dk + off:SL * dk + off + wdt],
                        start=f0, stop=fl), mm, 1)
            last_h = vmm
            w1pool.release(w1s, mm, vmm)
            p.wait("vector", mm, vmm)
            p.wait("vector", ev)
            v1 = p.do("vector", lambda eng, ph=ph, e=e, fo=fo: eng.tensor_scalar(
                relu_t[:], ph[:], b1_sb[:, 32 * e + fo:32 * e + fo + 1], 0.0,
                ALU.add, ALU.max), ev, 1)
            v_hev[fo % 2] = v1
            p.wait("vector", sesem, sev)
            p.wait("vector", ev, v1)
            p.do("vector", lambda eng, fo=fo, seb=seb: eng.tensor_mul(
                h1_big[:, SL * fo:SL * (fo + 1)], relu_t[:], seb[:]), ev, 1)
        xep.release(xes, mm, last_h)
        sepool.release(ses, ev, ev.n)
        if prev_lastw2 is not None:
            p.wait("scalar", mm, prev_lastw2)
        vw2c = None
        for ck in range(16):
            q2 = "gpsimd" if ck % 2 == 0 else "sync"
            w2f, w2fs = w2fpool.acquire(p, q2)
            w2fsem, w2fv = w2fpool.dma(p, q2, w2fs, lambda eng, w2f=w2f, e=e, ck=ck: eng.dma_start(
                out=w2f[:].rearrange("p (c o) -> p c o", c=2),
                in_=r.d_w2[FF * e + 256 * ck:FF * e + 256 * (ck + 1), :].rearrange(
                    "(c p) o -> p c o", p=128)))
            p.wait("scalar", w2fsem, w2fv)
            vw2c = p.do("scalar", lambda eng, w2f=w2f, ck=ck: eng.copy(
                w2e[:, 2 * D * ck:2 * D * (ck + 1)], w2f[:]), sq, 1)
            w2fpool.release(w2fs, sq, vw2c)
        p.wait("tensor", sq, vw2c)
        p.wait("tensor", ev)
        p.wait("tensor", srsem, srv)
        p.wait("tensor", b2sem, b2v)
        lastw2 = None
        for sc in range(SL // 128):
            po = ps_o[sc % 2]
            if v_oev[sc % 2] is not None:
                p.wait("tensor", ev, v_oev[sc % 2])
            vmm = None
            for half in range(2):
                for fk in range(32):
                    vmm = p.do("tensor", lambda eng, po=po, fk=fk, sc=sc, half=half,
                               f0=(fk == 0), fl=(fk == 31): eng.matmul(
                        po[:, 512 * half:512 * (half + 1)],
                        h1_big[:, SL * fk + 128 * sc:SL * fk + 128 * sc + 128],
                        w2e[:, D * fk + 512 * half:D * fk + 512 * half + 512],
                        start=f0, stop=fl), mm, 1)
                vmm = p.do("tensor", lambda eng, po=po, sc=sc, half=half, sro=sro,
                           b2r=b2r: eng.matmul(
                    po[:, 512 * half:512 * (half + 1)],
                    sro[:, 128 * sc:128 * (sc + 1)],
                    b2r[:, 512 * half:512 * (half + 1)], start=False, stop=True), mm, 1)
            lastw2 = vmm
            ob, os_ = opool.acquire(p, "vector")
            p.wait("vector", mm, vmm)
            vcp = p.do("vector", lambda eng, po=po, ob=ob: eng.tensor_copy(ob[:], po[:]), ev, 1)
            v_oev[sc % 2] = vcp
            p.wait("gpsimd", ev, vcp)
            ssem, sv = opool.dma(p, "gpsimd", os_, lambda eng, ob=ob, e=e, sc=sc: eng.dma_start(
                out=r.d_ybuf[SL * e + 128 * sc:SL * e + 128 * (sc + 1), :], in_=ob[:]))
            opool.release(os_, ssem, sv)
        srowp.release(srs, mm, lastw2)
        b2rp.release(b2s, mm, lastw2)
        prev_lastw2 = lastw2
    for s in opool.sems:
        p.wait("sync", s)
        p.wait("gpsimd", s)
    se.close()

    scg = scope.enter_context(contextlib.ExitStack())
    slots3 = r.sb(scg, [128, 32], mybir.dt.int32, base="fcs3")
    gthp = Pool(r, "p4", r.sb(scg, [128, D], F32, 2, "fcgt"))
    x1p = Pool(r, "p5", r.sb(scg, [128, NFC * 128], F32, 2, "fcx1"))
    xoutp = Pool(r, "p1", r.sb(scg, [128, NFC * 128], F32, 2, "fcxo"))
    ps_t = [r.ps(scg, [128, 128], "fcpt") for _ in range(2)]
    v_tev = [None, None]
    p.do("sync", lambda eng: eng.dma_start(out=slots3[:], in_=r.d_slots[:]), gld, 16)
    p.wait("gpsimd", gld)
    for lt in range(32):
        gth, gs_ = gthp.acquire(p, "gpsimd")
        gsem, gv = gthp.dma(p, "gpsimd", gs_, lambda eng, gth=gth, lt=lt: eng.indirect_dma_start(
            gth[:], None, r.d_ybuf[:], IndirectOffsetOnAxis(slots3[:, lt:lt + 1], 0),
            oob_is_err=False))
        x1b, x1s = x1p.acquire(p, "sync")
        x1sem, x1v = x1p.dma(p, "sync", x1s, lambda eng, x1b=x1b, lt=lt: eng.dma_start(
            out=x1b[:].rearrange("p (c l) -> p c l", c=NFC),
            in_=x_src[:, 128 * lt:128 * (lt + 1)].rearrange("(c p) l -> p c l", p=128)))
        xob, xos = xoutp.acquire(p, "vector")
        lastv = None
        for fc in range(NFC):
            pt = ps_t[fc % 2]
            if v_tev[fc % 2] is not None:
                p.wait("tensor", ev, v_tev[fc % 2])
            p.wait("tensor", gsem, gv)
            vt = p.do("tensor", lambda eng, gth=gth, fc=fc, pt=pt: eng.transpose(
                pt[:], gth[:, 128 * fc:128 * (fc + 1)], r.ident_f32[:]), mm, 1)
            p.wait("vector", mm, vt)
            p.wait("vector", x1sem, x1v)
            lastv = p.do("vector", lambda eng, pt=pt, x1b=x1b, xob=xob, fc=fc: eng.tensor_add(
                xob[:, 128 * fc:128 * (fc + 1)], pt[:], x1b[:, 128 * fc:128 * (fc + 1)]),
                ev, 1)
            v_tev[fc % 2] = lastv
        gthp.release(gs_, mm, mm.n)
        x1p.release(x1s, ev, lastv)
        p.wait("gpsimd", ev, lastv)
        ssem, sv = xoutp.dma(p, "gpsimd", xos, lambda eng, xob=xob, lt=lt: eng.dma_start(
            out=x_dst[:, 128 * lt:128 * (lt + 1)].rearrange("(c p) l -> p c l", p=128),
            in_=xob[:].rearrange("p (c l) -> p c l", c=NFC)))
        xoutp.release(xos, ssem, sv)
    for s in xoutp.sems:
        p.wait("gpsimd", s)
        p.wait("sync", s)
    scg.close()



def kernel(**inputs):
    nc = build_nc("all")
    x = np.asarray(inputs["L0_attn_a_x"], np.float32)
    w = {k: np.asarray(v, np.float32) for k, v in inputs.items()}
    col = lambda v: np.ascontiguousarray(v.reshape(8, 128).T)
    shared = {
        "wqT": np.ascontiguousarray(w["L0_attn_a_Wq"].T),
        "wkT": np.ascontiguousarray(w["L0_attn_a_Wk"].T),
        "wvT": np.ascontiguousarray(w["L0_attn_a_Wv"].T),
        "woT": np.ascontiguousarray(w["L0_attn_a_Wo"].T),
        "wg": np.ascontiguousarray(w["L0_ffn_Wg"]),
        "w1": np.ascontiguousarray(w["L0_ffn_W1"].reshape(NE * D, FF)),
        "b1p": np.ascontiguousarray(
            w["L0_ffn_b1"].reshape(NE, 32, 128).transpose(2, 0, 1).reshape(128, 256)),
        "w2": np.ascontiguousarray(w["L0_ffn_W2"].reshape(NE * FF, D)),
        "b2m": np.ascontiguousarray(w["L0_ffn_b2"]),
        "winT": np.ascontiguousarray(w["L0_attn_b_w_in"].T),
        "binp": col(w["L0_attn_b_b_in"]),
        "cvT": np.ascontiguousarray(np.concatenate(
            [w["L0_attn_b_conv_w"][:, :, kk].T for kk in range(3)], axis=0)),
        "cbp": col(w["L0_attn_b_conv_b"]),
        "alogd": np.ascontiguousarray(w["L0_attn_b_A_log"]),
        "Dp": col(w["L0_attn_b_D"]),
        "wbcdT": np.ascontiguousarray(np.concatenate(
            [w["L0_attn_b_w_d1"].T, w["L0_attn_b_w_B"].T, w["L0_attn_b_w_C"].T], axis=1)),
        "bbcd": np.ascontiguousarray(np.concatenate(
            [w["L0_attn_b_b_d1"], w["L0_attn_b_b_B"], w["L0_attn_b_b_C"]])[:, None]),
        "wd2T": np.ascontiguousarray(w["L0_attn_b_w_d2"].T),
        "bd2p": col(w["L0_attn_b_b_d2"]),
        "woutT": np.ascontiguousarray(w["L0_attn_b_w_out"].T),
        "boutp": col(w["L0_attn_b_b_out"]),
    }
    in_maps = []
    for c in range(NCORE):
        mask = np.zeros((32, 1), np.float32)
        for rr_ in range(c):
            mask[4 * rr_, 0] = 1.0
        m = dict(shared)
        m["xT"] = np.ascontiguousarray(x[c].T)
        m["memask"] = mask
        in_maps.append(m)
    res = run_bass_kernel_spmd(nc, in_maps, core_ids=list(range(NCORE)))
    out = np.stack([np.ascontiguousarray(res.results[c]["outT"].T)
                    for c in range(NCORE)], axis=0)
    return out
